# version 05
# baseline (speedup 1.0000x reference)
import numpy as np
import concourse.bass as bass
import concourse.mybir as mybir
from concourse.bass_utils import run_bass_kernel_spmd
from contextlib import ExitStack

F32 = mybir.dt.float32
BF16 = mybir.dt.bfloat16
FP8 = mybir.dt.float8e4
U8 = mybir.dt.uint8
AF = mybir.ActivationFunctionType
ALU = mybir.AluOpType

NCORES = 8
S, D = 2048, 1024
NT, ND = S // 128, D // 128
FFA, FFQ = 2048, 4096
LN_EPS = 1e-5
ATT_EPS = 1e-10
QE4 = dict(thr=2.0 ** -6, ovf=248.0, c1c=1, kc=2)
QE5 = dict(thr=2.0 ** -14, ovf=65520.0, c1c=3, kc=4)
PHASES = ("attn_a", "ffnq1", "attn_b", "ffnq2")
AA_STOP = [0]
AB_QGS = [None]
AB_NIC = [8]
AB_DBG = [False]
AB_PRS = [None]
AB_OUTP = [True]


class Prog:

    def __init__(self, nc, es):
        self.nc = nc
        self.eng = {'v': nc.vector, 's': nc.scalar, 't': nc.tensor,
                    'g': nc.gpsimd, 'y': nc.sync}
        self.sem, self.cnt = {}, {}
        for e in 'vstgy':
            self.sem[e] = es.enter_context(nc.semaphore("sm_" + e))
            self.cnt[e] = 0
        self.NDMA = 8
        self.dma_idx = {}
        for e in 'gy':
            for i in range(self.NDMA):
                nm = f'd{e}{i}'
                self.sem[nm] = es.enter_context(nc.semaphore("smd_" + nm))
                self.cnt[nm] = 0
            self.dma_idx[e] = 0
        self.wr, self.rd, self.mark = {}, {}, {}
        self.pend_w = {e: [] for e in 'vstgy'}
        self.pend_r = {e: [] for e in 'vstgy'}

    def _wait(self, e, tok):
        sem_name, val = tok
        if self.mark.get((e, sem_name), 0) >= val:
            return
        self.eng[e].wait_ge(self.sem[sem_name], val)
        self.mark[(e, sem_name)] = val

    def _pre(self, e, reads, writes):
        for k in reads:
            t = self.wr.get(k)
            if t is not None:
                self._wait(e, t)
        for k in writes:
            t = self.wr.get(k)
            if t is not None:
                self._wait(e, t)
            for t2 in self.rd.get(k, ()):
                self._wait(e, t2)

    def emit(self, e, reads, writes, fn, inc=True):
        self._pre(e, reads, writes)
        instr = fn(self.eng[e])
        if inc:
            self.cnt[e] += 1
            instr.then_inc(self.sem[e], 1)
            tok = (e, self.cnt[e])
            for k in self.pend_w[e]:
                self.wr[k] = tok
                self.rd[k] = []
            for k in self.pend_r[e]:
                self.rd.setdefault(k, []).append(tok)
            self.pend_w[e], self.pend_r[e] = [], []
            for k in writes:
                self.wr[k] = tok
                self.rd[k] = []
            for k in reads:
                self.rd.setdefault(k, []).append(tok)
            return tok
        self.pend_w[e].extend(writes)
        self.pend_r[e].extend(reads)
        return None

    def dma(self, e, reads, writes, fn):
        dn = f'd{e}{self.dma_idx[e]}'
        self.dma_idx[e] = (self.dma_idx[e] + 1) % self.NDMA
        self._wait(e, (dn, self.cnt[dn]))
        self._pre(e, reads, writes)
        instr = fn(self.eng[e])
        self.cnt[dn] += 16
        instr.then_inc(self.sem[dn], 16)
        tok = (dn, self.cnt[dn])
        for k in writes:
            self.wr[k] = tok
            self.rd[k] = []
        for k in reads:
            self.rd.setdefault(k, []).append(tok)
        return tok

    def barrier(self):
        for e in 'vstgy':
            assert not self.pend_w[e] and not self.pend_r[e], e
        for e in 'vstgy':
            for n in self.sem:
                self._wait(e, (n, self.cnt[n]))
        self.wr.clear()
        self.rd.clear()


def build(phases=PHASES):
    nc = bass.Bass("TRN2", target_bir_lowering=False)
    DT = lambda n, s_, d=F32, k="ExternalInput": nc.dram_tensor(n, s_, d, kind=k)
    xin = DT("xin", [S, D])
    xout = DT("xout", [S, D], F32, "ExternalOutput")
    wq_a, wk_a, wv_a = DT("wq_a", [D, D]), DT("wk_a", [D, D]), DT("wv_a", [D, D])
    qb_a, kb_a, vb_a = DT("qb_a", [D, 1]), DT("kb_a", [D, 1]), DT("vb_a", [1, D])
    lngb = DT("lngb", [128, 4 * D])
    ff1a_w, ff1a_b = DT("ff1a_w", [FFA, D]), DT("ff1a_b", [FFA, 1])
    ff2a_w, ff2a_b = DT("ff2a_w", [FFA, D]), DT("ff2a_b", [1, D])
    w1q_w, b1q_w = DT("w1q", [FFQ, D]), DT("b1q", [FFQ, 1])
    w2q_w, b2q_w = DT("w2q", [FFQ, D]), DT("b2q", [1, D])
    wq_b, wk_b = DT("wq_b", [D, D]), DT("wk_b", [D, D])
    wv_b, wo_b = DT("wv_b", [D, D]), DT("wo_b", [D, D])
    bo_b = DT("bo_b", [1, D])
    identc = DT("identc", [128, 128], BF16)
    maskc = DT("maskc", [128, 128], BF16)
    onesf = DT("onesf", [1, 128])
    hselc = DT("hselc", [128, 2], BF16)
    wqc = nc.dram_tensor("wqc", [FFQ, D], FP8)
    dbg = nc.dram_tensor("dbg", [128, 512], BF16, kind="ExternalOutput")

    es = ExitStack()
    with es:
        sb = lambda nm, shp, dt=F32: es.enter_context(nc.sbuf_tensor(nm, shp, dt))
        psb = lambda nm, shp, dt=F32: es.enter_context(nc.psum_tensor(nm, shp, dt))
        x = sb("x", [128, NT * D])
        xfm = sb("xfm", [128, ND * S], BF16)
        stg = sb("stg", [128, 1024])
        cons = sb("cons", [128, 16])
        ident = sb("ident", [128, 128], BF16)
        mask = sb("mask", [128, 128], BF16)
        ones_f = sb("ones_f", [1, 128])
        hsel = sb("hsel", [128, 2], BF16)
        infp = sb("infp", [128, 1])
        lnb = sb("lnb", [128, 16])
        pcb = sb("pcb", [128, 64])
        ybf = sb("ybf", [128, 2048], BF16)
        pb = [psb(f"pb{i}", [128, 512]) for i in range(7)]
        pT = psb("pT", [128, 1024], BF16)

        p = Prog(nc, es)
        E, DMA = p.emit, p.dma
        _uid = [0]

        def uname(base):
            _uid[0] += 1
            return f"{base}_{_uid[0]}"
        PK = lambda i: ('ps', i)

        def mmg(bank_key, parts):
            n = len(parts)
            for i, (reads, fn) in enumerate(parts):
                E('t', reads, [bank_key] if (i == 0 or i == n - 1) else [],
                  fn, inc=(i == n - 1))

        DMA('y', [], ['ident'], lambda e: e.dma_start(ident[:], identc[:]))
        DMA('y', [], ['mask'], lambda e: e.dma_start(mask[:], maskc[:]))
        DMA('y', [], ['ones_f'], lambda e: e.dma_start(ones_f[:], onesf[:]))
        DMA('y', [], ['hsel'], lambda e: e.dma_start(hsel[:], hselc[:]))
        for c in range(8):
            DMA('y', [], ['pcb'], lambda e, c=c: e.dma_start(
                pcb[:, c:c + 1], qb_a.rearrange("(c p) o -> c p o", p=128)[c]))
            DMA('y', [], ['pcb'], lambda e, c=c: e.dma_start(
                pcb[:, 8 + c:9 + c], kb_a.rearrange("(c p) o -> c p o", p=128)[c]))
        for c in range(32):
            DMA('y', [], ['pcb'], lambda e, c=c: e.dma_start(
                pcb[:, 16 + c:17 + c], b1q_w.rearrange("(c p) o -> c p o", p=128)[c]))
        for c in range(16):
            DMA('y', [], ['pcb'], lambda e, c=c: e.dma_start(
                pcb[:, 48 + c:49 + c], ff1a_b.rearrange("(c p) o -> c p o", p=128)[c]))
        for t in range(NT):
            DMA('y', [], [('x', t)], lambda e, t=t: e.dma_start(
                x[:, t * D:(t + 1) * D], xin.rearrange("(t p) d -> t p d", p=128)[t]))
        E('v', [], ['cons'], lambda e: e.memset(cons[:, 5:6], 1.0))
        E('v', ['cons'], ['cons'], lambda e: e.memset(cons[:, 0:1], 2.0 ** 13))
        for k in range(17, 53):
            tk = float(2.0 ** k)
            E('v', ['cons'], ['cons'], lambda e, tk=tk: e.tensor_scalar(
                cons[:, 6:7], cons[:, 5:6], tk, 1.0, ALU.add, ALU.mult))
            E('v', ['cons'], ['cons'], lambda e, tk=tk: e.tensor_scalar(
                cons[:, 6:7], cons[:, 6:7], tk, 1.0, ALU.subtract, ALU.mult))
            E('v', ['cons'], ['cons'], lambda e: e.tensor_scalar(
                cons[:, 7:8], cons[:, 6:7], 1.0, 1.0, ALU.add, ALU.mult))
            E('v', ['cons'], ['cons'], lambda e: e.tensor_tensor(
                cons[:, 0:1], cons[:, 0:1], cons[:, 7:8], ALU.mult))
        E('v', ['cons'], ['cons'], lambda e: e.tensor_scalar(
            cons[:, 1:2], cons[:, 0:1], 1.0, 1.0, ALU.add, ALU.mult))
        E('v', ['cons'], ['cons'], lambda e: e.tensor_scalar(
            cons[:, 2:3], cons[:, 0:1], float(12.0 * 2.0 ** -9), 1.0, ALU.mult, ALU.mult))
        E('v', ['cons'], ['cons'], lambda e: e.tensor_scalar(
            cons[:, 3:4], cons[:, 0:1], float(2.0 ** -7), 1.0, ALU.mult, ALU.mult))
        E('v', ['cons'], ['cons'], lambda e: e.tensor_scalar(
            cons[:, 3:4], cons[:, 3:4], 1.0, 1.0, ALU.add, ALU.mult))
        E('v', ['cons'], ['cons'], lambda e: e.tensor_scalar(
            cons[:, 4:5], cons[:, 0:1], float(12.0 * 2.0 ** -24), 1.0, ALU.mult, ALU.mult))
        E('v', ['cons'], ['cons'], lambda e: e.tensor_scalar(
            cons[:, 8:9], cons[:, 2:3], -1.0, 1.0, ALU.mult, ALU.mult))
        E('v', [], ['infp'], lambda e: e.memset(infp[:], float('inf')))
        E('v', ['pcb'], ['pcb'], lambda e: e.tensor_scalar(
            pcb[:, 0:8], pcb[:, 0:8], 0.125, 1.0, ALU.mult, ALU.mult))

        def ln_tile(src_key, src_ap, eps=LN_EPS):
            L = lnb
            E('v', [src_key, 'lnb'], ['lnb'], lambda e: e.bn_stats(L[:, 0:6], src_ap[:, 0:512]))
            E('v', [src_key, 'lnb'], ['lnb'], lambda e: e.bn_stats(L[:, 6:12], src_ap[:, 512:1024]))
            ms = [L[:, c:c + 1] for c in (1, 4, 7, 10)]
            cv = [L[:, c:c + 1] for c in (2, 5, 8, 11)]
            ee = lambda f: E('v', ['lnb'], ['lnb'], f)
            ee(lambda e: e.tensor_tensor(L[:, 12:13], ms[0], ms[1], ALU.add))
            ee(lambda e: e.tensor_tensor(L[:, 13:14], ms[2], ms[3], ALU.add))
            ee(lambda e: e.tensor_tensor(L[:, 12:13], L[:, 12:13], L[:, 13:14], ALU.add))
            ee(lambda e: e.tensor_scalar(L[:, 12:13], L[:, 12:13], 0.25, 1.0, ALU.mult, ALU.mult))
            ee(lambda e: e.tensor_tensor(L[:, 14:15], cv[0], cv[1], ALU.add))
            ee(lambda e: e.tensor_tensor(L[:, 15:16], cv[2], cv[3], ALU.add))
            ee(lambda e: e.tensor_tensor(L[:, 14:15], L[:, 14:15], L[:, 15:16], ALU.add))
            ee(lambda e: e.tensor_scalar(L[:, 14:15], L[:, 14:15], float(1 / 1024.0), 1.0, ALU.mult, ALU.mult))
            for m in ms:
                ee(lambda e, m=m: e.tensor_tensor(L[:, 15:16], m, m, ALU.mult))
                ee(lambda e: e.scalar_tensor_tensor(L[:, 14:15], L[:, 15:16], 0.25,
                                                    L[:, 14:15], ALU.mult, ALU.add))
            ee(lambda e: e.tensor_tensor(L[:, 15:16], L[:, 12:13], L[:, 12:13], ALU.mult))
            ee(lambda e: e.tensor_tensor(L[:, 14:15], L[:, 14:15], L[:, 15:16], ALU.subtract))
            ee(lambda e, ep=eps: e.tensor_scalar(L[:, 14:15], L[:, 14:15], float(ep), 1.0, ALU.add, ALU.mult))
            E('s', ['lnb'], ['lnb'], lambda e: e.activation(L[:, 14:15], L[:, 14:15], AF.Sqrt))
            ee(lambda e: e.reciprocal(L[:, 14:15], L[:, 14:15]))
            ee(lambda e: e.tensor_tensor(L[:, 13:14], L[:, 12:13], L[:, 14:15], ALU.mult))
            ee(lambda e: e.tensor_scalar(L[:, 13:14], L[:, 13:14], -1.0, 1.0, ALU.mult, ALU.mult))
            return L[:, 14:15], L[:, 13:14]

        def tile_to_xfm(src_key, src_bf_ap, t):
            for dcI in range(ND):
                E('t', [src_key, 'ident'],
                  [('pT',)] if dcI in (0, ND - 1) else [],
                  lambda e, dcI=dcI: e.transpose(
                      pT[:, dcI * 128:(dcI + 1) * 128],
                      src_bf_ap[:, dcI * 128:(dcI + 1) * 128], ident[:]),
                  inc=(dcI == ND - 1))
            dst = bass.AP(xfm, t * 128, [[ND * S, 128], [S, ND], [1, 128]])
            srcp = bass.AP(pT, 0, [[1024, 128], [128, ND], [1, 128]])
            E('v', [('pT',)], [('xfm', t)], lambda e: e.tensor_copy(dst, srcp))

        def ln_to_xfm(src_of, f32_dst_of=None):
            for t in range(NT):
                sk, sap = src_of(t)
                rstd, nmu = ln_tile(sk, sap)
                yb = ybf[:, (t % 2) * 1024:(t % 2 + 1) * 1024]
                ybk = ('ybf', t % 2)
                if f32_dst_of is not None:
                    dk, dap = f32_dst_of(t)
                    E('v', [sk, 'lnb'], [dk], lambda e, sap=sap, dap=dap, r=rstd, n=nmu:
                      e.tensor_scalar(dap, sap, r, n, ALU.mult, ALU.add))
                    E('v', [dk], [ybk], lambda e, dap=dap, yb=yb: e.tensor_copy(yb, dap))
                else:
                    E('v', [sk, 'lnb'], [ybk], lambda e, sap=sap, yb=yb, r=rstd, n=nmu:
                      e.tensor_scalar(yb, sap, r, n, ALU.mult, ALU.add))
                tile_to_xfm(ybk, yb, t)
            p.barrier()

        def emit_quant(src_key, src_ap, dst_key, dst_ap, sc, sck, m1, qp,
                       signed=False, n=512):
            T = sck
            srcs = (src_key if isinstance(src_key, list) else [src_key])
            npart = src_ap.shape[0]
            c1 = cons[0:npart, qp['c1c']:qp['c1c'] + 1]
            K = cons[0:npart, qp['kc']:qp['kc'] + 1]
            hpath = (not signed) and qp is QE4
            if signed:
                E('s', srcs + [T], [T], lambda e: e.activation(sc['f'], src_ap, AF.Abs))
                mag = sc['f']
            else:
                mag = src_ap
            E('v', srcs + [T], [T], lambda e: e.tensor_scalar(m1, mag, float(qp['thr']), None, ALU.is_ge))
            E('v', srcs + [T], [T], lambda e: e.tensor_scalar(
                sc['f'], mag, float(qp['ovf']), None, ALU.is_ge))
            E('v', [T], [T], lambda e: e.tensor_scalar(
                sc['f'], sc['f'], 3.4e38, 1.0, ALU.mult, ALU.add))
            if hpath:
                negK = cons[0:npart, 8:9]
                E('s', srcs + ['cons', T], [T], lambda e: e.activation(
                    sc['y'], src_ap, AF.Relu, bias=0.0, scale=c1))
                E('g', srcs + [T], [T], lambda e: e.tensor_tensor(sc['t'], sc['y'], src_ap, ALU.subtract))
                E('g', [T], [T], lambda e: e.tensor_tensor(sc['y'], sc['y'], sc['t'], ALU.subtract))
                E('g', [T], [T], lambda e: e.tensor_tensor(sc['y'], sc['y'], sc['f'], ALU.mult))
                E('s', srcs + ['cons', T], [T], lambda e: e.activation(
                    sc['u'], src_ap, AF.Relu, bias=K, scale=1.0))
                E('s', ['cons', T], [dst_key, T], lambda e: e.activation(
                    dst_ap, sc['u'], AF.Relu, bias=negK, scale=1.0))
            else:
                E('v', srcs + ['cons', T], [T], lambda e: e.tensor_scalar(
                    sc['y'], src_ap, c1, 1.0, ALU.mult, ALU.mult))
                E('g', srcs + [T], [T], lambda e: e.tensor_tensor(sc['t'], sc['y'], src_ap, ALU.subtract))
                E('g', [T], [T], lambda e: e.tensor_tensor(sc['y'], sc['y'], sc['t'], ALU.subtract))
                E('g', [T], [T], lambda e: e.tensor_tensor(sc['y'], sc['y'], sc['f'], ALU.mult))
                E('v', srcs + ['cons', T], [T], lambda e: e.tensor_scalar(
                    sc['u'], src_ap, K, 1.0, ALU.add, ALU.mult))
                E('v', ['cons', T], [dst_key, T], lambda e: e.tensor_scalar(
                    dst_ap, sc['u'], K, 1.0, ALU.subtract, ALU.mult))
            E('v', [dst_key, T], [dst_key], lambda e: e.copy_predicated(
                dst_ap, m1, sc['y']))

        def phase_attn_a():
            with ExitStack() as ps_:
                A = ps_.enter_context(nc.sbuf_tensor(uname("A"), [128, NT * D], F32))
                ps_in = ps_.enter_context(ExitStack())
                aB = ps_in.enter_context(nc.sbuf_tensor(uname("aB"), [128, 10240], BF16))
                vrow = ps_in.enter_context(nc.sbuf_tensor(uname("vrow"), [1, D], F32))
                DMA('y', [], ['vrow'], lambda e: e.dma_start(vrow[:], vb_a[:]))
                ln_to_xfm(lambda t: (('x', t), x[:, t * D:(t + 1) * D]),
                          f32_dst_of=lambda t: (('A', t), A[:, t * D:(t + 1) * D]))
                Qp, Kp = aB[:, 0:2048], aB[:, 2048:4096]
                PTr = [aB[:, 4096 + i * 256:4096 + (i + 1) * 256] for i in range(4)]
                wq_t, wk_t = aB[:, 5120:6144], aB[:, 6144:7168]
                VB = 7168
                for pr in range(8):
                    for wdram, wtile, wkn, soff in ((wq_a, wq_t, 'wqp', 0),
                                                    (wk_a, wk_t, 'wkp', 0)):
                        src = bass.AP(wdram, pr * 128 * D, [[D, 128], [1, D]])
                        DMA('y', [], [('stg', soff)], lambda e, src=src, soff=soff:
                            e.dma_start(stg[:, soff:soff + 1024], src))
                        E('v', [('stg', soff)], [(wkn, pr)],
                          lambda e, soff=soff, wtile=wtile: e.tensor_copy(
                              wtile, stg[:, soff:soff + 1024]))
                    for which, (wkn, wtile, dst, boff, scale) in enumerate(
                            (('wqp', wq_t, Qp, 0, 0.125), ('wkp', wk_t, Kp, 8, 1.0))):
                        for tg in range(4):
                            mmg(PK(which), [
                                ([(wkn, pr)],
                                 (lambda e, wtile=wtile, dc=dc, tg=tg, w=which: e.matmul(
                                     pb[w][:], wtile[:, dc * 128:(dc + 1) * 128],
                                     xfm[:, dc * S + tg * 512:dc * S + (tg + 1) * 512],
                                     start=(dc == 0), stop=(dc == ND - 1))))
                                for dc in range(ND)])
                            E('v', [PK(which), 'pcb'], [('qk', pr, which)],
                              lambda e, dst=dst, tg=tg, b=boff, sc_=scale, w=which:
                              e.tensor_scalar(dst[:, tg * 512:(tg + 1) * 512], pb[w][:],
                                              sc_, pcb[:, b + pr:b + pr + 1], ALU.mult, ALU.add))
                    src = bass.AP(wv_a, pr * 128 * D, [[D, 128], [1, D]])
                    DMA('y', [], [('stg', 0)], lambda e, src=src: e.dma_start(
                        stg[:, 0:1024], src))
                    E('v', [('stg', 0), ('wqp', pr)], [('wvp', pr)],
                      lambda e: e.tensor_copy(wq_t, stg[:, 0:1024]))
                    for t in range(NT):
                        mmg(PK(2),
                            [([('wvp', pr)],
                              (lambda e, dc=dc, t=t: e.matmul(
                                  pb[2][:, 0:128],
                                  xfm[:, dc * S + t * 128:dc * S + (t + 1) * 128],
                                  wq_t[:, dc * 128:(dc + 1) * 128],
                                  start=(dc == 0), stop=False)))
                             for dc in range(ND)] +
                            [(['ones_f', 'vrow'],
                              lambda e: e.matmul(pb[2][:, 0:128], ones_f[:],
                                                 vrow[0:1, pr * 128:(pr + 1) * 128],
                                                 start=False, stop=True))])
                        E('v', [PK(2)], [('vau', pr, t)], lambda e, t=t: e.tensor_copy(
                            bass.AP(aB, VB + t * 130, [[10240, 128], [65, 2], [1, 64]]),
                            bass.AP(pb[2], 0, [[512, 128], [64, 2], [1, 64]])))
                        E('v', [], [('vau', pr, t)], lambda e, t=t: e.memset(
                            bass.AP(aB, VB + t * 130 + 64, [[10240, 128], [65, 2], [1, 1]]), 1.0))
                    for kw in range(NT):
                        qn = 256 if kw < NT - 1 else 128
                        kwp = kw % 2
                        pA, pB_ = pb[3 + kwp], pb[5 + kwp]
                        E('t', [('qk', pr, 0), ('qk', pr, 1)], [PK(3 + kwp)],
                          lambda e, kw=kw, qn=qn, pA=pA: e.matmul(
                              pA[:, 0:qn], Kp[0:64, kw * 128:(kw + 1) * 128],
                              Qp[0:64, kw * 128:kw * 128 + qn], start=True, stop=True,
                              tile_position=(0, 0)))
                        E('t', [], [PK(5 + kwp)],
                          lambda e, kw=kw, qn=qn, pB_=pB_: e.matmul(
                              pB_[:, 0:qn], Kp[64:128, kw * 128:(kw + 1) * 128],
                              Qp[64:128, kw * 128:kw * 128 + qn], start=True, stop=True,
                              tile_position=(64, 0)))
                        for hh, pp, bk in ((0, pA, PK(3 + kwp)), (1, pB_, PK(5 + kwp))):
                            rt = PTr[kwp * 2 + hh]
                            E('s', [bk], [('pt', hh, kwp)], lambda e, rt=rt, qn=qn, pp=pp:
                              e.activation(rt[:, 0:qn], pp[:, 0:qn], AF.Exp))
                            E('v', [('pt', hh, kwp), 'mask'], [('pt', hh, kwp)],
                              lambda e, rt=rt: e.tensor_tensor(
                                  rt[:, 0:128], rt[:, 0:128], mask[:], ALU.mult))
                        pav = pb[kwp]
                        parts = []
                        for hh in range(2):
                            rprev = PTr[((kw - 1) % 2) * 2 + hh]
                            rown = PTr[kwp * 2 + hh]
                            vsl = bass.AP(aB, VB + kw * 130 + hh * 65, [[10240, 128], [1, 65]])
                            vslp = bass.AP(aB, VB + (kw - 1) * 130 + hh * 65, [[10240, 128], [1, 65]])
                            if kw > 0:
                                parts.append(([('pt', hh, (kw - 1) % 2), ('vau', pr, kw - 1)],
                                              (lambda e, rprev=rprev, vslp=vslp, hh=hh:
                                               e.matmul(pav[:, hh * 65:(hh + 1) * 65],
                                                        rprev[:, 128:256], vslp,
                                                        start=True, stop=False))))
                            parts.append(([('pt', hh, kwp), ('vau', pr, kw)],
                                          (lambda e, rown=rown, vsl=vsl, hh=hh, kw=kw:
                                           e.matmul(pav[:, hh * 65:(hh + 1) * 65],
                                                    rown[:, 0:128], vsl,
                                                    start=(kw == 0), stop=True))))
                        mmg(PK(kwp), parts)
                        for hh in range(2):
                            hcol = (2 * pr + hh) * 64
                            E('v', [PK(kwp), 'lnb'], ['lnb'], lambda e, hh=hh, pav=pav:
                              e.reciprocal(lnb[:, hh:hh + 1], pav[:, hh * 65 + 64:hh * 65 + 65]))
                            E('v', [PK(kwp), 'lnb', ('A', kw)], [('A', kw)],
                              lambda e, hh=hh, kw=kw, hcol=hcol, pav=pav: e.scalar_tensor_tensor(
                                  A[:, kw * D + hcol:kw * D + hcol + 64],
                                  pav[:, hh * 65:hh * 65 + 64], lnb[:, hh:hh + 1],
                                  A[:, kw * D + hcol:kw * D + hcol + 64], ALU.mult, ALU.add))
                p.barrier()
                if AA_STOP[0] == 1:
                    for t in range(NT):
                        E('v', [('A', t)], [('x', t)], lambda e, t=t: e.tensor_copy(
                            x[:, t * D:(t + 1) * D], A[:, t * D:(t + 1) * D]))
                    p.barrier()
                    return
                ps_in.close()
                with ExitStack() as ps2:
                    def inner_ln(gbt, add_to_x):
                        for t in range(NT):
                            Aap = A[:, t * D:(t + 1) * D]
                            rstd, nmu = ln_tile(('A', t), Aap)
                            E('v', [('A', t), 'lnb'], [('A', t)], lambda e, Aap=Aap, r=rstd, n=nmu:
                              e.tensor_scalar(Aap, Aap, r, n, ALU.mult, ALU.add))
                            E('v', [('A', t), 'gbt'], [('A', t)], lambda e, Aap=Aap:
                              e.tensor_tensor(Aap, Aap, gbt[:, 0:D], ALU.mult))
                            E('v', [('A', t), 'gbt'], [('A', t)], lambda e, Aap=Aap:
                              e.tensor_tensor(Aap, Aap, gbt[:, D:2 * D], ALU.add))
                            if add_to_x:
                                E('v', [('A', t), ('x', t)], [('x', t)], lambda e, t=t, Aap=Aap:
                                  e.tensor_tensor(x[:, t * D:(t + 1) * D],
                                                  x[:, t * D:(t + 1) * D], Aap, ALU.add))
                            else:
                                yb = ybf[:, (t % 2) * 1024:(t % 2 + 1) * 1024]
                                E('v', [('A', t)], [('ybf', t % 2)], lambda e, Aap=Aap, yb=yb:
                                  e.tensor_copy(yb, Aap))
                                tile_to_xfm(('ybf', t % 2), yb, t)
                    with nc.sbuf_tensor(uname("gbt1"), [128, 2 * D], F32) as gbt:
                        DMA('y', [], ['gbt'], lambda e: e.dma_start(gbt[:], lngb[:, 0:2 * D]))
                        inner_ln(gbt, add_to_x=False)
                        p.barrier()
                    if AA_STOP[0] == 2:
                        for t in range(NT):
                            E('v', [('A', t)], [('x', t)], lambda e, t=t: e.tensor_copy(
                                x[:, t * D:(t + 1) * D], A[:, t * D:(t + 1) * D]))
                        p.barrier()
                        return
                    emit_ffn(A, lambda t: ('A', t), hid=FFA, w1d=ff1a_w, b1off=48,
                             w2d=ff2a_w, brow_dram=ff2a_b, quant=False, tokgrp=512)
                    p.barrier()
                    if AA_STOP[0] == 3:
                        for t in range(NT):
                            E('v', [('A', t)], [('x', t)], lambda e, t=t: e.tensor_copy(
                                x[:, t * D:(t + 1) * D], A[:, t * D:(t + 1) * D]))
                        p.barrier()
                        return
                    with nc.sbuf_tensor(uname("gbt2"), [128, 2 * D], F32) as gbt:
                        DMA('y', [], ['gbt'], lambda e: e.dma_start(gbt[:], lngb[:, 2 * D:4 * D]))
                        inner_ln(gbt, add_to_x=True)
                        p.barrier()

        wqc_state = {'done': False}

        def emit_ffn(rbuf, rkey, hid, w1d, b1off, w2d, brow_dram, quant,
                     tokgrp=1024):
            nj = hid // 128
            ngj = tokgrp // 512
            with ExitStack() as ps_:
                fw = ps_.enter_context(nc.sbuf_tensor(uname("fw"), [128, 2048], BF16))
                fstg = ps_.enter_context(nc.sbuf_tensor(uname("fstg"), [128, 2048], F32))
                brow = ps_.enter_context(nc.sbuf_tensor(uname("fbrow"), [1, D], F32))
                if quant:
                    fh = ps_.enter_context(nc.sbuf_tensor(uname("fh"), [128, 2 * tokgrp], F32))
                    fsc = ps_.enter_context(nc.sbuf_tensor(uname("fsc"), [128, 6 * tokgrp], F32))
                    fm8 = ps_.enter_context(nc.sbuf_tensor(uname("fm8"), [128, 2 * tokgrp], U8))
                    hq8 = ps_.enter_context(
                        nc.sbuf_tensor(uname("fq8"), [128, nj * tokgrp + 1536], FP8))
                    SC = [dict(y=fsc[:, p * 3 * tokgrp + 0 * tokgrp:p * 3 * tokgrp + 1 * tokgrp],
                               t=fsc[:, p * 3 * tokgrp + 1 * tokgrp:p * 3 * tokgrp + 2 * tokgrp],
                               u=fsc[:, p * 3 * tokgrp + 1 * tokgrp:p * 3 * tokgrp + 2 * tokgrp],
                               f=fsc[:, p * 3 * tokgrp + 2 * tokgrp:p * 3 * tokgrp + 3 * tokgrp])
                          for p in range(2)]
                    M1 = [fm8[:, p * tokgrp:(p + 1) * tokgrp] for p in range(2)]
                else:
                    hB = ps_.enter_context(
                        nc.sbuf_tensor(uname("fhB"), [128, nj * tokgrp], BF16))
                DMA('y', [], ['brow'], lambda e: e.dma_start(brow[:], brow_dram[:]))
                if quant:
                    scW = dict(y=fsc[0:1, 0:512], t=fsc[0:1, 512:1024],
                               u=fsc[0:1, 512:1024], f=fsc[0:1, 1024:1536])
                    for hf in range(2):
                        emit_quant('brow', brow[0:1, hf * 512:(hf + 1) * 512], 'brow',
                                   brow[0:1, hf * 512:(hf + 1) * 512],
                                   scW, ('fscr', 0), fm8[0:1, 0:512],
                                   QE5, signed=True, n=512)
                building = quant and not wqc_state['done']
                emit_wqc_chain = None
                if building:
                    wstg1 = ps_.enter_context(
                        nc.sbuf_tensor(uname("wstg1"), [128, 1024], F32))
                    wslot = [stg[:, 0:1024], wstg1[:, 0:1024]]
                    wsc3 = ps_.enter_context(
                        nc.sbuf_tensor(uname("wsc3"), [128, 1536], F32))
                    wm1k = ps_.enter_context(
                        nc.sbuf_tensor(uname("wm1k"), [128, 512], U8))
                    wq82 = ps_.enter_context(
                        nc.sbuf_tensor(uname("wq82"), [128, 2048], FP8))
                    scQ = dict(y=wsc3[:, 0:512], t=wsc3[:, 512:1024],
                               u=wsc3[:, 512:1024], f=wsc3[:, 1024:1536])

                    def emit_wqc_chain(j):
                        jw = j % 2
                        DMA('y' if jw == 0 else 'g', [], [('wstg', jw)],
                            lambda e, j=j, jw=jw: e.dma_start(
                                wslot[jw],
                                w2q_w.rearrange("(c p) d -> c p d", p=128)[j]))
                        for hf in range(2):
                            emit_quant(('wstg', jw), wslot[jw][:, hf * 512:(hf + 1) * 512],
                                       ('wq8w', jw),
                                       wq82[:, jw * 1024 + hf * 512:jw * 1024 + (hf + 1) * 512],
                                       scQ, ('wscr',), wm1k[:], QE4, signed=True)
                        DMA('g' if jw == 0 else 'y', [('wq8w', jw)], [('wqc', j)],
                            lambda e, j=j, jw=jw: e.dma_start(
                                wqc.rearrange("(c p) d -> c p d", p=128)[j],
                                wq82[:, jw * 1024:(jw + 1) * 1024]))
                    wqc_state['done'] = True
                for half in range(S // tokgrp):
                    hb = half * tokgrp
                    for j in range(nj):
                        jb = j % 2
                        wsl = fw[:, jb * 1024:(jb + 1) * 1024]
                        aw = [('fws', jb)] + ([('fwsb', 0), ('fwsb', 1)] if jb == 0 else [('fwsb', 2)])
                        DMA('y' if jb == 0 else 'g', [('fw', jb)], aw,
                            lambda e, j=j, jb=jb: e.dma_start(
                                fstg[:, jb * 1024:(jb + 1) * 1024],
                                bass.AP(w1d, j * 128 * D, [[D, 128], [1, D]])))
                        cw = [('fw', jb)] + ([('fwb', 0), ('fwb', 1)] if jb == 0 else [('fwb', 2)])
                        E('v', [('fws', jb)], cw,
                          lambda e, jb=jb, wsl=wsl: e.tensor_copy(
                              wsl, fstg[:, jb * 1024:(jb + 1) * 1024]))
                        for tg in range(ngj):
                            pki = ngj * jb + tg
                            mmg(PK(pki), [
                                ([('fw', jb)],
                                 (lambda e, wsl=wsl, dc=dc, tg=tg, pki=pki: e.matmul(
                                     pb[pki][:], wsl[:, dc * 128:(dc + 1) * 128],
                                     xfm[:, dc * S + hb + tg * 512:dc * S + hb + (tg + 1) * 512],
                                     start=(dc == 0), stop=(dc == ND - 1))))
                                for dc in range(ND)])
                            if quant:
                                hsl = fh[:, jb * tokgrp + tg * 512:jb * tokgrp + (tg + 1) * 512]
                                E('s', [PK(pki), 'pcb', ('fh', jb, tg)], [('fh', jb, tg)],
                                  lambda e, pki=pki, j=j, hsl=hsl: e.activation(
                                      hsl, pb[pki][:], AF.Relu,
                                      bias=pcb[:, b1off + j:b1off + j + 1], scale=1.0))
                            else:
                                E('s', [PK(pki), 'pcb', ('h8', j, tg)], [('h8', j, tg)],
                                  lambda e, pki=pki, j=j, tg=tg: e.activation(
                                      hB[:, j * tokgrp + tg * 512:j * tokgrp + (tg + 1) * 512],
                                      pb[pki][:], AF.Relu,
                                      bias=pcb[:, b1off + j:b1off + j + 1], scale=1.0))
                        if quant:
                            emit_quant([('fh', jb, tg2) for tg2 in range(ngj)],
                                       fh[:, jb * tokgrp:(jb + 1) * tokgrp],
                                       ('h8', j, 'all'),
                                       hq8[:, j * tokgrp:(j + 1) * tokgrp],
                                       SC[jb], ('fscr', jb), M1[jb], QE4, n=tokgrp)
                        if emit_wqc_chain is not None and half == 0:
                            emit_wqc_chain(j)
                    for dc in range(2):
                        for sub in range(ngj):
                            first = [True] * 4
                            for j in range(nj):
                                jb = j % 3
                                if quant:
                                    rt = hq8[:, nj * tokgrp + jb * 512:nj * tokgrp + (jb + 1) * 512]
                                    DMA('y' if jb % 2 == 0 else 'g',
                                        [('fq', jb), ('wqc', j)], [('fq', jb)],
                                        lambda e, j=j, dc=dc, rt=rt: e.dma_start(
                                            rt, bass.AP(wqc, j * 128 * D + dc * 512,
                                                        [[D, 128], [1, 512]])))
                                    rk = ('fq', jb)
                                else:
                                    DMA('y', [('fwb', jb)], [('fwsb', jb), ('fws', jb // 2)],
                                        lambda e, j=j, dc=dc, jb=jb: e.dma_start(
                                            fstg[:, jb * 512:(jb + 1) * 512],
                                            bass.AP(w2d, j * 128 * D + dc * 512,
                                                    [[D, 128], [1, 512]])))
                                    rt = fw[:, jb * 512:(jb + 1) * 512]
                                    E('v', [('fwsb', jb)], [('fwb', jb), ('fw', jb // 2)],
                                      lambda e, jb=jb, rt=rt: e.tensor_copy(
                                          rt, fstg[:, jb * 512:(jb + 1) * 512]))
                                    rk = ('fwb', jb)
                                for tt in range(4):
                                    pki = 3 + tt
                                    lap = (hq8 if quant else hB)[:, j * tokgrp + sub * 512:
                                                                 j * tokgrp + sub * 512 + 512]
                                    hk = ('h8', j, 'all') if quant else ('h8', j, sub)
                                    E('t', [hk, rk],
                                      [PK(pki)] if first[tt] else [],
                                      lambda e, lap=lap, rt=rt, tt=tt, j=j, pki=pki: e.matmul(
                                          pb[pki][:], lap[:, tt * 128:(tt + 1) * 128], rt,
                                          start=(j == 0), stop=False), inc=(tt == 3))
                                    first[tt] = False
                            for tt in range(4):
                                pki = 3 + tt
                                E('t', ['brow', 'ones_f'], [PK(pki)],
                                  lambda e, dc=dc, pki=pki: e.matmul(
                                      pb[pki][:], ones_f[:], brow[0:1, dc * 512:(dc + 1) * 512],
                                      start=False, stop=True))
                            for tt in range(4):
                                pki = 3 + tt
                                t_tile = (hb + sub * 512 + tt * 128) // 128
                                rap = rbuf[:, t_tile * D + dc * 512:t_tile * D + (dc + 1) * 512]
                                if quant:
                                    par = tt % 2
                                    osl = fh[:, par * tokgrp:par * tokgrp + 512]
                                    ok = ('fh', par, 0)
                                    E('v', [PK(pki)], [ok, ('fh', par, 1)],
                                      lambda e, pki=pki, osl=osl: e.tensor_copy(osl, pb[pki][:]))
                                    emit_quant(ok, osl, ok, osl,
                                               {k: v[:, 0:512] for k, v in SC[par].items()},
                                               ('fscr', par), M1[par][:, 0:512], QE5, signed=True)
                                    E('v', [ok, rkey(t_tile)], [rkey(t_tile)],
                                      lambda e, osl=osl, rap=rap: e.scalar_tensor_tensor(
                                          rap, osl, 1.0, rap, ALU.mult, ALU.add))
                                else:
                                    E('v', [PK(pki), rkey(t_tile)], [rkey(t_tile)],
                                      lambda e, pki=pki, rap=rap: e.scalar_tensor_tensor(
                                          rap, pb[pki][:], 1.0, rap, ALU.mult, ALU.add))
                p.barrier()

        def phase_ffnq():
            ln_to_xfm(lambda t: (('x', t), x[:, t * D:(t + 1) * D]))
            emit_ffn(x, lambda t: ('x', t), hid=FFQ, w1d=w1q_w, b1off=16,
                     w2d=w2q_w, brow_dram=b2q_w, quant=True)

        def phase_attn_b():
            with ExitStack() as ps_:
                Kf = ps_.enter_context(nc.sbuf_tensor(uname("Kf"), [128, 16384], BF16))
                Va = ps_.enter_context(nc.sbuf_tensor(uname("Va"), [128, NT * 1040], BF16))
                Qs = ps_.enter_context(nc.sbuf_tensor(uname("Qs"), [128, 4096], BF16))
                af = ps_.enter_context(nc.sbuf_tensor(uname("af"), [128, 4096], BF16))
                ptb = ps_.enter_context(nc.sbuf_tensor(uname("ptb"), [128, 2048], BF16))
                wcs = ps_.enter_context(nc.sbuf_tensor(uname("wcs"), [128, 1024], BF16))
                sm2 = ps_.enter_context(nc.sbuf_tensor(uname("sm2"), [1, 3072], F32))
                rbc = ps_.enter_context(nc.sbuf_tensor(uname("rbc"), [64, 512], F32))
                kmx = ps_.enter_context(nc.sbuf_tensor(uname("kmx"), [1, 24], F32))
                borow = sm2[0:1, 2048:3072]
                ln_to_xfm(lambda t: (('x', t), x[:, t * D:(t + 1) * D]))
                DMA('y', ['sm2'], ['sm2'], lambda e: e.dma_start(borow[:], bo_b[:]))
                for pr in range(8):
                    src = bass.AP(wk_b, pr * 128 * D, [[D, 128], [1, D]])
                    DMA('y', [], [('stg', 0)], lambda e, src=src: e.dma_start(
                        stg[:, 0:1024], src))
                    E('v', [('stg', 0)], [('wcs',)], lambda e: e.tensor_copy(
                        wcs[:], stg[:, 0:1024]))
                    for tg in range(4):
                        mmg(PK(0), [
                            ([('wcs',)],
                             (lambda e, dc=dc, tg=tg: e.matmul(
                                 pb[0][:], wcs[:, dc * 128:(dc + 1) * 128],
                                 xfm[:, dc * S + tg * 512:dc * S + (tg + 1) * 512],
                                 start=(dc == 0), stop=(dc == ND - 1))))
                            for dc in range(ND)])
                        E('v', [PK(0)], [('kf', pr)], lambda e, pr=pr, tg=tg: e.tensor_copy(
                            Kf[:, pr * 2048 + tg * 512:pr * 2048 + (tg + 1) * 512], pb[0][:]))
                    E('s', [('kf', pr)], [('ksq',)], lambda e, pr=pr: e.activation(
                        ptb[:, 0:2048], Kf[:, pr * 2048:(pr + 1) * 2048], AF.Square))
                    for hh in range(2):
                        for tg in range(4):
                            E('t', [('ksq',), 'hsel'], [PK(1)], lambda e, tg=tg, hh=hh: e.matmul(
                                pb[1][0:1, 0:512], hsel[:, hh:hh + 1],
                                ptb[:, tg * 512:(tg + 1) * 512], start=True, stop=True))
                            E('v', [PK(1), 'kmx'], ['kmx'], lambda e: e.tensor_reduce(
                                kmx[0:1, 16:17], pb[1][0:1, 0:512], mybir.AxisListType.X, ALU.max))
                            col = hh * 8 + pr
                            if tg == 0:
                                E('v', ['kmx'], ['kmx'], lambda e, col=col: e.tensor_copy(
                                    kmx[0:1, col:col + 1], kmx[0:1, 16:17]))
                            else:
                                E('v', ['kmx'], ['kmx'], lambda e, col=col: e.tensor_tensor(
                                    kmx[0:1, col:col + 1], kmx[0:1, col:col + 1], kmx[0:1, 16:17], ALU.max))
                    src = bass.AP(wv_b, pr * 128 * D, [[D, 128], [1, D]])
                    DMA('y', [], [('stg', 0)], lambda e, src=src: e.dma_start(
                        stg[:, 0:1024], src))
                    E('v', [('stg', 0)], [('wcs',)], lambda e: e.tensor_copy(
                        wcs[:], stg[:, 0:1024]))
                    for t in range(NT):
                        mmg(PK(2), [
                            ([('wcs',)],
                             (lambda e, dc=dc, t=t: e.matmul(
                                 pb[2][:, 0:128],
                                 xfm[:, dc * S + t * 128:dc * S + (t + 1) * 128],
                                 wcs[:, dc * 128:(dc + 1) * 128],
                                 start=(dc == 0), stop=(dc == ND - 1))))
                            for dc in range(ND)])
                        E('v', [PK(2)], [('vab', pr, t)], lambda e, t=t, pr=pr: e.tensor_copy(
                            bass.AP(Va, t * 1040 + pr * 130, [[NT * 1040, 128], [65, 2], [1, 64]]),
                            bass.AP(pb[2], 0, [[512, 128], [64, 2], [1, 64]])))
                        E('v', [], [('vab', pr, t)], lambda e, t=t, pr=pr: e.memset(
                            bass.AP(Va, t * 1040 + pr * 130 + 64, [[NT * 1040, 128], [65, 2], [1, 1]]), 1.0))
                E('s', ['kmx'], ['kmx'], lambda e: e.activation(kmx[0:1, 0:16], kmx[0:1, 0:16], AF.Sqrt))
                p.barrier()
                for qg in (range(4) if AB_QGS[0] is None else AB_QGS[0]):
                    qb_ = qg * 512
                    for pr in range(8):
                        src = bass.AP(wq_b, pr * 128 * D, [[D, 128], [1, D]])
                        DMA('y', [], [('stg', 0)], lambda e, src=src: e.dma_start(
                            stg[:, 0:1024], src))
                        E('v', [('stg', 0)], [('wcs',)], lambda e: e.tensor_copy(
                            wcs[:], stg[:, 0:1024]))
                        mmg(PK(0), [
                            ([('wcs',)],
                             (lambda e, dc=dc, qb_=qb_: e.matmul(
                                 pb[0][:], wcs[:, dc * 128:(dc + 1) * 128],
                                 xfm[:, dc * S + qb_:dc * S + qb_ + 512],
                                 start=(dc == 0), stop=(dc == ND - 1))))
                            for dc in range(ND)])
                        E('v', [PK(0)], [('qs', pr)], lambda e, pr=pr: e.tensor_scalar(
                            Qs[:, pr * 512:(pr + 1) * 512], pb[0][:], 0.125, 1.0,
                            ALU.mult, ALU.mult))
                    for pr in (range(8) if AB_PRS[0] is None else AB_PRS[0]):
                        E('s', [('qs', pr)], [('wcs',)], lambda e, pr=pr: e.activation(
                            wcs[:, 0:512], Qs[:, pr * 512:(pr + 1) * 512], AF.Square))
                        for hh in range(2):
                            csl = sm2[0:1, hh * 512:(hh + 1) * 512]
                            E('t', [('wcs',), 'hsel'], [PK(6)], lambda e, hh=hh: e.matmul(
                                pb[6][0:1, 0:512], hsel[:, hh:hh + 1], wcs[:, 0:512],
                                start=True, stop=True))
                            E('s', [PK(6), 'sm2'], ['sm2'], lambda e, csl=csl: e.activation(
                                csl, pb[6][0:1, 0:512], AF.Sqrt))
                            col = hh * 8 + pr
                            E('v', ['sm2', 'kmx'], ['sm2'], lambda e, csl=csl, col=col:
                              e.tensor_scalar(csl, csl, kmx[0:1, col:col + 1], 1.0625,
                                              ALU.mult, ALU.mult))
                            E('s', ['sm2'], ['sm2'], lambda e, csl=csl: e.activation(
                                csl, csl, AF.Exp))
                        def emit_st(kt):
                            kp = kt % 2
                            pS0, pS1 = pb[3 + kp], pb[5 + kp]
                            E('t', [('kf', pr), ('qs', pr)], [PK(3 + kp)],
                              lambda e, kt=kt, pS0=pS0: e.matmul(
                                  pS0[:], Kf[0:64, pr * 2048 + kt * 128:pr * 2048 + (kt + 1) * 128],
                                  Qs[0:64, pr * 512:(pr + 1) * 512], start=True, stop=True,
                                  tile_position=(0, 0)))
                            E('t', [], [PK(5 + kp)],
                              lambda e, kt=kt, pS1=pS1: e.matmul(
                                  pS1[:], Kf[64:128, pr * 2048 + kt * 128:pr * 2048 + (kt + 1) * 128],
                                  Qs[64:128, pr * 512:(pr + 1) * 512], start=True, stop=True,
                                  tile_position=(64, 0)))
                            E('s', [PK(3 + kp)], [('ptb', 0, kp)],
                              lambda e, kp=kp, pS0=pS0: e.activation(
                                  ptb[:, kp * 512:(kp + 1) * 512], pS0[:], AF.Exp))
                            E('s', [PK(5 + kp)], [('ptb', 1, kp)],
                              lambda e, kp=kp, pS1=pS1: e.activation(
                                  ptb[:, (2 + kp) * 512:(3 + kp) * 512], pS1[:], AF.Exp))

                        def emit_av(kt):
                            kp = kt % 2
                            for hh in range(2):
                                psl = ptb[:, (2 * hh + kp) * 512:(2 * hh + kp + 1) * 512]
                                vsl = bass.AP(Va, kt * 1040 + pr * 130 + hh * 65,
                                              [[NT * 1040, 128], [1, 65]])
                                E('t', [('ptb', hh, kp), ('vab', pr, kt)],
                                  [PK(hh)] if kt in (0, NT - 1) else [],
                                  lambda e, psl=psl, vsl=vsl, hh=hh, kt=kt:
                                  e.matmul(pb[hh][0:65, :], vsl, psl,
                                           start=(kt == 0), stop=(kt == NT - 1)),
                                  inc=(kt == NT - 1))
                        emit_st(0)
                        for kt in range(NT):
                            if kt + 1 < NT:
                                emit_st(kt + 1)
                            emit_av(kt)
                        for hh in range(2):
                            dsl = sm2[0:1, 1024 + hh * 512:1536 + hh * 512]
                            E('v', [PK(hh), 'sm2'], ['sm2'], lambda e, hh=hh, dsl=dsl:
                              e.scalar_tensor_tensor(dsl,
                                                     sm2[0:1, hh * 512:(hh + 1) * 512],
                                                     ATT_EPS, pb[hh][64:65, :], ALU.mult, ALU.add))
                            E('v', ['sm2'], ['sm2'], lambda e, dsl=dsl: e.reciprocal(dsl, dsl))
                            E('t', ['sm2', 'ones_f'], [PK(6)], lambda e, dsl=dsl: e.matmul(
                                pb[6][0:64, :], ones_f[0:1, 0:64], dsl,
                                start=True, stop=True))
                            E('v', [PK(6)], [('rbc',)], lambda e: e.tensor_copy(
                                rbc[:], pb[6][0:64, :]))
                            E('v', [PK(hh), ('rbc',)], [('af', pr, hh)],
                              lambda e, hh=hh, pr=pr: e.tensor_tensor(
                                  af[hh * 64:(hh + 1) * 64, pr * 512:(pr + 1) * 512],
                                  pb[hh][0:64, :], rbc[:], ALU.mult))
                    for dc in (range(2) if AB_OUTP[0] else []):
                        first = [True] * 4
                        for ic in range(AB_NIC[0]):
                            DMA('y', [], [('stg', 0)], lambda e, ic=ic, dc=dc:
                                e.dma_start(stg[:, 0:512],
                                            bass.AP(wo_b, ic * 128 * D + dc * 512,
                                                    [[D, 128], [1, 512]])))
                            E('v', [('stg', 0)], [('wcs',)], lambda e: e.tensor_copy(
                                wcs[:, 0:512], stg[:, 0:512]))
                            for ts_ in range(4):
                                pki = 3 + ts_
                                E('t', [('af', ic, 0), ('af', ic, 1), ('wcs',)],
                                  [PK(pki)] if first[ts_] else [],
                                  lambda e, ic=ic, ts_=ts_, pki=pki: e.matmul(
                                      pb[pki][:],
                                      af[:, ic * 512 + ts_ * 128:ic * 512 + (ts_ + 1) * 128],
                                      wcs[:, 0:512], start=(ic == 0), stop=False), inc=(ts_ == 3))
                            if False:
                                pass
                                first[ts_] = False
                        for ts_ in range(4):
                            pki = 3 + ts_
                            E('t', ['sm2', 'ones_f'], [PK(pki)],
                              lambda e, dc=dc, pki=pki: e.matmul(
                                  pb[pki][:], ones_f[:], borow[0:1, dc * 512:(dc + 1) * 512],
                                  start=False, stop=True))
                        for ts_ in range(4):
                            pki = 3 + ts_
                            t_tile = qg * 4 + ts_
                            rap = x[:, t_tile * D + dc * 512:t_tile * D + (dc + 1) * 512]
                            if dc == 0 and ts_ == 0 and AB_DBG[0]:
                                E('v', [PK(pki), ('ptb', 0, 0)], [('dbgs',), ('ptb', 0, 0)],
                                  lambda e: e.tensor_copy(ptb[:, 0:512], pb[pki][:]))
                                DMA('g', [('dbgs',)], [], lambda e: e.dma_start(
                                    dbg[:], ptb[:, 0:512]))
                            E('v', [PK(pki), ('x', t_tile)], [('x', t_tile)],
                              lambda e, pki=pki, rap=rap: e.scalar_tensor_tensor(
                                  rap, pb[pki][:], 1.0, rap, ALU.mult, ALU.add))
                p.barrier()

        if "attn_a" in phases:
            phase_attn_a()
        if "ffnq1" in phases:
            phase_ffnq()
        if "attn_b" in phases:
            phase_attn_b()
        if "ffnq2" in phases:
            phase_ffnq()

        for t in range(NT):
            DMA('g', [('x', t)], [], lambda e, t=t: e.dma_start(
                xout.rearrange("(t p) d -> t p d", p=128)[t], x[:, t * D:(t + 1) * D]))
        for i in range(p.NDMA):
            p._wait('g', (f'dg{i}', p.cnt[f'dg{i}']))
    return nc


def _prep(inputs):
    import ml_dtypes
    g = lambda k: np.asarray(inputs[k], dtype=np.float32)
    o = {}
    rowc = lambda a: np.ascontiguousarray(
        a.reshape(8, 128, 8, 128).transpose(2, 1, 0, 3).reshape(D, D))
    o['wq_a'] = rowc(g('L0_attn_a_q_proj').transpose(1, 0, 2).reshape(D, D))
    o['wk_a'] = rowc(g('L0_attn_a_k_proj').transpose(1, 0, 2).reshape(D, D))
    o['wv_a'] = rowc(g('L0_attn_a_v_proj').transpose(1, 0, 2).reshape(D, D))
    o['qb_a'] = np.ascontiguousarray(g('L0_attn_a_q_bias').reshape(D, 1))
    o['kb_a'] = np.ascontiguousarray(g('L0_attn_a_k_bias').reshape(D, 1))
    o['vb_a'] = np.ascontiguousarray(g('L0_attn_a_v_bias').reshape(1, D))
    o['lngb'] = np.ascontiguousarray(np.broadcast_to(np.concatenate(
        [g('L0_attn_a_ln1_g'), g('L0_attn_a_ln1_b'),
         g('L0_attn_a_ln2_g'), g('L0_attn_a_ln2_b')])[None, :], (128, 4 * D)))
    o['ff1a_w'] = np.ascontiguousarray(
        g('L0_attn_a_ff1_w').reshape(D, FFA // 128, 128).transpose(1, 0, 2)
        .reshape(FFA // 128, 8, 128, 128).transpose(0, 2, 1, 3).reshape(FFA, D))
    o['ff1a_b'] = np.ascontiguousarray(g('L0_attn_a_ff1_b').reshape(FFA, 1))
    o['ff2a_w'] = np.ascontiguousarray(g('L0_attn_a_ff2_w'))
    o['ff2a_b'] = np.ascontiguousarray(g('L0_attn_a_ff2_b').reshape(1, D))
    o['w1q'] = np.ascontiguousarray(
        g('L0_ffn_w1').T.reshape(D, FFQ // 128, 128).transpose(1, 0, 2)
        .reshape(FFQ // 128, 8, 128, 128).transpose(0, 2, 1, 3).reshape(FFQ, D))
    o['b1q'] = np.ascontiguousarray(g('L0_ffn_b1').reshape(FFQ, 1))
    o['w2q'] = np.ascontiguousarray(g('L0_ffn_w2').T)
    o['b2q'] = np.ascontiguousarray(g('L0_ffn_b2').reshape(1, D))
    o['wq_b'] = rowc(np.ascontiguousarray(g('L0_attn_b_Wq').T))
    o['wk_b'] = rowc(np.ascontiguousarray(g('L0_attn_b_Wk').T))
    o['wv_b'] = rowc(np.ascontiguousarray(g('L0_attn_b_Wv').T))
    o['wo_b'] = np.ascontiguousarray(g('L0_attn_b_Wo').T)
    o['bo_b'] = np.ascontiguousarray(g('L0_attn_b_bo').reshape(1, D))
    o['identc'] = np.eye(128, dtype=ml_dtypes.bfloat16)
    km = np.arange(128)
    o['maskc'] = (km[:, None] <= km[None, :]).astype(ml_dtypes.bfloat16)
    o['onesf'] = np.ones((1, 128), dtype=np.float32)
    hs = np.zeros((128, 2), dtype=ml_dtypes.bfloat16)
    hs[0:64, 0] = 1
    hs[64:128, 1] = 1
    o['hselc'] = hs
    return o


def kernel(**inputs):
    xfull = np.asarray(inputs['L0_attn_a_x'], dtype=np.float32)
    shared = _prep(inputs)
    nc = build()
    in_maps = []
    for c in range(NCORES):
        m = dict(shared)
        m['xin'] = np.ascontiguousarray(xfull[c])
        in_maps.append(m)
    res = run_bass_kernel_spmd(nc, in_maps, core_ids=list(range(NCORES)))
    return np.stack([r['xout'] for r in res.results], axis=0)


# version 06
# speedup vs baseline: 1.0564x; 1.0564x over previous
import numpy as np
import concourse.bass as bass
import concourse.mybir as mybir
from concourse.bass_utils import run_bass_kernel_spmd
from contextlib import ExitStack

F32 = mybir.dt.float32
BF16 = mybir.dt.bfloat16
FP8 = mybir.dt.float8e4
U8 = mybir.dt.uint8
AF = mybir.ActivationFunctionType
ALU = mybir.AluOpType

NCORES = 8
S, D = 2048, 1024
NT, ND = S // 128, D // 128
FFA, FFQ = 2048, 4096
LN_EPS = 1e-5
ATT_EPS = 1e-10
QE4 = dict(thr=2.0 ** -6, ovf=248.0, c1c=1, kc=2)
QE5 = dict(thr=2.0 ** -14, ovf=65520.0, c1c=3, kc=4)
PHASES = ("attn_a", "ffnq1", "attn_b", "ffnq2")
AA_STOP = [0]
AB_QGS = [None]
AB_NIC = [8]
AB_DBG = [False]
AB_PRS = [None]
AB_OUTP = [True]


class Prog:

    def __init__(self, nc, es):
        self.nc = nc
        self.eng = {'v': nc.vector, 's': nc.scalar, 't': nc.tensor,
                    'g': nc.gpsimd, 'y': nc.sync}
        self.sem, self.cnt = {}, {}
        for e in 'vstgy':
            self.sem[e] = es.enter_context(nc.semaphore("sm_" + e))
            self.cnt[e] = 0
        self.NDMA = 8
        self.dma_idx = {}
        for e in 'gy':
            for i in range(self.NDMA):
                nm = f'd{e}{i}'
                self.sem[nm] = es.enter_context(nc.semaphore("smd_" + nm))
                self.cnt[nm] = 0
            self.dma_idx[e] = 0
        self.wr, self.rd, self.mark = {}, {}, {}
        self.pend_w = {e: [] for e in 'vstgy'}
        self.pend_r = {e: [] for e in 'vstgy'}

    def _wait(self, e, tok):
        sem_name, val = tok
        if self.mark.get((e, sem_name), 0) >= val:
            return
        self.eng[e].wait_ge(self.sem[sem_name], val)
        self.mark[(e, sem_name)] = val

    def _pre(self, e, reads, writes):
        for k in reads:
            t = self.wr.get(k)
            if t is not None:
                self._wait(e, t)
        for k in writes:
            t = self.wr.get(k)
            if t is not None:
                self._wait(e, t)
            for t2 in self.rd.get(k, ()):
                self._wait(e, t2)

    def emit(self, e, reads, writes, fn, inc=True):
        self._pre(e, reads, writes)
        instr = fn(self.eng[e])
        if inc:
            self.cnt[e] += 1
            instr.then_inc(self.sem[e], 1)
            tok = (e, self.cnt[e])
            for k in self.pend_w[e]:
                self.wr[k] = tok
                self.rd[k] = []
            for k in self.pend_r[e]:
                self.rd.setdefault(k, []).append(tok)
            self.pend_w[e], self.pend_r[e] = [], []
            for k in writes:
                self.wr[k] = tok
                self.rd[k] = []
            for k in reads:
                self.rd.setdefault(k, []).append(tok)
            return tok
        self.pend_w[e].extend(writes)
        self.pend_r[e].extend(reads)
        return None

    def dma(self, e, reads, writes, fn):
        dn = f'd{e}{self.dma_idx[e]}'
        self.dma_idx[e] = (self.dma_idx[e] + 1) % self.NDMA
        self._wait(e, (dn, self.cnt[dn]))
        self._pre(e, reads, writes)
        instr = fn(self.eng[e])
        self.cnt[dn] += 16
        instr.then_inc(self.sem[dn], 16)
        tok = (dn, self.cnt[dn])
        for k in writes:
            self.wr[k] = tok
            self.rd[k] = []
        for k in reads:
            self.rd.setdefault(k, []).append(tok)
        return tok

    def barrier(self):
        for e in 'vstgy':
            assert not self.pend_w[e] and not self.pend_r[e], e
        for e in 'vstgy':
            for n in self.sem:
                self._wait(e, (n, self.cnt[n]))
        self.wr.clear()
        self.rd.clear()


def build(phases=PHASES):
    nc = bass.Bass("TRN2", target_bir_lowering=False)
    DT = lambda n, s_, d=F32, k="ExternalInput": nc.dram_tensor(n, s_, d, kind=k)
    xin = DT("xin", [S, D])
    xout = DT("xout", [S, D], F32, "ExternalOutput")
    wq_a, wk_a, wv_a = DT("wq_a", [D, D]), DT("wk_a", [D, D]), DT("wv_a", [D, D])
    qb_a, kb_a, vb_a = DT("qb_a", [D, 1]), DT("kb_a", [D, 1]), DT("vb_a", [1, D])
    lngb = DT("lngb", [128, 4 * D])
    ff1a_w, ff1a_b = DT("ff1a_w", [FFA, D]), DT("ff1a_b", [FFA, 1])
    ff2a_w, ff2a_b = DT("ff2a_w", [FFA, D]), DT("ff2a_b", [1, D])
    w1q_w, b1q_w = DT("w1q", [FFQ, D]), DT("b1q", [FFQ, 1])
    w2q_w, b2q_w = DT("w2q", [FFQ, D]), DT("b2q", [1, D])
    wq_b, wk_b = DT("wq_b", [D, D]), DT("wk_b", [D, D])
    wv_b, wo_b = DT("wv_b", [D, D]), DT("wo_b", [D, D])
    bo_b = DT("bo_b", [1, D])
    identc = DT("identc", [128, 128], BF16)
    maskc = DT("maskc", [128, 128], BF16)
    onesf = DT("onesf", [1, 128])
    hselc = DT("hselc", [128, 2], BF16)
    wqc = nc.dram_tensor("wqc", [FFQ, D], FP8)
    dbg = nc.dram_tensor("dbg", [128, 512], BF16, kind="ExternalOutput")

    es = ExitStack()
    with es:
        sb = lambda nm, shp, dt=F32: es.enter_context(nc.sbuf_tensor(nm, shp, dt))
        psb = lambda nm, shp, dt=F32: es.enter_context(nc.psum_tensor(nm, shp, dt))
        x = sb("x", [128, NT * D])
        xfm = sb("xfm", [128, ND * S], BF16)
        stg = sb("stg", [128, 1024])
        cons = sb("cons", [128, 16])
        ident = sb("ident", [128, 128], BF16)
        mask = sb("mask", [128, 128], BF16)
        ones_f = sb("ones_f", [1, 128])
        hsel = sb("hsel", [128, 2], BF16)
        infp = sb("infp", [128, 1])
        lnb = sb("lnb", [128, 16])
        pcb = sb("pcb", [128, 64])
        ybf = sb("ybf", [128, 2048], BF16)
        pb = [psb(f"pb{i}", [128, 512]) for i in range(7)]
        pT = psb("pT", [128, 1024], BF16)

        p = Prog(nc, es)
        E, DMA = p.emit, p.dma
        _uid = [0]

        def uname(base):
            _uid[0] += 1
            return f"{base}_{_uid[0]}"
        PK = lambda i: ('ps', i)

        def mmg(bank_key, parts):
            n = len(parts)
            for i, (reads, fn) in enumerate(parts):
                E('t', reads, [bank_key] if (i == 0 or i == n - 1) else [],
                  fn, inc=(i == n - 1))

        DMA('y', [], ['ident'], lambda e: e.dma_start(ident[:], identc[:]))
        DMA('y', [], ['mask'], lambda e: e.dma_start(mask[:], maskc[:]))
        DMA('y', [], ['ones_f'], lambda e: e.dma_start(ones_f[:], onesf[:]))
        DMA('y', [], ['hsel'], lambda e: e.dma_start(hsel[:], hselc[:]))
        for c in range(8):
            DMA('y', [], ['pcb'], lambda e, c=c: e.dma_start(
                pcb[:, c:c + 1], qb_a.rearrange("(c p) o -> c p o", p=128)[c]))
            DMA('y', [], ['pcb'], lambda e, c=c: e.dma_start(
                pcb[:, 8 + c:9 + c], kb_a.rearrange("(c p) o -> c p o", p=128)[c]))
        for c in range(32):
            DMA('y', [], ['pcb'], lambda e, c=c: e.dma_start(
                pcb[:, 16 + c:17 + c], b1q_w.rearrange("(c p) o -> c p o", p=128)[c]))
        for c in range(16):
            DMA('y', [], ['pcb'], lambda e, c=c: e.dma_start(
                pcb[:, 48 + c:49 + c], ff1a_b.rearrange("(c p) o -> c p o", p=128)[c]))
        for t in range(NT):
            DMA('y', [], [('x', t)], lambda e, t=t: e.dma_start(
                x[:, t * D:(t + 1) * D], xin.rearrange("(t p) d -> t p d", p=128)[t]))
        E('v', [], ['cons'], lambda e: e.memset(cons[:, 5:6], 1.0))
        E('v', ['cons'], ['cons'], lambda e: e.memset(cons[:, 0:1], 2.0 ** 13))
        for k in range(17, 53):
            tk = float(2.0 ** k)
            E('v', ['cons'], ['cons'], lambda e, tk=tk: e.tensor_scalar(
                cons[:, 6:7], cons[:, 5:6], tk, 1.0, ALU.add, ALU.mult))
            E('v', ['cons'], ['cons'], lambda e, tk=tk: e.tensor_scalar(
                cons[:, 6:7], cons[:, 6:7], tk, 1.0, ALU.subtract, ALU.mult))
            E('v', ['cons'], ['cons'], lambda e: e.tensor_scalar(
                cons[:, 7:8], cons[:, 6:7], 1.0, 1.0, ALU.add, ALU.mult))
            E('v', ['cons'], ['cons'], lambda e: e.tensor_tensor(
                cons[:, 0:1], cons[:, 0:1], cons[:, 7:8], ALU.mult))
        E('v', ['cons'], ['cons'], lambda e: e.tensor_scalar(
            cons[:, 1:2], cons[:, 0:1], 1.0, 1.0, ALU.add, ALU.mult))
        E('v', ['cons'], ['cons'], lambda e: e.tensor_scalar(
            cons[:, 2:3], cons[:, 0:1], float(12.0 * 2.0 ** -9), 1.0, ALU.mult, ALU.mult))
        E('v', ['cons'], ['cons'], lambda e: e.tensor_scalar(
            cons[:, 3:4], cons[:, 0:1], float(2.0 ** -7), 1.0, ALU.mult, ALU.mult))
        E('v', ['cons'], ['cons'], lambda e: e.tensor_scalar(
            cons[:, 3:4], cons[:, 3:4], 1.0, 1.0, ALU.add, ALU.mult))
        E('v', ['cons'], ['cons'], lambda e: e.tensor_scalar(
            cons[:, 4:5], cons[:, 0:1], float(12.0 * 2.0 ** -24), 1.0, ALU.mult, ALU.mult))
        E('v', ['cons'], ['cons'], lambda e: e.tensor_scalar(
            cons[:, 8:9], cons[:, 2:3], -1.0, 1.0, ALU.mult, ALU.mult))
        E('v', [], ['infp'], lambda e: e.memset(infp[:], float('inf')))
        E('v', ['pcb'], ['pcb'], lambda e: e.tensor_scalar(
            pcb[:, 0:8], pcb[:, 0:8], 0.125, 1.0, ALU.mult, ALU.mult))

        def ln_tile(src_key, src_ap, eps=LN_EPS):
            L = lnb
            E('v', [src_key, 'lnb'], ['lnb'], lambda e: e.bn_stats(L[:, 0:6], src_ap[:, 0:512]))
            E('v', [src_key, 'lnb'], ['lnb'], lambda e: e.bn_stats(L[:, 6:12], src_ap[:, 512:1024]))
            ms = [L[:, c:c + 1] for c in (1, 4, 7, 10)]
            cv = [L[:, c:c + 1] for c in (2, 5, 8, 11)]
            ee = lambda f: E('v', ['lnb'], ['lnb'], f)
            ee(lambda e: e.tensor_tensor(L[:, 12:13], ms[0], ms[1], ALU.add))
            ee(lambda e: e.tensor_tensor(L[:, 13:14], ms[2], ms[3], ALU.add))
            ee(lambda e: e.tensor_tensor(L[:, 12:13], L[:, 12:13], L[:, 13:14], ALU.add))
            ee(lambda e: e.tensor_scalar(L[:, 12:13], L[:, 12:13], 0.25, 1.0, ALU.mult, ALU.mult))
            ee(lambda e: e.tensor_tensor(L[:, 14:15], cv[0], cv[1], ALU.add))
            ee(lambda e: e.tensor_tensor(L[:, 15:16], cv[2], cv[3], ALU.add))
            ee(lambda e: e.tensor_tensor(L[:, 14:15], L[:, 14:15], L[:, 15:16], ALU.add))
            ee(lambda e: e.tensor_scalar(L[:, 14:15], L[:, 14:15], float(1 / 1024.0), 1.0, ALU.mult, ALU.mult))
            for m in ms:
                ee(lambda e, m=m: e.tensor_tensor(L[:, 15:16], m, m, ALU.mult))
                ee(lambda e: e.scalar_tensor_tensor(L[:, 14:15], L[:, 15:16], 0.25,
                                                    L[:, 14:15], ALU.mult, ALU.add))
            ee(lambda e: e.tensor_tensor(L[:, 15:16], L[:, 12:13], L[:, 12:13], ALU.mult))
            ee(lambda e: e.tensor_tensor(L[:, 14:15], L[:, 14:15], L[:, 15:16], ALU.subtract))
            ee(lambda e, ep=eps: e.tensor_scalar(L[:, 14:15], L[:, 14:15], float(ep), 1.0, ALU.add, ALU.mult))
            E('s', ['lnb'], ['lnb'], lambda e: e.activation(L[:, 14:15], L[:, 14:15], AF.Sqrt))
            ee(lambda e: e.reciprocal(L[:, 14:15], L[:, 14:15]))
            ee(lambda e: e.tensor_tensor(L[:, 13:14], L[:, 12:13], L[:, 14:15], ALU.mult))
            ee(lambda e: e.tensor_scalar(L[:, 13:14], L[:, 13:14], -1.0, 1.0, ALU.mult, ALU.mult))
            return L[:, 14:15], L[:, 13:14]

        def tile_to_xfm(src_key, src_bf_ap, t):
            for dcI in range(ND):
                E('t', [src_key, 'ident'],
                  [('pT',)] if dcI in (0, ND - 1) else [],
                  lambda e, dcI=dcI: e.transpose(
                      pT[:, dcI * 128:(dcI + 1) * 128],
                      src_bf_ap[:, dcI * 128:(dcI + 1) * 128], ident[:]),
                  inc=(dcI == ND - 1))
            dst = bass.AP(xfm, t * 128, [[ND * S, 128], [S, ND], [1, 128]])
            srcp = bass.AP(pT, 0, [[1024, 128], [128, ND], [1, 128]])
            E('v', [('pT',)], [('xfm', t)], lambda e: e.tensor_copy(dst, srcp))

        def ln_to_xfm(src_of, f32_dst_of=None):
            for t in range(NT):
                sk, sap = src_of(t)
                rstd, nmu = ln_tile(sk, sap)
                yb = ybf[:, (t % 2) * 1024:(t % 2 + 1) * 1024]
                ybk = ('ybf', t % 2)
                if f32_dst_of is not None:
                    dk, dap = f32_dst_of(t)
                    E('v', [sk, 'lnb'], [dk], lambda e, sap=sap, dap=dap, r=rstd, n=nmu:
                      e.tensor_scalar(dap, sap, r, n, ALU.mult, ALU.add))
                    E('v', [dk], [ybk], lambda e, dap=dap, yb=yb: e.tensor_copy(yb, dap))
                else:
                    E('v', [sk, 'lnb'], [ybk], lambda e, sap=sap, yb=yb, r=rstd, n=nmu:
                      e.tensor_scalar(yb, sap, r, n, ALU.mult, ALU.add))
                tile_to_xfm(ybk, yb, t)
            p.barrier()

        def emit_quant(src_key, src_ap, dst_key, dst_ap, sc, sck, m1, qp,
                       signed=False, n=512, tt_engs=('g', 'g', 'g')):
            T = sck
            srcs = (src_key if isinstance(src_key, list) else [src_key])
            npart = src_ap.shape[0]
            c1 = cons[0:npart, qp['c1c']:qp['c1c'] + 1]
            K = cons[0:npart, qp['kc']:qp['kc'] + 1]
            hpath = (not signed) and qp is QE4
            if signed:
                E('s', srcs + [T], [T], lambda e: e.activation(sc['f'], src_ap, AF.Abs))
                mag = sc['f']
            else:
                mag = src_ap
            E('v', srcs + [T], [T], lambda e: e.tensor_scalar(m1, mag, float(qp['thr']), None, ALU.is_ge))
            E('v', srcs + [T], [T], lambda e: e.tensor_scalar(
                sc['f'], mag, float(qp['ovf']), None, ALU.is_ge))
            E('v', [T], [T], lambda e: e.tensor_scalar(
                sc['f'], sc['f'], 3.4e38, 1.0, ALU.mult, ALU.add))
            e1, e2, e3 = tt_engs
            if hpath:
                negK = cons[0:npart, 8:9]
                E('s', srcs + ['cons', T], [T], lambda e: e.activation(
                    sc['y'], src_ap, AF.Relu, bias=0.0, scale=c1))
                E(e1, srcs + [T], [T], lambda e: e.tensor_tensor(sc['t'], sc['y'], src_ap, ALU.subtract))
                E(e2, [T], [T], lambda e: e.tensor_tensor(sc['y'], sc['y'], sc['t'], ALU.subtract))
                E(e3, [T], [T], lambda e: e.tensor_tensor(sc['y'], sc['y'], sc['f'], ALU.mult))
                E('s', srcs + ['cons', T], [T], lambda e: e.activation(
                    sc['u'], src_ap, AF.Relu, bias=K, scale=1.0))
                E('s', ['cons', T], [dst_key, T], lambda e: e.activation(
                    dst_ap, sc['u'], AF.Relu, bias=negK, scale=1.0))
            else:
                E('v', srcs + ['cons', T], [T], lambda e: e.tensor_scalar(
                    sc['y'], src_ap, c1, 1.0, ALU.mult, ALU.mult))
                E(e1, srcs + [T], [T], lambda e: e.tensor_tensor(sc['t'], sc['y'], src_ap, ALU.subtract))
                E(e2, [T], [T], lambda e: e.tensor_tensor(sc['y'], sc['y'], sc['t'], ALU.subtract))
                E(e3, [T], [T], lambda e: e.tensor_tensor(sc['y'], sc['y'], sc['f'], ALU.mult))
                E('v', srcs + ['cons', T], [T], lambda e: e.tensor_scalar(
                    sc['u'], src_ap, K, 1.0, ALU.add, ALU.mult))
                E('v', ['cons', T], [dst_key, T], lambda e: e.tensor_scalar(
                    dst_ap, sc['u'], K, 1.0, ALU.subtract, ALU.mult))
            E('v', [dst_key, T], [dst_key], lambda e: e.copy_predicated(
                dst_ap, m1, sc['y']))

        def phase_attn_a():
            with ExitStack() as ps_:
                A = ps_.enter_context(nc.sbuf_tensor(uname("A"), [128, NT * D], F32))
                ps_in = ps_.enter_context(ExitStack())
                aB = ps_in.enter_context(nc.sbuf_tensor(uname("aB"), [128, 10240], BF16))
                vrow = ps_in.enter_context(nc.sbuf_tensor(uname("vrow"), [1, D], F32))
                DMA('y', [], ['vrow'], lambda e: e.dma_start(vrow[:], vb_a[:]))
                ln_to_xfm(lambda t: (('x', t), x[:, t * D:(t + 1) * D]),
                          f32_dst_of=lambda t: (('A', t), A[:, t * D:(t + 1) * D]))
                Qp, Kp = aB[:, 0:2048], aB[:, 2048:4096]
                PTr = [aB[:, 4096 + i * 256:4096 + (i + 1) * 256] for i in range(4)]
                wq_t, wk_t = aB[:, 5120:6144], aB[:, 6144:7168]
                VB = 7168
                for pr in range(8):
                    for wdram, wtile, wkn, soff in ((wq_a, wq_t, 'wqp', 0),
                                                    (wk_a, wk_t, 'wkp', 0)):
                        src = bass.AP(wdram, pr * 128 * D, [[D, 128], [1, D]])
                        DMA('y', [], [('stg', soff)], lambda e, src=src, soff=soff:
                            e.dma_start(stg[:, soff:soff + 1024], src))
                        E('v', [('stg', soff)], [(wkn, pr)],
                          lambda e, soff=soff, wtile=wtile: e.tensor_copy(
                              wtile, stg[:, soff:soff + 1024]))
                    for which, (wkn, wtile, dst, boff, scale) in enumerate(
                            (('wqp', wq_t, Qp, 0, 0.125), ('wkp', wk_t, Kp, 8, 1.0))):
                        for tg in range(4):
                            mmg(PK(which), [
                                ([(wkn, pr)],
                                 (lambda e, wtile=wtile, dc=dc, tg=tg, w=which: e.matmul(
                                     pb[w][:], wtile[:, dc * 128:(dc + 1) * 128],
                                     xfm[:, dc * S + tg * 512:dc * S + (tg + 1) * 512],
                                     start=(dc == 0), stop=(dc == ND - 1))))
                                for dc in range(ND)])
                            E('v', [PK(which), 'pcb'], [('qk', pr, which)],
                              lambda e, dst=dst, tg=tg, b=boff, sc_=scale, w=which:
                              e.tensor_scalar(dst[:, tg * 512:(tg + 1) * 512], pb[w][:],
                                              sc_, pcb[:, b + pr:b + pr + 1], ALU.mult, ALU.add))
                    src = bass.AP(wv_a, pr * 128 * D, [[D, 128], [1, D]])
                    DMA('y', [], [('stg', 0)], lambda e, src=src: e.dma_start(
                        stg[:, 0:1024], src))
                    E('v', [('stg', 0), ('wqp', pr)], [('wvp', pr)],
                      lambda e: e.tensor_copy(wq_t, stg[:, 0:1024]))
                    for t in range(NT):
                        mmg(PK(2),
                            [([('wvp', pr)],
                              (lambda e, dc=dc, t=t: e.matmul(
                                  pb[2][:, 0:128],
                                  xfm[:, dc * S + t * 128:dc * S + (t + 1) * 128],
                                  wq_t[:, dc * 128:(dc + 1) * 128],
                                  start=(dc == 0), stop=False)))
                             for dc in range(ND)] +
                            [(['ones_f', 'vrow'],
                              lambda e: e.matmul(pb[2][:, 0:128], ones_f[:],
                                                 vrow[0:1, pr * 128:(pr + 1) * 128],
                                                 start=False, stop=True))])
                        E('v', [PK(2)], [('vau', pr, t)], lambda e, t=t: e.tensor_copy(
                            bass.AP(aB, VB + t * 130, [[10240, 128], [65, 2], [1, 64]]),
                            bass.AP(pb[2], 0, [[512, 128], [64, 2], [1, 64]])))
                        E('v', [], [('vau', pr, t)], lambda e, t=t: e.memset(
                            bass.AP(aB, VB + t * 130 + 64, [[10240, 128], [65, 2], [1, 1]]), 1.0))
                    for kw in range(NT):
                        qn = 256 if kw < NT - 1 else 128
                        kwp = kw % 2
                        pA, pB_ = pb[3 + kwp], pb[5 + kwp]
                        E('t', [('qk', pr, 0), ('qk', pr, 1)], [PK(3 + kwp)],
                          lambda e, kw=kw, qn=qn, pA=pA: e.matmul(
                              pA[:, 0:qn], Kp[0:64, kw * 128:(kw + 1) * 128],
                              Qp[0:64, kw * 128:kw * 128 + qn], start=True, stop=True,
                              tile_position=(0, 0)))
                        E('t', [], [PK(5 + kwp)],
                          lambda e, kw=kw, qn=qn, pB_=pB_: e.matmul(
                              pB_[:, 0:qn], Kp[64:128, kw * 128:(kw + 1) * 128],
                              Qp[64:128, kw * 128:kw * 128 + qn], start=True, stop=True,
                              tile_position=(64, 0)))
                        for hh, pp, bk in ((0, pA, PK(3 + kwp)), (1, pB_, PK(5 + kwp))):
                            rt = PTr[kwp * 2 + hh]
                            E('s', [bk], [('pt', hh, kwp)], lambda e, rt=rt, qn=qn, pp=pp:
                              e.activation(rt[:, 0:qn], pp[:, 0:qn], AF.Exp))
                            E('v', [('pt', hh, kwp), 'mask'], [('pt', hh, kwp)],
                              lambda e, rt=rt: e.tensor_tensor(
                                  rt[:, 0:128], rt[:, 0:128], mask[:], ALU.mult))
                        pav = pb[kwp]
                        parts = []
                        for hh in range(2):
                            rprev = PTr[((kw - 1) % 2) * 2 + hh]
                            rown = PTr[kwp * 2 + hh]
                            vsl = bass.AP(aB, VB + kw * 130 + hh * 65, [[10240, 128], [1, 65]])
                            vslp = bass.AP(aB, VB + (kw - 1) * 130 + hh * 65, [[10240, 128], [1, 65]])
                            if kw > 0:
                                parts.append(([('pt', hh, (kw - 1) % 2), ('vau', pr, kw - 1)],
                                              (lambda e, rprev=rprev, vslp=vslp, hh=hh:
                                               e.matmul(pav[:, hh * 65:(hh + 1) * 65],
                                                        rprev[:, 128:256], vslp,
                                                        start=True, stop=False))))
                            parts.append(([('pt', hh, kwp), ('vau', pr, kw)],
                                          (lambda e, rown=rown, vsl=vsl, hh=hh, kw=kw:
                                           e.matmul(pav[:, hh * 65:(hh + 1) * 65],
                                                    rown[:, 0:128], vsl,
                                                    start=(kw == 0), stop=True))))
                        mmg(PK(kwp), parts)
                        for hh in range(2):
                            hcol = (2 * pr + hh) * 64
                            E('v', [PK(kwp), 'lnb'], ['lnb'], lambda e, hh=hh, pav=pav:
                              e.reciprocal(lnb[:, hh:hh + 1], pav[:, hh * 65 + 64:hh * 65 + 65]))
                            E('v', [PK(kwp), 'lnb', ('A', kw)], [('A', kw)],
                              lambda e, hh=hh, kw=kw, hcol=hcol, pav=pav: e.scalar_tensor_tensor(
                                  A[:, kw * D + hcol:kw * D + hcol + 64],
                                  pav[:, hh * 65:hh * 65 + 64], lnb[:, hh:hh + 1],
                                  A[:, kw * D + hcol:kw * D + hcol + 64], ALU.mult, ALU.add))
                p.barrier()
                if AA_STOP[0] == 1:
                    for t in range(NT):
                        E('v', [('A', t)], [('x', t)], lambda e, t=t: e.tensor_copy(
                            x[:, t * D:(t + 1) * D], A[:, t * D:(t + 1) * D]))
                    p.barrier()
                    return
                ps_in.close()
                with ExitStack() as ps2:
                    def inner_ln(gbt, add_to_x):
                        for t in range(NT):
                            Aap = A[:, t * D:(t + 1) * D]
                            rstd, nmu = ln_tile(('A', t), Aap)
                            E('v', [('A', t), 'lnb'], [('A', t)], lambda e, Aap=Aap, r=rstd, n=nmu:
                              e.tensor_scalar(Aap, Aap, r, n, ALU.mult, ALU.add))
                            E('v', [('A', t), 'gbt'], [('A', t)], lambda e, Aap=Aap:
                              e.tensor_tensor(Aap, Aap, gbt[:, 0:D], ALU.mult))
                            E('v', [('A', t), 'gbt'], [('A', t)], lambda e, Aap=Aap:
                              e.tensor_tensor(Aap, Aap, gbt[:, D:2 * D], ALU.add))
                            if add_to_x:
                                E('v', [('A', t), ('x', t)], [('x', t)], lambda e, t=t, Aap=Aap:
                                  e.tensor_tensor(x[:, t * D:(t + 1) * D],
                                                  x[:, t * D:(t + 1) * D], Aap, ALU.add))
                            else:
                                yb = ybf[:, (t % 2) * 1024:(t % 2 + 1) * 1024]
                                E('v', [('A', t)], [('ybf', t % 2)], lambda e, Aap=Aap, yb=yb:
                                  e.tensor_copy(yb, Aap))
                                tile_to_xfm(('ybf', t % 2), yb, t)
                    with nc.sbuf_tensor(uname("gbt1"), [128, 2 * D], F32) as gbt:
                        DMA('y', [], ['gbt'], lambda e: e.dma_start(gbt[:], lngb[:, 0:2 * D]))
                        inner_ln(gbt, add_to_x=False)
                        p.barrier()
                    if AA_STOP[0] == 2:
                        for t in range(NT):
                            E('v', [('A', t)], [('x', t)], lambda e, t=t: e.tensor_copy(
                                x[:, t * D:(t + 1) * D], A[:, t * D:(t + 1) * D]))
                        p.barrier()
                        return
                    emit_ffn(A, lambda t: ('A', t), hid=FFA, w1d=ff1a_w, b1off=48,
                             w2d=ff2a_w, brow_dram=ff2a_b, quant=False, tokgrp=512)
                    p.barrier()
                    if AA_STOP[0] == 3:
                        for t in range(NT):
                            E('v', [('A', t)], [('x', t)], lambda e, t=t: e.tensor_copy(
                                x[:, t * D:(t + 1) * D], A[:, t * D:(t + 1) * D]))
                        p.barrier()
                        return
                    with nc.sbuf_tensor(uname("gbt2"), [128, 2 * D], F32) as gbt:
                        DMA('y', [], ['gbt'], lambda e: e.dma_start(gbt[:], lngb[:, 2 * D:4 * D]))
                        inner_ln(gbt, add_to_x=True)
                        p.barrier()

        wqc_state = {'done': False}

        def emit_ffn(rbuf, rkey, hid, w1d, b1off, w2d, brow_dram, quant,
                     tokgrp=1024):
            nj = hid // 128
            ngj = tokgrp // 512
            with ExitStack() as ps_:
                fw = ybf
                fstg = ps_.enter_context(nc.sbuf_tensor(uname("fstg"), [128, 2048], F32))
                brow = ps_.enter_context(nc.sbuf_tensor(uname("fbrow"), [1, D], F32))
                if quant:
                    fh = ps_.enter_context(nc.sbuf_tensor(uname("fh"), [128, 2 * tokgrp], F32))
                    fsc = ps_.enter_context(nc.sbuf_tensor(uname("fsc"), [128, 6 * tokgrp], F32))
                    fm8 = ps_.enter_context(nc.sbuf_tensor(uname("fm8"), [128, 2 * tokgrp], U8))
                    hq8 = ps_.enter_context(
                        nc.sbuf_tensor(uname("fq8"), [128, nj * tokgrp + 1536], FP8))
                    SC = [dict(y=fsc[:, p * 3 * tokgrp + 0 * tokgrp:p * 3 * tokgrp + 1 * tokgrp],
                               t=fsc[:, p * 3 * tokgrp + 1 * tokgrp:p * 3 * tokgrp + 2 * tokgrp],
                               u=fsc[:, p * 3 * tokgrp + 1 * tokgrp:p * 3 * tokgrp + 2 * tokgrp],
                               f=fsc[:, p * 3 * tokgrp + 2 * tokgrp:p * 3 * tokgrp + 3 * tokgrp])
                          for p in range(2)]
                    M1 = [fm8[:, p * tokgrp:(p + 1) * tokgrp] for p in range(2)]
                else:
                    hB = ps_.enter_context(
                        nc.sbuf_tensor(uname("fhB"), [128, nj * tokgrp], BF16))
                DMA('y', [], ['brow'], lambda e: e.dma_start(brow[:], brow_dram[:]))
                if quant:
                    scW = dict(y=fsc[0:1, 0:512], t=fsc[0:1, 512:1024],
                               u=fsc[0:1, 512:1024], f=fsc[0:1, 1024:1536])
                    for hf in range(2):
                        emit_quant('brow', brow[0:1, hf * 512:(hf + 1) * 512], 'brow',
                                   brow[0:1, hf * 512:(hf + 1) * 512],
                                   scW, ('fscr', 0), fm8[0:1, 0:512],
                                   QE5, signed=True, n=512)
                building = quant and not wqc_state['done']
                emit_wqc_chain = None
                wq_dma = None
                if building:
                    wstg1 = ps_.enter_context(
                        nc.sbuf_tensor(uname("wstg1"), [128, 1024], F32))
                    wslot = [stg[:, 0:1024], wstg1[:, 0:1024]]
                    wsc3 = ps_.enter_context(
                        nc.sbuf_tensor(uname("wsc3"), [128, 3072], F32))
                    wm1k = ps_.enter_context(
                        nc.sbuf_tensor(uname("wm1k"), [128, 1024], U8))
                    wq82 = ps_.enter_context(
                        nc.sbuf_tensor(uname("wq82"), [128, 1024], FP8))
                    scQ = [dict(y=wsc3[:, p * 1536 + 0:p * 1536 + 512],
                                t=wsc3[:, p * 1536 + 512:p * 1536 + 1024],
                                u=wsc3[:, p * 1536 + 512:p * 1536 + 1024],
                                f=wsc3[:, p * 1536 + 1024:p * 1536 + 1536])
                           for p in range(2)]

                    def wq_dma(j):
                        jw = j % 2
                        DMA('y' if jw == 0 else 'g', [], [('wstg', jw)],
                            lambda e, j=j, jw=jw: e.dma_start(
                                wslot[jw],
                                w2q_w.rearrange("(c p) d -> c p d", p=128)[j]))

                    def emit_wqc_chain(j):
                        jw = j % 2
                        te = ('g', 'g', 'g') if jw == 0 else ('v', 'v', 'v')
                        for hf in range(2):
                            emit_quant(('wstg', jw), wslot[jw][:, hf * 512:(hf + 1) * 512],
                                       ('wq8w',),
                                       wq82[:, hf * 512:(hf + 1) * 512],
                                       scQ[jw], ('wscr', jw), wm1k[:, jw * 512:(jw + 1) * 512],
                                       QE4, signed=True, tt_engs=te)
                        DMA('g' if jw == 0 else 'y', [('wq8w',)], [('wqc', j)],
                            lambda e, j=j: e.dma_start(
                                wqc.rearrange("(c p) d -> c p d", p=128)[j],
                                wq82[:]))
                    wqc_state['done'] = True
                for half in range(S // tokgrp):
                    hb = half * tokgrp

                    def w1_dma(j):
                        jb = j % 2
                        aw = [('fws', jb)] + ([('fwsb', 0), ('fwsb', 1)] if jb == 0 else [('fwsb', 2)])
                        DMA('y' if jb == 0 else 'g', [], aw,
                            lambda e, j=j, jb=jb: e.dma_start(
                                fstg[:, jb * 1024:(jb + 1) * 1024],
                                bass.AP(w1d, j * 128 * D, [[D, 128], [1, D]])))
                    w1_dma(0)
                    w1_dma(1)
                    if building and half == 0:
                        wq_dma(0)
                    for j in range(nj):
                        jb = j % 2
                        wsl = fw[:, jb * 1024:(jb + 1) * 1024]
                        cw = [('fw', jb)] + ([('fwb', 0), ('fwb', 1)] if jb == 0 else [('fwb', 2)])
                        E('v', [('fws', jb)], cw,
                          lambda e, jb=jb, wsl=wsl: e.tensor_copy(
                              wsl, fstg[:, jb * 1024:(jb + 1) * 1024]))
                        if j + 2 < nj:
                            w1_dma(j + 2)
                        if building and half == 0 and j + 1 < nj:
                            wq_dma(j + 1)
                        for tg in range(ngj):
                            pki = ngj * jb + tg
                            mmg(PK(pki), [
                                ([('fw', jb)],
                                 (lambda e, wsl=wsl, dc=dc, tg=tg, pki=pki: e.matmul(
                                     pb[pki][:], wsl[:, dc * 128:(dc + 1) * 128],
                                     xfm[:, dc * S + hb + tg * 512:dc * S + hb + (tg + 1) * 512],
                                     start=(dc == 0), stop=(dc == ND - 1))))
                                for dc in range(ND)])
                            if quant:
                                hsl = fh[:, jb * tokgrp + tg * 512:jb * tokgrp + (tg + 1) * 512]
                                E('s', [PK(pki), 'pcb', ('fh', jb, tg)], [('fh', jb, tg)],
                                  lambda e, pki=pki, j=j, hsl=hsl: e.activation(
                                      hsl, pb[pki][:], AF.Relu,
                                      bias=pcb[:, b1off + j:b1off + j + 1], scale=1.0))
                            else:
                                E('s', [PK(pki), 'pcb', ('h8', j, tg)], [('h8', j, tg)],
                                  lambda e, pki=pki, j=j, tg=tg: e.activation(
                                      hB[:, j * tokgrp + tg * 512:j * tokgrp + (tg + 1) * 512],
                                      pb[pki][:], AF.Relu,
                                      bias=pcb[:, b1off + j:b1off + j + 1], scale=1.0))
                        if quant:
                            emit_quant([('fh', jb, tg2) for tg2 in range(ngj)],
                                       fh[:, jb * tokgrp:(jb + 1) * tokgrp],
                                       ('h8', j, 'all'),
                                       hq8[:, j * tokgrp:(j + 1) * tokgrp],
                                       SC[jb], ('fscr', jb), M1[jb], QE4, n=tokgrp,
                                       tt_engs=('g', 'g', 'v'))
                        if emit_wqc_chain is not None and half == 0:
                            emit_wqc_chain(j)
                    for dc in range(2):
                        for sub in range(ngj):
                            first = [True] * 4
                            for j in range(nj):
                                jb = j % 3
                                if quant:
                                    rt = hq8[:, nj * tokgrp + jb * 512:nj * tokgrp + (jb + 1) * 512]
                                    DMA('y' if jb % 2 == 0 else 'g',
                                        [('fq', jb), ('wqc', j)], [('fq', jb)],
                                        lambda e, j=j, dc=dc, rt=rt: e.dma_start(
                                            rt, bass.AP(wqc, j * 128 * D + dc * 512,
                                                        [[D, 128], [1, 512]])))
                                    rk = ('fq', jb)
                                else:
                                    DMA('y', [('fwb', jb)], [('fwsb', jb), ('fws', jb // 2)],
                                        lambda e, j=j, dc=dc, jb=jb: e.dma_start(
                                            fstg[:, jb * 512:(jb + 1) * 512],
                                            bass.AP(w2d, j * 128 * D + dc * 512,
                                                    [[D, 128], [1, 512]])))
                                    rt = fw[:, jb * 512:(jb + 1) * 512]
                                    E('v', [('fwsb', jb)], [('fwb', jb), ('fw', jb // 2)],
                                      lambda e, jb=jb, rt=rt: e.tensor_copy(
                                          rt, fstg[:, jb * 512:(jb + 1) * 512]))
                                    rk = ('fwb', jb)
                                for tt in range(4):
                                    pki = 3 + tt
                                    lap = (hq8 if quant else hB)[:, j * tokgrp + sub * 512:
                                                                 j * tokgrp + sub * 512 + 512]
                                    hk = ('h8', j, 'all') if quant else ('h8', j, sub)
                                    E('t', [hk, rk],
                                      [PK(pki)] if first[tt] else [],
                                      lambda e, lap=lap, rt=rt, tt=tt, j=j, pki=pki: e.matmul(
                                          pb[pki][:], lap[:, tt * 128:(tt + 1) * 128], rt,
                                          start=(j == 0), stop=False), inc=(tt == 3))
                                    first[tt] = False
                            for tt in range(4):
                                pki = 3 + tt
                                E('t', ['brow', 'ones_f'], [PK(pki)],
                                  lambda e, dc=dc, pki=pki: e.matmul(
                                      pb[pki][:], ones_f[:], brow[0:1, dc * 512:(dc + 1) * 512],
                                      start=False, stop=True))
                            for tt in range(4):
                                pki = 3 + tt
                                t_tile = (hb + sub * 512 + tt * 128) // 128
                                rap = rbuf[:, t_tile * D + dc * 512:t_tile * D + (dc + 1) * 512]
                                if quant:
                                    par = tt % 2
                                    osl = fh[:, par * tokgrp:par * tokgrp + 512]
                                    ok = ('fh', par, 0)
                                    E('v', [PK(pki)], [ok, ('fh', par, 1)],
                                      lambda e, pki=pki, osl=osl: e.tensor_copy(osl, pb[pki][:]))
                                    emit_quant(ok, osl, ok, osl,
                                               {k: v[:, 0:512] for k, v in SC[par].items()},
                                               ('fscr', par), M1[par][:, 0:512], QE5, signed=True)
                                    E('v', [ok, rkey(t_tile)], [rkey(t_tile)],
                                      lambda e, osl=osl, rap=rap: e.scalar_tensor_tensor(
                                          rap, osl, 1.0, rap, ALU.mult, ALU.add))
                                else:
                                    E('v', [PK(pki), rkey(t_tile)], [rkey(t_tile)],
                                      lambda e, pki=pki, rap=rap: e.scalar_tensor_tensor(
                                          rap, pb[pki][:], 1.0, rap, ALU.mult, ALU.add))
                p.barrier()

        def phase_ffnq():
            ln_to_xfm(lambda t: (('x', t), x[:, t * D:(t + 1) * D]))
            emit_ffn(x, lambda t: ('x', t), hid=FFQ, w1d=w1q_w, b1off=16,
                     w2d=w2q_w, brow_dram=b2q_w, quant=True)

        def phase_attn_b():
            with ExitStack() as ps_:
                Kf = ps_.enter_context(nc.sbuf_tensor(uname("Kf"), [128, 16384], BF16))
                Va = ps_.enter_context(nc.sbuf_tensor(uname("Va"), [128, NT * 1040], BF16))
                Qs = ps_.enter_context(nc.sbuf_tensor(uname("Qs"), [128, 4096], BF16))
                af = ps_.enter_context(nc.sbuf_tensor(uname("af"), [128, 4096], BF16))
                ptb = ps_.enter_context(nc.sbuf_tensor(uname("ptb"), [128, 2048], BF16))
                wcs = ps_.enter_context(nc.sbuf_tensor(uname("wcs"), [128, 1024], BF16))
                sm2 = ps_.enter_context(nc.sbuf_tensor(uname("sm2"), [1, 3072], F32))
                rbc = ps_.enter_context(nc.sbuf_tensor(uname("rbc"), [64, 512], F32))
                kmx = ps_.enter_context(nc.sbuf_tensor(uname("kmx"), [1, 24], F32))
                borow = sm2[0:1, 2048:3072]
                ln_to_xfm(lambda t: (('x', t), x[:, t * D:(t + 1) * D]))
                DMA('y', ['sm2'], ['sm2'], lambda e: e.dma_start(borow[:], bo_b[:]))
                for pr in range(8):
                    src = bass.AP(wk_b, pr * 128 * D, [[D, 128], [1, D]])
                    DMA('y', [], [('stg', 0)], lambda e, src=src: e.dma_start(
                        stg[:, 0:1024], src))
                    E('v', [('stg', 0)], [('wcs',)], lambda e: e.tensor_copy(
                        wcs[:], stg[:, 0:1024]))
                    for tg in range(4):
                        mmg(PK(0), [
                            ([('wcs',)],
                             (lambda e, dc=dc, tg=tg: e.matmul(
                                 pb[0][:], wcs[:, dc * 128:(dc + 1) * 128],
                                 xfm[:, dc * S + tg * 512:dc * S + (tg + 1) * 512],
                                 start=(dc == 0), stop=(dc == ND - 1))))
                            for dc in range(ND)])
                        E('v', [PK(0)], [('kf', pr)], lambda e, pr=pr, tg=tg: e.tensor_copy(
                            Kf[:, pr * 2048 + tg * 512:pr * 2048 + (tg + 1) * 512], pb[0][:]))
                    E('s', [('kf', pr)], [('ksq',)], lambda e, pr=pr: e.activation(
                        ptb[:, 0:2048], Kf[:, pr * 2048:(pr + 1) * 2048], AF.Square))
                    for hh in range(2):
                        for tg in range(4):
                            E('t', [('ksq',), 'hsel'], [PK(1)], lambda e, tg=tg, hh=hh: e.matmul(
                                pb[1][0:1, 0:512], hsel[:, hh:hh + 1],
                                ptb[:, tg * 512:(tg + 1) * 512], start=True, stop=True))
                            E('v', [PK(1), 'kmx'], ['kmx'], lambda e: e.tensor_reduce(
                                kmx[0:1, 16:17], pb[1][0:1, 0:512], mybir.AxisListType.X, ALU.max))
                            col = hh * 8 + pr
                            if tg == 0:
                                E('v', ['kmx'], ['kmx'], lambda e, col=col: e.tensor_copy(
                                    kmx[0:1, col:col + 1], kmx[0:1, 16:17]))
                            else:
                                E('v', ['kmx'], ['kmx'], lambda e, col=col: e.tensor_tensor(
                                    kmx[0:1, col:col + 1], kmx[0:1, col:col + 1], kmx[0:1, 16:17], ALU.max))
                    src = bass.AP(wv_b, pr * 128 * D, [[D, 128], [1, D]])
                    DMA('y', [], [('stg', 0)], lambda e, src=src: e.dma_start(
                        stg[:, 0:1024], src))
                    E('v', [('stg', 0)], [('wcs',)], lambda e: e.tensor_copy(
                        wcs[:], stg[:, 0:1024]))
                    for t in range(NT):
                        mmg(PK(2), [
                            ([('wcs',)],
                             (lambda e, dc=dc, t=t: e.matmul(
                                 pb[2][:, 0:128],
                                 xfm[:, dc * S + t * 128:dc * S + (t + 1) * 128],
                                 wcs[:, dc * 128:(dc + 1) * 128],
                                 start=(dc == 0), stop=(dc == ND - 1))))
                            for dc in range(ND)])
                        E('v', [PK(2)], [('vab', pr, t)], lambda e, t=t, pr=pr: e.tensor_copy(
                            bass.AP(Va, t * 1040 + pr * 130, [[NT * 1040, 128], [65, 2], [1, 64]]),
                            bass.AP(pb[2], 0, [[512, 128], [64, 2], [1, 64]])))
                        E('v', [], [('vab', pr, t)], lambda e, t=t, pr=pr: e.memset(
                            bass.AP(Va, t * 1040 + pr * 130 + 64, [[NT * 1040, 128], [65, 2], [1, 1]]), 1.0))
                E('s', ['kmx'], ['kmx'], lambda e: e.activation(kmx[0:1, 0:16], kmx[0:1, 0:16], AF.Sqrt))
                p.barrier()
                for qg in (range(4) if AB_QGS[0] is None else AB_QGS[0]):
                    qb_ = qg * 512
                    for pr in range(8):
                        src = bass.AP(wq_b, pr * 128 * D, [[D, 128], [1, D]])
                        DMA('y', [], [('stg', 0)], lambda e, src=src: e.dma_start(
                            stg[:, 0:1024], src))
                        E('v', [('stg', 0)], [('wcs',)], lambda e: e.tensor_copy(
                            wcs[:], stg[:, 0:1024]))
                        mmg(PK(0), [
                            ([('wcs',)],
                             (lambda e, dc=dc, qb_=qb_: e.matmul(
                                 pb[0][:], wcs[:, dc * 128:(dc + 1) * 128],
                                 xfm[:, dc * S + qb_:dc * S + qb_ + 512],
                                 start=(dc == 0), stop=(dc == ND - 1))))
                            for dc in range(ND)])
                        E('v', [PK(0)], [('qs', pr)], lambda e, pr=pr: e.tensor_scalar(
                            Qs[:, pr * 512:(pr + 1) * 512], pb[0][:], 0.125, 1.0,
                            ALU.mult, ALU.mult))
                    for pr in (range(8) if AB_PRS[0] is None else AB_PRS[0]):
                        E('s', [('qs', pr)], [('wcs',)], lambda e, pr=pr: e.activation(
                            wcs[:, 0:512], Qs[:, pr * 512:(pr + 1) * 512], AF.Square))
                        for hh in range(2):
                            csl = sm2[0:1, hh * 512:(hh + 1) * 512]
                            E('t', [('wcs',), 'hsel'], [PK(6)], lambda e, hh=hh: e.matmul(
                                pb[6][0:1, 0:512], hsel[:, hh:hh + 1], wcs[:, 0:512],
                                start=True, stop=True))
                            E('s', [PK(6), 'sm2'], ['sm2'], lambda e, csl=csl: e.activation(
                                csl, pb[6][0:1, 0:512], AF.Sqrt))
                            col = hh * 8 + pr
                            E('v', ['sm2', 'kmx'], ['sm2'], lambda e, csl=csl, col=col:
                              e.tensor_scalar(csl, csl, kmx[0:1, col:col + 1], 1.0625,
                                              ALU.mult, ALU.mult))
                            E('s', ['sm2'], ['sm2'], lambda e, csl=csl: e.activation(
                                csl, csl, AF.Exp))
                        def emit_st(kt):
                            kp = kt % 2
                            pS0, pS1 = pb[3 + kp], pb[5 + kp]
                            E('t', [('kf', pr), ('qs', pr)], [PK(3 + kp)],
                              lambda e, kt=kt, pS0=pS0: e.matmul(
                                  pS0[:], Kf[0:64, pr * 2048 + kt * 128:pr * 2048 + (kt + 1) * 128],
                                  Qs[0:64, pr * 512:(pr + 1) * 512], start=True, stop=True,
                                  tile_position=(0, 0)))
                            E('t', [], [PK(5 + kp)],
                              lambda e, kt=kt, pS1=pS1: e.matmul(
                                  pS1[:], Kf[64:128, pr * 2048 + kt * 128:pr * 2048 + (kt + 1) * 128],
                                  Qs[64:128, pr * 512:(pr + 1) * 512], start=True, stop=True,
                                  tile_position=(64, 0)))
                            E('s', [PK(3 + kp)], [('ptb', 0, kp)],
                              lambda e, kp=kp, pS0=pS0: e.activation(
                                  ptb[:, kp * 512:(kp + 1) * 512], pS0[:], AF.Exp))
                            E('s', [PK(5 + kp)], [('ptb', 1, kp)],
                              lambda e, kp=kp, pS1=pS1: e.activation(
                                  ptb[:, (2 + kp) * 512:(3 + kp) * 512], pS1[:], AF.Exp))

                        def emit_av(kt):
                            kp = kt % 2
                            for hh in range(2):
                                psl = ptb[:, (2 * hh + kp) * 512:(2 * hh + kp + 1) * 512]
                                vsl = bass.AP(Va, kt * 1040 + pr * 130 + hh * 65,
                                              [[NT * 1040, 128], [1, 65]])
                                E('t', [('ptb', hh, kp), ('vab', pr, kt)],
                                  [PK(hh)] if kt in (0, NT - 1) else [],
                                  lambda e, psl=psl, vsl=vsl, hh=hh, kt=kt:
                                  e.matmul(pb[hh][0:65, :], vsl, psl,
                                           start=(kt == 0), stop=(kt == NT - 1)),
                                  inc=(kt == NT - 1))
                        emit_st(0)
                        for kt in range(NT):
                            if kt + 1 < NT:
                                emit_st(kt + 1)
                            emit_av(kt)
                        for hh in range(2):
                            dsl = sm2[0:1, 1024 + hh * 512:1536 + hh * 512]
                            E('v', [PK(hh), 'sm2'], ['sm2'], lambda e, hh=hh, dsl=dsl:
                              e.scalar_tensor_tensor(dsl,
                                                     sm2[0:1, hh * 512:(hh + 1) * 512],
                                                     ATT_EPS, pb[hh][64:65, :], ALU.mult, ALU.add))
                            E('v', ['sm2'], ['sm2'], lambda e, dsl=dsl: e.reciprocal(dsl, dsl))
                            E('t', ['sm2', 'ones_f'], [PK(6)], lambda e, dsl=dsl: e.matmul(
                                pb[6][0:64, :], ones_f[0:1, 0:64], dsl,
                                start=True, stop=True))
                            E('v', [PK(6)], [('rbc',)], lambda e: e.tensor_copy(
                                rbc[:], pb[6][0:64, :]))
                            E('v', [PK(hh), ('rbc',)], [('af', pr, hh)],
                              lambda e, hh=hh, pr=pr: e.tensor_tensor(
                                  af[hh * 64:(hh + 1) * 64, pr * 512:(pr + 1) * 512],
                                  pb[hh][0:64, :], rbc[:], ALU.mult))
                    for dc in (range(2) if AB_OUTP[0] else []):
                        first = [True] * 4
                        for ic in range(AB_NIC[0]):
                            DMA('y', [], [('stg', 0)], lambda e, ic=ic, dc=dc:
                                e.dma_start(stg[:, 0:512],
                                            bass.AP(wo_b, ic * 128 * D + dc * 512,
                                                    [[D, 128], [1, 512]])))
                            E('v', [('stg', 0)], [('wcs',)], lambda e: e.tensor_copy(
                                wcs[:, 0:512], stg[:, 0:512]))
                            for ts_ in range(4):
                                pki = 3 + ts_
                                E('t', [('af', ic, 0), ('af', ic, 1), ('wcs',)],
                                  [PK(pki)] if first[ts_] else [],
                                  lambda e, ic=ic, ts_=ts_, pki=pki: e.matmul(
                                      pb[pki][:],
                                      af[:, ic * 512 + ts_ * 128:ic * 512 + (ts_ + 1) * 128],
                                      wcs[:, 0:512], start=(ic == 0), stop=False), inc=(ts_ == 3))
                            if False:
                                pass
                                first[ts_] = False
                        for ts_ in range(4):
                            pki = 3 + ts_
                            E('t', ['sm2', 'ones_f'], [PK(pki)],
                              lambda e, dc=dc, pki=pki: e.matmul(
                                  pb[pki][:], ones_f[:], borow[0:1, dc * 512:(dc + 1) * 512],
                                  start=False, stop=True))
                        for ts_ in range(4):
                            pki = 3 + ts_
                            t_tile = qg * 4 + ts_
                            rap = x[:, t_tile * D + dc * 512:t_tile * D + (dc + 1) * 512]
                            if dc == 0 and ts_ == 0 and AB_DBG[0]:
                                E('v', [PK(pki), ('ptb', 0, 0)], [('dbgs',), ('ptb', 0, 0)],
                                  lambda e: e.tensor_copy(ptb[:, 0:512], pb[pki][:]))
                                DMA('g', [('dbgs',)], [], lambda e: e.dma_start(
                                    dbg[:], ptb[:, 0:512]))
                            E('v', [PK(pki), ('x', t_tile)], [('x', t_tile)],
                              lambda e, pki=pki, rap=rap: e.scalar_tensor_tensor(
                                  rap, pb[pki][:], 1.0, rap, ALU.mult, ALU.add))
                p.barrier()

        if "attn_a" in phases:
            phase_attn_a()
        if "ffnq1" in phases:
            phase_ffnq()
        if "attn_b" in phases:
            phase_attn_b()
        if "ffnq2" in phases:
            phase_ffnq()

        for t in range(NT):
            DMA('g', [('x', t)], [], lambda e, t=t: e.dma_start(
                xout.rearrange("(t p) d -> t p d", p=128)[t], x[:, t * D:(t + 1) * D]))
        for i in range(p.NDMA):
            p._wait('g', (f'dg{i}', p.cnt[f'dg{i}']))
    return nc


def _prep(inputs):
    import ml_dtypes
    g = lambda k: np.asarray(inputs[k], dtype=np.float32)
    o = {}
    rowc = lambda a: np.ascontiguousarray(
        a.reshape(8, 128, 8, 128).transpose(2, 1, 0, 3).reshape(D, D))
    o['wq_a'] = rowc(g('L0_attn_a_q_proj').transpose(1, 0, 2).reshape(D, D))
    o['wk_a'] = rowc(g('L0_attn_a_k_proj').transpose(1, 0, 2).reshape(D, D))
    o['wv_a'] = rowc(g('L0_attn_a_v_proj').transpose(1, 0, 2).reshape(D, D))
    o['qb_a'] = np.ascontiguousarray(g('L0_attn_a_q_bias').reshape(D, 1))
    o['kb_a'] = np.ascontiguousarray(g('L0_attn_a_k_bias').reshape(D, 1))
    o['vb_a'] = np.ascontiguousarray(g('L0_attn_a_v_bias').reshape(1, D))
    o['lngb'] = np.ascontiguousarray(np.broadcast_to(np.concatenate(
        [g('L0_attn_a_ln1_g'), g('L0_attn_a_ln1_b'),
         g('L0_attn_a_ln2_g'), g('L0_attn_a_ln2_b')])[None, :], (128, 4 * D)))
    o['ff1a_w'] = np.ascontiguousarray(
        g('L0_attn_a_ff1_w').reshape(D, FFA // 128, 128).transpose(1, 0, 2)
        .reshape(FFA // 128, 8, 128, 128).transpose(0, 2, 1, 3).reshape(FFA, D))
    o['ff1a_b'] = np.ascontiguousarray(g('L0_attn_a_ff1_b').reshape(FFA, 1))
    o['ff2a_w'] = np.ascontiguousarray(g('L0_attn_a_ff2_w'))
    o['ff2a_b'] = np.ascontiguousarray(g('L0_attn_a_ff2_b').reshape(1, D))
    o['w1q'] = np.ascontiguousarray(
        g('L0_ffn_w1').T.reshape(D, FFQ // 128, 128).transpose(1, 0, 2)
        .reshape(FFQ // 128, 8, 128, 128).transpose(0, 2, 1, 3).reshape(FFQ, D))
    o['b1q'] = np.ascontiguousarray(g('L0_ffn_b1').reshape(FFQ, 1))
    o['w2q'] = np.ascontiguousarray(g('L0_ffn_w2').T)
    o['b2q'] = np.ascontiguousarray(g('L0_ffn_b2').reshape(1, D))
    o['wq_b'] = rowc(np.ascontiguousarray(g('L0_attn_b_Wq').T))
    o['wk_b'] = rowc(np.ascontiguousarray(g('L0_attn_b_Wk').T))
    o['wv_b'] = rowc(np.ascontiguousarray(g('L0_attn_b_Wv').T))
    o['wo_b'] = np.ascontiguousarray(g('L0_attn_b_Wo').T)
    o['bo_b'] = np.ascontiguousarray(g('L0_attn_b_bo').reshape(1, D))
    o['identc'] = np.eye(128, dtype=ml_dtypes.bfloat16)
    km = np.arange(128)
    o['maskc'] = (km[:, None] <= km[None, :]).astype(ml_dtypes.bfloat16)
    o['onesf'] = np.ones((1, 128), dtype=np.float32)
    hs = np.zeros((128, 2), dtype=ml_dtypes.bfloat16)
    hs[0:64, 0] = 1
    hs[64:128, 1] = 1
    o['hselc'] = hs
    return o


def kernel(**inputs):
    xfull = np.asarray(inputs['L0_attn_a_x'], dtype=np.float32)
    shared = _prep(inputs)
    nc = build()
    in_maps = []
    for c in range(NCORES):
        m = dict(shared)
        m['xin'] = np.ascontiguousarray(xfull[c])
        in_maps.append(m)
    res = run_bass_kernel_spmd(nc, in_maps, core_ids=list(range(NCORES)))
    return np.stack([r['xout'] for r in res.results], axis=0)


# version 08
# speedup vs baseline: 1.1193x; 1.0595x over previous
import numpy as np
import concourse.bass as bass
import concourse.mybir as mybir
from concourse.bass_utils import run_bass_kernel_spmd
from contextlib import ExitStack

F32 = mybir.dt.float32
BF16 = mybir.dt.bfloat16
FP8 = mybir.dt.float8e4
U8 = mybir.dt.uint8
AF = mybir.ActivationFunctionType
ALU = mybir.AluOpType

NCORES = 8
S, D = 2048, 1024
NT, ND = S // 128, D // 128
FFA, FFQ = 2048, 4096
LN_EPS = 1e-5
ATT_EPS = 1e-10
QE4 = dict(thr=2.0 ** -6, ovf=248.0, c1c=1, kc=2)
QE5 = dict(thr=2.0 ** -14, ovf=65520.0, c1c=3, kc=4)
PHASES = ("attn_a", "ffnq1", "attn_b", "ffnq2")
AA_STOP = [0]
AB_QGS = [None]
AB_NIC = [8]
AB_DBG = [False]
AB_PRS = [None]
AB_OUTP = [True]


class Prog:

    def __init__(self, nc, es):
        self.nc = nc
        self.eng = {'v': nc.vector, 's': nc.scalar, 't': nc.tensor,
                    'g': nc.gpsimd, 'y': nc.sync}
        self.sem, self.cnt = {}, {}
        for e in 'vstgy':
            self.sem[e] = es.enter_context(nc.semaphore("sm_" + e))
            self.cnt[e] = 0
        self.NDMA = 8
        self.dma_idx = {}
        for e in 'gy':
            for i in range(self.NDMA):
                nm = f'd{e}{i}'
                self.sem[nm] = es.enter_context(nc.semaphore("smd_" + nm))
                self.cnt[nm] = 0
            self.dma_idx[e] = 0
        self.wr, self.rd, self.mark = {}, {}, {}
        self.pend_w = {e: [] for e in 'vstgy'}
        self.pend_r = {e: [] for e in 'vstgy'}

    def _wait(self, e, tok):
        sem_name, val = tok
        if self.mark.get((e, sem_name), 0) >= val:
            return
        self.eng[e].wait_ge(self.sem[sem_name], val)
        self.mark[(e, sem_name)] = val

    def _pre(self, e, reads, writes):
        for k in reads:
            t = self.wr.get(k)
            if t is not None:
                self._wait(e, t)
        for k in writes:
            t = self.wr.get(k)
            if t is not None:
                self._wait(e, t)
            for t2 in self.rd.get(k, ()):
                self._wait(e, t2)

    def emit(self, e, reads, writes, fn, inc=True):
        self._pre(e, reads, writes)
        instr = fn(self.eng[e])
        if inc:
            self.cnt[e] += 1
            instr.then_inc(self.sem[e], 1)
            tok = (e, self.cnt[e])
            for k in self.pend_w[e]:
                self.wr[k] = tok
                self.rd[k] = []
            for k in self.pend_r[e]:
                self.rd.setdefault(k, []).append(tok)
            self.pend_w[e], self.pend_r[e] = [], []
            for k in writes:
                self.wr[k] = tok
                self.rd[k] = []
            for k in reads:
                self.rd.setdefault(k, []).append(tok)
            return tok
        self.pend_w[e].extend(writes)
        self.pend_r[e].extend(reads)
        return None

    def dma(self, e, reads, writes, fn):
        dn = f'd{e}{self.dma_idx[e]}'
        self.dma_idx[e] = (self.dma_idx[e] + 1) % self.NDMA
        self._wait(e, (dn, self.cnt[dn]))
        self._pre(e, reads, writes)
        instr = fn(self.eng[e])
        self.cnt[dn] += 16
        instr.then_inc(self.sem[dn], 16)
        tok = (dn, self.cnt[dn])
        for k in writes:
            self.wr[k] = tok
            self.rd[k] = []
        for k in reads:
            self.rd.setdefault(k, []).append(tok)
        return tok

    def barrier(self):
        for e in 'vstgy':
            assert not self.pend_w[e] and not self.pend_r[e], e
        for e in 'vstgy':
            for n in self.sem:
                self._wait(e, (n, self.cnt[n]))
        self.wr.clear()
        self.rd.clear()


def build(phases=PHASES):
    nc = bass.Bass("TRN2", target_bir_lowering=False)
    DT = lambda n, s_, d=F32, k="ExternalInput": nc.dram_tensor(n, s_, d, kind=k)
    xin = DT("xin", [S, D])
    xout = DT("xout", [S, D], F32, "ExternalOutput")
    wq_a, wk_a, wv_a = DT("wq_a", [D, D]), DT("wk_a", [D, D]), DT("wv_a", [D, D])
    qb_a, kb_a, vb_a = DT("qb_a", [D, 1]), DT("kb_a", [D, 1]), DT("vb_a", [1, D])
    lngb = DT("lngb", [128, 4 * D])
    ff1a_w, ff1a_b = DT("ff1a_w", [FFA, D]), DT("ff1a_b", [FFA, 1])
    ff2a_w, ff2a_b = DT("ff2a_w", [FFA, D]), DT("ff2a_b", [1, D])
    w1q_w, b1q_w = DT("w1q", [FFQ, D]), DT("b1q", [FFQ, 1])
    w2q_w, b2q_w = DT("w2q", [FFQ, D]), DT("b2q", [1, D])
    wq_b, wk_b = DT("wq_b", [D, D]), DT("wk_b", [D, D])
    wv_b, wo_b = DT("wv_b", [D, D]), DT("wo_b", [D, D])
    bo_b = DT("bo_b", [1, D])
    identc = DT("identc", [128, 128], BF16)
    maskc = DT("maskc", [128, 128], BF16)
    onesf = DT("onesf", [1, 128])
    hselc = DT("hselc", [128, 2], BF16)
    wqc = nc.dram_tensor("wqc", [FFQ, D], FP8)
    dbg = nc.dram_tensor("dbg", [128, 512], BF16, kind="ExternalOutput")

    es = ExitStack()
    with es:
        sb = lambda nm, shp, dt=F32: es.enter_context(nc.sbuf_tensor(nm, shp, dt))
        psb = lambda nm, shp, dt=F32: es.enter_context(nc.psum_tensor(nm, shp, dt))
        x = sb("x", [128, NT * D])
        xfm = sb("xfm", [128, ND * S], BF16)
        stg = sb("stg", [128, 1024])
        cons = sb("cons", [128, 16])
        ident = sb("ident", [128, 128], BF16)
        mask = sb("mask", [128, 128], BF16)
        ones_f = sb("ones_f", [1, 128])
        hsel = sb("hsel", [128, 2], BF16)
        infp = sb("infp", [128, 1])
        lnb = sb("lnb", [128, 16])
        pcb = sb("pcb", [128, 64])
        ybf = sb("ybf", [128, 2048], BF16)
        pb = [psb(f"pb{i}", [128, 512]) for i in range(7)]
        pT = psb("pT", [128, 1024], BF16)

        p = Prog(nc, es)
        E, DMA = p.emit, p.dma
        _uid = [0]

        def uname(base):
            _uid[0] += 1
            return f"{base}_{_uid[0]}"
        PK = lambda i: ('ps', i)

        def mmg(bank_key, parts):
            n = len(parts)
            for i, (reads, fn) in enumerate(parts):
                E('t', reads, [bank_key] if (i == 0 or i == n - 1) else [],
                  fn, inc=(i == n - 1))

        DMA('y', [], ['ident'], lambda e: e.dma_start(ident[:], identc[:]))
        DMA('y', [], ['mask'], lambda e: e.dma_start(mask[:], maskc[:]))
        DMA('y', [], ['ones_f'], lambda e: e.dma_start(ones_f[:], onesf[:]))
        DMA('y', [], ['hsel'], lambda e: e.dma_start(hsel[:], hselc[:]))
        for c in range(8):
            DMA('y', [], ['pcb'], lambda e, c=c: e.dma_start(
                pcb[:, c:c + 1], qb_a.rearrange("(c p) o -> c p o", p=128)[c]))
            DMA('y', [], ['pcb'], lambda e, c=c: e.dma_start(
                pcb[:, 8 + c:9 + c], kb_a.rearrange("(c p) o -> c p o", p=128)[c]))
        for c in range(32):
            DMA('y', [], ['pcb'], lambda e, c=c: e.dma_start(
                pcb[:, 16 + c:17 + c], b1q_w.rearrange("(c p) o -> c p o", p=128)[c]))
        for c in range(16):
            DMA('y', [], ['pcb'], lambda e, c=c: e.dma_start(
                pcb[:, 48 + c:49 + c], ff1a_b.rearrange("(c p) o -> c p o", p=128)[c]))
        for t in range(NT):
            DMA('y', [], [('x', t)], lambda e, t=t: e.dma_start(
                x[:, t * D:(t + 1) * D], xin.rearrange("(t p) d -> t p d", p=128)[t]))
        E('v', [], ['cons'], lambda e: e.memset(cons[:, 5:6], 1.0))
        E('v', ['cons'], ['cons'], lambda e: e.memset(cons[:, 0:1], 2.0 ** 13))
        for k in range(17, 53):
            tk = float(2.0 ** k)
            E('v', ['cons'], ['cons'], lambda e, tk=tk: e.tensor_scalar(
                cons[:, 6:7], cons[:, 5:6], tk, 1.0, ALU.add, ALU.mult))
            E('v', ['cons'], ['cons'], lambda e, tk=tk: e.tensor_scalar(
                cons[:, 6:7], cons[:, 6:7], tk, 1.0, ALU.subtract, ALU.mult))
            E('v', ['cons'], ['cons'], lambda e: e.tensor_scalar(
                cons[:, 7:8], cons[:, 6:7], 1.0, 1.0, ALU.add, ALU.mult))
            E('v', ['cons'], ['cons'], lambda e: e.tensor_tensor(
                cons[:, 0:1], cons[:, 0:1], cons[:, 7:8], ALU.mult))
        E('v', ['cons'], ['cons'], lambda e: e.tensor_scalar(
            cons[:, 1:2], cons[:, 0:1], 1.0, 1.0, ALU.add, ALU.mult))
        E('v', ['cons'], ['cons'], lambda e: e.tensor_scalar(
            cons[:, 2:3], cons[:, 0:1], float(12.0 * 2.0 ** -9), 1.0, ALU.mult, ALU.mult))
        E('v', ['cons'], ['cons'], lambda e: e.tensor_scalar(
            cons[:, 3:4], cons[:, 0:1], float(2.0 ** -7), 1.0, ALU.mult, ALU.mult))
        E('v', ['cons'], ['cons'], lambda e: e.tensor_scalar(
            cons[:, 3:4], cons[:, 3:4], 1.0, 1.0, ALU.add, ALU.mult))
        E('v', ['cons'], ['cons'], lambda e: e.tensor_scalar(
            cons[:, 4:5], cons[:, 0:1], float(12.0 * 2.0 ** -24), 1.0, ALU.mult, ALU.mult))
        E('v', ['cons'], ['cons'], lambda e: e.tensor_scalar(
            cons[:, 8:9], cons[:, 2:3], -1.0, 1.0, ALU.mult, ALU.mult))
        E('v', [], ['infp'], lambda e: e.memset(infp[:], float('inf')))
        E('v', ['pcb'], ['pcb'], lambda e: e.tensor_scalar(
            pcb[:, 0:8], pcb[:, 0:8], 0.125, 1.0, ALU.mult, ALU.mult))

        def ln_tile(src_key, src_ap, eps=LN_EPS):
            L = lnb
            E('v', [src_key, 'lnb'], ['lnb'], lambda e: e.bn_stats(L[:, 0:6], src_ap[:, 0:512]))
            E('v', [src_key, 'lnb'], ['lnb'], lambda e: e.bn_stats(L[:, 6:12], src_ap[:, 512:1024]))
            ms = [L[:, c:c + 1] for c in (1, 4, 7, 10)]
            cv = [L[:, c:c + 1] for c in (2, 5, 8, 11)]
            ee = lambda f: E('v', ['lnb'], ['lnb'], f)
            ee(lambda e: e.tensor_tensor(L[:, 12:13], ms[0], ms[1], ALU.add))
            ee(lambda e: e.tensor_tensor(L[:, 13:14], ms[2], ms[3], ALU.add))
            ee(lambda e: e.tensor_tensor(L[:, 12:13], L[:, 12:13], L[:, 13:14], ALU.add))
            ee(lambda e: e.tensor_scalar(L[:, 12:13], L[:, 12:13], 0.25, 1.0, ALU.mult, ALU.mult))
            ee(lambda e: e.tensor_tensor(L[:, 14:15], cv[0], cv[1], ALU.add))
            ee(lambda e: e.tensor_tensor(L[:, 15:16], cv[2], cv[3], ALU.add))
            ee(lambda e: e.tensor_tensor(L[:, 14:15], L[:, 14:15], L[:, 15:16], ALU.add))
            ee(lambda e: e.tensor_scalar(L[:, 14:15], L[:, 14:15], float(1 / 1024.0), 1.0, ALU.mult, ALU.mult))
            for m in ms:
                ee(lambda e, m=m: e.tensor_tensor(L[:, 15:16], m, m, ALU.mult))
                ee(lambda e: e.scalar_tensor_tensor(L[:, 14:15], L[:, 15:16], 0.25,
                                                    L[:, 14:15], ALU.mult, ALU.add))
            ee(lambda e: e.tensor_tensor(L[:, 15:16], L[:, 12:13], L[:, 12:13], ALU.mult))
            ee(lambda e: e.tensor_tensor(L[:, 14:15], L[:, 14:15], L[:, 15:16], ALU.subtract))
            ee(lambda e, ep=eps: e.tensor_scalar(L[:, 14:15], L[:, 14:15], float(ep), 1.0, ALU.add, ALU.mult))
            E('s', ['lnb'], ['lnb'], lambda e: e.activation(L[:, 14:15], L[:, 14:15], AF.Sqrt))
            ee(lambda e: e.reciprocal(L[:, 14:15], L[:, 14:15]))
            ee(lambda e: e.tensor_tensor(L[:, 13:14], L[:, 12:13], L[:, 14:15], ALU.mult))
            ee(lambda e: e.tensor_scalar(L[:, 13:14], L[:, 13:14], -1.0, 1.0, ALU.mult, ALU.mult))
            return L[:, 14:15], L[:, 13:14]

        def tile_to_xfm(src_key, src_bf_ap, t):
            for dcI in range(ND):
                E('t', [src_key, 'ident'],
                  [('pT',)] if dcI in (0, ND - 1) else [],
                  lambda e, dcI=dcI: e.transpose(
                      pT[:, dcI * 128:(dcI + 1) * 128],
                      src_bf_ap[:, dcI * 128:(dcI + 1) * 128], ident[:]),
                  inc=(dcI == ND - 1))
            dst = bass.AP(xfm, t * 128, [[ND * S, 128], [S, ND], [1, 128]])
            srcp = bass.AP(pT, 0, [[1024, 128], [128, ND], [1, 128]])
            E('v', [('pT',)], [('xfm', t)], lambda e: e.tensor_copy(dst, srcp))

        def ln_to_xfm(src_of, f32_dst_of=None):
            for t in range(NT):
                sk, sap = src_of(t)
                rstd, nmu = ln_tile(sk, sap)
                yb = ybf[:, (t % 2) * 1024:(t % 2 + 1) * 1024]
                ybk = ('ybf', t % 2)
                if f32_dst_of is not None:
                    dk, dap = f32_dst_of(t)
                    E('v', [sk, 'lnb'], [dk], lambda e, sap=sap, dap=dap, r=rstd, n=nmu:
                      e.tensor_scalar(dap, sap, r, n, ALU.mult, ALU.add))
                    E('v', [dk], [ybk], lambda e, dap=dap, yb=yb: e.tensor_copy(yb, dap))
                else:
                    E('v', [sk, 'lnb'], [ybk], lambda e, sap=sap, yb=yb, r=rstd, n=nmu:
                      e.tensor_scalar(yb, sap, r, n, ALU.mult, ALU.add))
                tile_to_xfm(ybk, yb, t)
            p.barrier()

        def emit_quant(src_key, src_ap, dst_key, dst_ap, sc, sck, m1, qp,
                       signed=False, n=512, tt_engs=('g', 'g', 'g'),
                       m2=None, infw_ap=None, defer_cp=False):
            T = sck
            srcs = (src_key if isinstance(src_key, list) else [src_key])
            npart = src_ap.shape[0]
            c1 = cons[0:npart, qp['c1c']:qp['c1c'] + 1]
            K = cons[0:npart, qp['kc']:qp['kc'] + 1]
            hpath = (not signed) and qp is QE4
            e1, e2, e3 = tt_engs
            use_m2 = m2 is not None
            if signed:
                E('s', srcs + [T], [T], lambda e: e.activation(sc['f'], src_ap, AF.Abs))
                mag = sc['f']
            else:
                mag = src_ap
            E('v', srcs + [T], [T], lambda e: e.tensor_scalar(m1, mag, float(qp['thr']), None, ALU.is_ge))
            if use_m2:
                E('v', srcs + [T], [T], lambda e: e.tensor_scalar(
                    m2, mag, float(qp['ovf']), None, ALU.is_ge))
            else:
                E('v', srcs + [T], [T], lambda e: e.tensor_scalar(
                    sc['f'], mag, float(qp['ovf']), None, ALU.is_ge))
                E('v', [T], [T], lambda e: e.tensor_scalar(
                    sc['f'], sc['f'], 3.4e38, 1.0, ALU.mult, ALU.add))
            if hpath:
                negK = cons[0:npart, 8:9]
                E('s', srcs + ['cons', T], [T], lambda e: e.activation(
                    sc['y'], src_ap, AF.Relu, bias=0.0, scale=c1))
                E(e1, srcs + [T], [T], lambda e: e.tensor_tensor(sc['t'], sc['y'], src_ap, ALU.subtract))
                E(e2, [T], [T], lambda e: e.tensor_tensor(sc['y'], sc['y'], sc['t'], ALU.subtract))
                if not use_m2:
                    E(e3, [T], [T], lambda e: e.tensor_tensor(sc['y'], sc['y'], sc['f'], ALU.mult))
                E('s', srcs + ['cons', T], [T], lambda e: e.activation(
                    sc['u'], src_ap, AF.Relu, bias=K, scale=1.0))
                E('s', ['cons', T], [dst_key, T], lambda e: e.activation(
                    dst_ap, sc['u'], AF.Relu, bias=negK, scale=1.0))
            else:
                E('v', srcs + ['cons', T], [T], lambda e: e.tensor_scalar(
                    sc['y'], src_ap, c1, 1.0, ALU.mult, ALU.mult))
                E(e1, srcs + [T], [T], lambda e: e.tensor_tensor(sc['t'], sc['y'], src_ap, ALU.subtract))
                E(e2, [T], [T], lambda e: e.tensor_tensor(sc['y'], sc['y'], sc['t'], ALU.subtract))
                E(e3, [T], [T], lambda e: e.tensor_tensor(sc['y'], sc['y'], sc['f'], ALU.mult))
                E('v', srcs + ['cons', T], [T], lambda e: e.tensor_scalar(
                    sc['u'], src_ap, K, 1.0, ALU.add, ALU.mult))
                E('v', ['cons', T], [dst_key, T], lambda e: e.tensor_scalar(
                    dst_ap, sc['u'], K, 1.0, ALU.subtract, ALU.mult))

            def fin():
                E('v', [dst_key, T], [dst_key] if not use_m2 else [],
                  lambda e: e.copy_predicated(dst_ap, m1, sc['y']))
                if use_m2:
                    E('v', [dst_key, T, 'infw'], [dst_key],
                      lambda e: e.copy_predicated(dst_ap, m2, infw_ap))
            if defer_cp:
                return fin
            fin()

        def phase_attn_a():
            with ExitStack() as ps_:
                A = ps_.enter_context(nc.sbuf_tensor(uname("A"), [128, NT * D], F32))
                ps_in = ps_.enter_context(ExitStack())
                aB = ps_in.enter_context(nc.sbuf_tensor(uname("aB"), [128, 10240], BF16))
                vrow = ps_in.enter_context(nc.sbuf_tensor(uname("vrow"), [1, D], F32))
                DMA('y', [], ['vrow'], lambda e: e.dma_start(vrow[:], vb_a[:]))
                ln_to_xfm(lambda t: (('x', t), x[:, t * D:(t + 1) * D]),
                          f32_dst_of=lambda t: (('A', t), A[:, t * D:(t + 1) * D]))
                Qp, Kp = aB[:, 0:2048], aB[:, 2048:4096]
                PTr = [aB[:, 4096 + i * 256:4096 + (i + 1) * 256] for i in range(4)]
                wq_t, wk_t = aB[:, 5120:6144], aB[:, 6144:7168]
                VB = 7168
                for pr in range(8):
                    for wdram, wtile, wkn, soff in ((wq_a, wq_t, 'wqp', 0),
                                                    (wk_a, wk_t, 'wkp', 0)):
                        src = bass.AP(wdram, pr * 128 * D, [[D, 128], [1, D]])
                        DMA('y', [], [('stg', soff)], lambda e, src=src, soff=soff:
                            e.dma_start(stg[:, soff:soff + 1024], src))
                        E('v', [('stg', soff)], [(wkn, pr)],
                          lambda e, soff=soff, wtile=wtile: e.tensor_copy(
                              wtile, stg[:, soff:soff + 1024]))
                    for which, (wkn, wtile, dst, boff, scale) in enumerate(
                            (('wqp', wq_t, Qp, 0, 0.125), ('wkp', wk_t, Kp, 8, 1.0))):
                        for tg in range(4):
                            mmg(PK(which), [
                                ([(wkn, pr)],
                                 (lambda e, wtile=wtile, dc=dc, tg=tg, w=which: e.matmul(
                                     pb[w][:], wtile[:, dc * 128:(dc + 1) * 128],
                                     xfm[:, dc * S + tg * 512:dc * S + (tg + 1) * 512],
                                     start=(dc == 0), stop=(dc == ND - 1))))
                                for dc in range(ND)])
                            E('v', [PK(which), 'pcb'], [('qk', pr, which)],
                              lambda e, dst=dst, tg=tg, b=boff, sc_=scale, w=which:
                              e.tensor_scalar(dst[:, tg * 512:(tg + 1) * 512], pb[w][:],
                                              sc_, pcb[:, b + pr:b + pr + 1], ALU.mult, ALU.add))
                    src = bass.AP(wv_a, pr * 128 * D, [[D, 128], [1, D]])
                    DMA('y', [], [('stg', 0)], lambda e, src=src: e.dma_start(
                        stg[:, 0:1024], src))
                    E('v', [('stg', 0), ('wqp', pr)], [('wvp', pr)],
                      lambda e: e.tensor_copy(wq_t, stg[:, 0:1024]))
                    for t in range(NT):
                        mmg(PK(2),
                            [([('wvp', pr)],
                              (lambda e, dc=dc, t=t: e.matmul(
                                  pb[2][:, 0:128],
                                  xfm[:, dc * S + t * 128:dc * S + (t + 1) * 128],
                                  wq_t[:, dc * 128:(dc + 1) * 128],
                                  start=(dc == 0), stop=False)))
                             for dc in range(ND)] +
                            [(['ones_f', 'vrow'],
                              lambda e: e.matmul(pb[2][:, 0:128], ones_f[:],
                                                 vrow[0:1, pr * 128:(pr + 1) * 128],
                                                 start=False, stop=True))])
                        E('v', [PK(2)], [('vau', pr, t)], lambda e, t=t: e.tensor_copy(
                            bass.AP(aB, VB + t * 130, [[10240, 128], [65, 2], [1, 64]]),
                            bass.AP(pb[2], 0, [[512, 128], [64, 2], [1, 64]])))
                        E('v', [], [('vau', pr, t)], lambda e, t=t: e.memset(
                            bass.AP(aB, VB + t * 130 + 64, [[10240, 128], [65, 2], [1, 1]]), 1.0))
                    for kw in range(NT):
                        qn = 256 if kw < NT - 1 else 128
                        kwp = kw % 2
                        pA, pB_ = pb[3 + kwp], pb[5 + kwp]
                        E('t', [('qk', pr, 0), ('qk', pr, 1)], [PK(3 + kwp)],
                          lambda e, kw=kw, qn=qn, pA=pA: e.matmul(
                              pA[:, 0:qn], Kp[0:64, kw * 128:(kw + 1) * 128],
                              Qp[0:64, kw * 128:kw * 128 + qn], start=True, stop=True,
                              tile_position=(0, 0)))
                        E('t', [], [PK(5 + kwp)],
                          lambda e, kw=kw, qn=qn, pB_=pB_: e.matmul(
                              pB_[:, 0:qn], Kp[64:128, kw * 128:(kw + 1) * 128],
                              Qp[64:128, kw * 128:kw * 128 + qn], start=True, stop=True,
                              tile_position=(64, 0)))
                        for hh, pp, bk in ((0, pA, PK(3 + kwp)), (1, pB_, PK(5 + kwp))):
                            rt = PTr[kwp * 2 + hh]
                            E('s', [bk], [('pt', hh, kwp)], lambda e, rt=rt, qn=qn, pp=pp:
                              e.activation(rt[:, 0:qn], pp[:, 0:qn], AF.Exp))
                            E('v', [('pt', hh, kwp), 'mask'], [('pt', hh, kwp)],
                              lambda e, rt=rt: e.tensor_tensor(
                                  rt[:, 0:128], rt[:, 0:128], mask[:], ALU.mult))
                        pav = pb[kwp]
                        parts = []
                        for hh in range(2):
                            rprev = PTr[((kw - 1) % 2) * 2 + hh]
                            rown = PTr[kwp * 2 + hh]
                            vsl = bass.AP(aB, VB + kw * 130 + hh * 65, [[10240, 128], [1, 65]])
                            vslp = bass.AP(aB, VB + (kw - 1) * 130 + hh * 65, [[10240, 128], [1, 65]])
                            if kw > 0:
                                parts.append(([('pt', hh, (kw - 1) % 2), ('vau', pr, kw - 1)],
                                              (lambda e, rprev=rprev, vslp=vslp, hh=hh:
                                               e.matmul(pav[:, hh * 65:(hh + 1) * 65],
                                                        rprev[:, 128:256], vslp,
                                                        start=True, stop=False))))
                            parts.append(([('pt', hh, kwp), ('vau', pr, kw)],
                                          (lambda e, rown=rown, vsl=vsl, hh=hh, kw=kw:
                                           e.matmul(pav[:, hh * 65:(hh + 1) * 65],
                                                    rown[:, 0:128], vsl,
                                                    start=(kw == 0), stop=True))))
                        mmg(PK(kwp), parts)
                        for hh in range(2):
                            hcol = (2 * pr + hh) * 64
                            E('v', [PK(kwp), 'lnb'], ['lnb'], lambda e, hh=hh, pav=pav:
                              e.reciprocal(lnb[:, hh:hh + 1], pav[:, hh * 65 + 64:hh * 65 + 65]))
                            E('v', [PK(kwp), 'lnb', ('A', kw)], [('A', kw)],
                              lambda e, hh=hh, kw=kw, hcol=hcol, pav=pav: e.scalar_tensor_tensor(
                                  A[:, kw * D + hcol:kw * D + hcol + 64],
                                  pav[:, hh * 65:hh * 65 + 64], lnb[:, hh:hh + 1],
                                  A[:, kw * D + hcol:kw * D + hcol + 64], ALU.mult, ALU.add))
                p.barrier()
                if AA_STOP[0] == 1:
                    for t in range(NT):
                        E('v', [('A', t)], [('x', t)], lambda e, t=t: e.tensor_copy(
                            x[:, t * D:(t + 1) * D], A[:, t * D:(t + 1) * D]))
                    p.barrier()
                    return
                ps_in.close()
                with ExitStack() as ps2:
                    def inner_ln(gbt, add_to_x):
                        for t in range(NT):
                            Aap = A[:, t * D:(t + 1) * D]
                            rstd, nmu = ln_tile(('A', t), Aap)
                            E('v', [('A', t), 'lnb'], [('A', t)], lambda e, Aap=Aap, r=rstd, n=nmu:
                              e.tensor_scalar(Aap, Aap, r, n, ALU.mult, ALU.add))
                            E('v', [('A', t), 'gbt'], [('A', t)], lambda e, Aap=Aap:
                              e.tensor_tensor(Aap, Aap, gbt[:, 0:D], ALU.mult))
                            E('v', [('A', t), 'gbt'], [('A', t)], lambda e, Aap=Aap:
                              e.tensor_tensor(Aap, Aap, gbt[:, D:2 * D], ALU.add))
                            if add_to_x:
                                E('v', [('A', t), ('x', t)], [('x', t)], lambda e, t=t, Aap=Aap:
                                  e.tensor_tensor(x[:, t * D:(t + 1) * D],
                                                  x[:, t * D:(t + 1) * D], Aap, ALU.add))
                            else:
                                yb = ybf[:, (t % 2) * 1024:(t % 2 + 1) * 1024]
                                E('v', [('A', t)], [('ybf', t % 2)], lambda e, Aap=Aap, yb=yb:
                                  e.tensor_copy(yb, Aap))
                                tile_to_xfm(('ybf', t % 2), yb, t)
                    with nc.sbuf_tensor(uname("gbt1"), [128, 2 * D], F32) as gbt:
                        DMA('y', [], ['gbt'], lambda e: e.dma_start(gbt[:], lngb[:, 0:2 * D]))
                        inner_ln(gbt, add_to_x=False)
                        p.barrier()
                    if AA_STOP[0] == 2:
                        for t in range(NT):
                            E('v', [('A', t)], [('x', t)], lambda e, t=t: e.tensor_copy(
                                x[:, t * D:(t + 1) * D], A[:, t * D:(t + 1) * D]))
                        p.barrier()
                        return
                    emit_ffn(A, lambda t: ('A', t), hid=FFA, w1d=ff1a_w, b1off=48,
                             w2d=ff2a_w, brow_dram=ff2a_b, quant=False, tokgrp=512)
                    p.barrier()
                    if AA_STOP[0] == 3:
                        for t in range(NT):
                            E('v', [('A', t)], [('x', t)], lambda e, t=t: e.tensor_copy(
                                x[:, t * D:(t + 1) * D], A[:, t * D:(t + 1) * D]))
                        p.barrier()
                        return
                    with nc.sbuf_tensor(uname("gbt2"), [128, 2 * D], F32) as gbt:
                        DMA('y', [], ['gbt'], lambda e: e.dma_start(gbt[:], lngb[:, 2 * D:4 * D]))
                        inner_ln(gbt, add_to_x=True)
                        p.barrier()

        wqc_state = {'done': False}

        def emit_ffn(rbuf, rkey, hid, w1d, b1off, w2d, brow_dram, quant,
                     tokgrp=512):
            nj = hid // 128
            nhalf = S // tokgrp
            with ExitStack() as ps_:
                fw = ybf
                fstg = ps_.enter_context(nc.sbuf_tensor(uname("fstg"), [128, 2048], F32))
                brow = ps_.enter_context(nc.sbuf_tensor(uname("fbrow"), [1, D], F32))
                if quant:
                    fh = ps_.enter_context(nc.sbuf_tensor(uname("fh"), [128, 1024], F32))
                    fsc = ps_.enter_context(nc.sbuf_tensor(uname("fsc"), [128, 3072], F32))
                    fm8 = ps_.enter_context(nc.sbuf_tensor(uname("fm8"), [128, 2048], U8))
                    hq8 = ps_.enter_context(
                        nc.sbuf_tensor(uname("fq8"), [128, nj * 512 + 1536], FP8))
                    SC = [dict(y=fsc[:, p * 1536 + 0:p * 1536 + 512],
                               t=fsc[:, p * 1536 + 512:p * 1536 + 1024],
                               u=fsc[:, p * 1536 + 512:p * 1536 + 1024],
                               f=fsc[:, p * 1536 + 1024:p * 1536 + 1536])
                          for p in range(2)]
                    M1 = [fm8[:, p * 512:(p + 1) * 512] for p in range(2)]
                    M2 = [fm8[:, 1024 + p * 512:1024 + (p + 1) * 512] for p in range(2)]
                else:
                    hB = ps_.enter_context(
                        nc.sbuf_tensor(uname("fhB"), [128, nj * 512], BF16))
                DMA('y', [], ['brow'], lambda e: e.dma_start(brow[:], brow_dram[:]))
                if quant:
                    scW = dict(y=fsc[0:1, 0:512], t=fsc[0:1, 512:1024],
                               u=fsc[0:1, 512:1024], f=fsc[0:1, 1024:1536])
                    for hf in range(2):
                        emit_quant('brow', brow[0:1, hf * 512:(hf + 1) * 512], 'brow',
                                   brow[0:1, hf * 512:(hf + 1) * 512],
                                   scW, ('fscr', 0), fm8[0:1, 0:512],
                                   QE5, signed=True, n=512)
                building = quant and not wqc_state['done']
                wq_dma = emit_wqc_chain = None
                if building:
                    wstg1 = ps_.enter_context(
                        nc.sbuf_tensor(uname("wstg1"), [128, 1024], F32))
                    wslot = [stg[:, 0:1024], wstg1[:, 0:1024]]
                    wsc3 = ps_.enter_context(
                        nc.sbuf_tensor(uname("wsc3"), [128, 6144], F32))
                    wm1k = ps_.enter_context(
                        nc.sbuf_tensor(uname("wm1k"), [128, 2048], U8))
                    wq82 = ps_.enter_context(
                        nc.sbuf_tensor(uname("wq82"), [128, 2048], FP8))
                    scQ = [dict(y=wsc3[:, p * 3072 + 0:p * 3072 + 1024],
                                t=wsc3[:, p * 3072 + 1024:p * 3072 + 2048],
                                u=wsc3[:, p * 3072 + 1024:p * 3072 + 2048],
                                f=wsc3[:, p * 3072 + 2048:p * 3072 + 3072])
                           for p in range(2)]

                    def wq_dma(j):
                        jw = j % 2
                        DMA('y' if jw == 0 else 'g', [], [('wstg', jw)],
                            lambda e, j=j, jw=jw: e.dma_start(
                                wslot[jw],
                                w2q_w.rearrange("(c p) d -> c p d", p=128)[j]))

                    def emit_wqc_chain(j, pendw):
                        jw = j % 2
                        te = ('g', 'g', 'g')
                        f0 = emit_quant(('wstg', jw), wslot[jw],
                                        ('wq8w', jw), wq82[:, jw * 1024:(jw + 1) * 1024],
                                        scQ[jw], ('wscr', jw),
                                        wm1k[:, jw * 1024:(jw + 1) * 1024],
                                        QE4, signed=True, n=1024, tt_engs=te,
                                        defer_cp=True)

                        def fin(j=j, jw=jw, f0=f0):
                            f0()
                            DMA('g' if jw == 0 else 'y', [('wq8w', jw)], [('wqc', j)],
                                lambda e, j=j, jw=jw: e.dma_start(
                                    wqc.rearrange("(c p) d -> c p d", p=128)[j],
                                    wq82[:, jw * 1024:(jw + 1) * 1024]))
                        pendw.append(fin)
                    wqc_state['done'] = True
                for half in range(nhalf):
                    hb = half * tokgrp

                    def w1_dma(j):
                        jb = j % 2
                        aw = [('fws', jb)] + ([('fwsb', 0), ('fwsb', 1)] if jb == 0 else [('fwsb', 2)])
                        DMA('y' if jb == 0 else 'g', [], aw,
                            lambda e, j=j, jb=jb: e.dma_start(
                                fstg[:, jb * 1024:(jb + 1) * 1024],
                                bass.AP(w1d, j * 128 * D, [[D, 128], [1, D]])))
                    w1_dma(0)
                    w1_dma(1)

                    def w1_cast(j):
                        jb = j % 2
                        wsl = fw[:, jb * 1024:(jb + 1) * 1024]
                        cw = [('fw', jb)] + ([('fwb', 0), ('fwb', 1)] if jb == 0 else [('fwb', 2)])
                        E('v', [('fws', jb)], cw,
                          lambda e, jb=jb, wsl=wsl: e.tensor_copy(
                              wsl, fstg[:, jb * 1024:(jb + 1) * 1024]))
                    w1_cast(0)
                    bld = building and half == 0
                    if bld:
                        wq_dma(0)
                    pend, pendw = [], []
                    for j in range(nj + 1):
                        if j < nj:
                            jb = j % 2
                            wsl = fw[:, jb * 1024:(jb + 1) * 1024]
                            if j + 1 < nj:
                                w1_cast(j + 1)
                            if j + 2 < nj:
                                w1_dma(j + 2)
                            if bld and j + 1 < nj:
                                wq_dma(j + 1)
                            mmg(PK(jb), [
                                ([('fw', jb)],
                                 (lambda e, wsl=wsl, dc=dc, jb=jb: e.matmul(
                                     pb[jb][:], wsl[:, dc * 128:(dc + 1) * 128],
                                     xfm[:, dc * S + hb:dc * S + hb + 512],
                                     start=(dc == 0), stop=(dc == ND - 1))))
                                for dc in range(ND)])
                            if quant:
                                hsl = fh[:, jb * 512:(jb + 1) * 512]
                                E('s', [PK(jb), 'pcb', ('fh', jb)], [('fh', jb)],
                                  lambda e, jb=jb, j=j, hsl=hsl: e.activation(
                                      hsl, pb[jb][:], AF.Relu,
                                      bias=pcb[:, b1off + j:b1off + j + 1], scale=1.0))
                                pend.append(emit_quant(
                                    ('fh', jb), hsl,
                                    ('h8', j), hq8[:, j * 512:(j + 1) * 512],
                                    SC[jb], ('fscr', jb), M1[jb], QE4,
                                    tt_engs=('g', 'g', 'v'), defer_cp=True))
                            else:
                                E('s', [PK(jb), 'pcb', ('h8', j)], [('h8', j)],
                                  lambda e, jb=jb, j=j: e.activation(
                                      hB[:, j * 512:(j + 1) * 512],
                                      pb[jb][:], AF.Relu,
                                      bias=pcb[:, b1off + j:b1off + j + 1], scale=1.0))
                            if bld:
                                emit_wqc_chain(j, pendw)
                        while pend and (len(pend) > 1 or j == nj):
                            pend.pop(0)()
                        while pendw and (len(pendw) > 1 or j == nj):
                            pendw.pop(0)()
                    if quant:
                        rts = [hq8[:, nj * 512 + q * 512:nj * 512 + (q + 1) * 512]
                               for q in range(3)]
                        for dc in range(2):
                            def rt_dma(jq, dc=dc):
                                q = jq % 3
                                DMA('y' if q % 2 == 0 else 'g',
                                    [('fq', q), ('wqc', jq)], [('fq', q)],
                                    lambda e, jq=jq, q=q: e.dma_start(
                                        rts[q], bass.AP(wqc, jq * 128 * D + dc * 512,
                                                        [[D, 128], [1, 512]])))
                            for q0 in range(3):
                                rt_dma(q0)
                            for j in range(nj):
                                q = j % 3
                                rt = rts[q]
                                for tt in range(4):
                                    pki = 3 + tt
                                    lap = hq8[:, j * 512 + tt * 128:j * 512 + (tt + 1) * 128]
                                    E('t', [('h8', j), ('fq', q)],
                                      [PK(pki)] if j == 0 else [],
                                      lambda e, lap=lap, rt=rt, pki=pki, j=j: e.matmul(
                                          pb[pki][:], lap, rt, start=(j == 0), stop=False),
                                      inc=(tt == 3))
                                if j + 3 < nj:
                                    rt_dma(j + 3)
                            for tt in range(4):
                                pki = 3 + tt
                                E('t', ['brow', 'ones_f'], [PK(pki)],
                                  lambda e, dc=dc, pki=pki: e.matmul(
                                      pb[pki][:], ones_f[:], brow[0:1, dc * 512:(dc + 1) * 512],
                                      start=False, stop=True))
                            efin = []
                            for tt in range(4):
                                if len(efin) >= 2:
                                    efin.pop(0)()
                                pki = 3 + tt
                                t_tile = half * (tokgrp // 128) + tt
                                par = tt % 2
                                osl = fh[:, par * 512:(par + 1) * 512]
                                rap = rbuf[:, t_tile * D + dc * 512:t_tile * D + (dc + 1) * 512]
                                E('v', [PK(pki)], [('fh', par)],
                                  lambda e, pki=pki, osl=osl: e.tensor_copy(osl, pb[pki][:]))
                                f0 = emit_quant(('fh', par), osl, ('fh', par), osl,
                                                SC[par], ('fscr', par), M1[par], QE5,
                                                signed=True, tt_engs=('g', 'g', 'v'),
                                                defer_cp=True)

                                def fin(f0=f0, osl=osl, rap=rap, t_tile=t_tile, par=par):
                                    f0()
                                    E('v', [('fh', par), rkey(t_tile)], [rkey(t_tile)],
                                      lambda e: e.scalar_tensor_tensor(
                                          rap, osl, 1.0, rap, ALU.mult, ALU.add))
                                efin.append(fin)
                            for f in efin:
                                f()
                    else:
                        for dc in range(2):
                            def w2_dma(j, dc=dc):
                                q = j % 3
                                DMA('y' if q % 2 == 0 else 'g',
                                    [('fwb', q)], [('fwsb', q), ('fws', q // 2)],
                                    lambda e, j=j, dc=dc, q=q: e.dma_start(
                                        fstg[:, q * 512:(q + 1) * 512],
                                        bass.AP(w2d, j * 128 * D + dc * 512,
                                                [[D, 128], [1, 512]])))
                            w2_dma(0)
                            w2_dma(1)
                            w2_dma(2)
                            for j in range(nj):
                                q = j % 3
                                rt = fw[:, q * 512:(q + 1) * 512]
                                E('v', [('fwsb', q)], [('fwb', q), ('fw', q // 2)],
                                  lambda e, q=q, rt=rt: e.tensor_copy(
                                      rt, fstg[:, q * 512:(q + 1) * 512]))
                                for tt in range(4):
                                    pki = 3 + tt
                                    lap = hB[:, j * 512 + tt * 128:j * 512 + (tt + 1) * 128]
                                    E('t', [('h8', j), ('fwb', q)],
                                      [PK(pki)] if j == 0 else [],
                                      lambda e, lap=lap, rt=rt, pki=pki, j=j: e.matmul(
                                          pb[pki][:], lap, rt, start=(j == 0), stop=False),
                                      inc=(tt == 3))
                                if j + 3 < nj:
                                    w2_dma(j + 3)
                            for tt in range(4):
                                pki = 3 + tt
                                E('t', ['brow', 'ones_f'], [PK(pki)],
                                  lambda e, dc=dc, pki=pki: e.matmul(
                                      pb[pki][:], ones_f[:], brow[0:1, dc * 512:(dc + 1) * 512],
                                      start=False, stop=True))
                            for tt in range(4):
                                pki = 3 + tt
                                t_tile = half * (tokgrp // 128) + tt
                                rap = rbuf[:, t_tile * D + dc * 512:t_tile * D + (dc + 1) * 512]
                                E('v', [PK(pki), rkey(t_tile)], [rkey(t_tile)],
                                  lambda e, pki=pki, rap=rap: e.scalar_tensor_tensor(
                                      rap, pb[pki][:], 1.0, rap, ALU.mult, ALU.add))
                p.barrier()

        def phase_ffnq():
            ln_to_xfm(lambda t: (('x', t), x[:, t * D:(t + 1) * D]))
            emit_ffn(x, lambda t: ('x', t), hid=FFQ, w1d=w1q_w, b1off=16,
                     w2d=w2q_w, brow_dram=b2q_w, quant=True)

        def phase_attn_b():
            with ExitStack() as ps_:
                Kf = ps_.enter_context(nc.sbuf_tensor(uname("Kf"), [128, 16384], BF16))
                Va = ps_.enter_context(nc.sbuf_tensor(uname("Va"), [128, NT * 1040], BF16))
                Qs = ps_.enter_context(nc.sbuf_tensor(uname("Qs"), [128, 4096], BF16))
                af = ps_.enter_context(nc.sbuf_tensor(uname("af"), [128, 4096], BF16))
                ptb = ps_.enter_context(nc.sbuf_tensor(uname("ptb"), [128, 2048], BF16))
                wcs = ps_.enter_context(nc.sbuf_tensor(uname("wcs"), [128, 1024], BF16))
                sm2 = ps_.enter_context(nc.sbuf_tensor(uname("sm2"), [1, 3072], F32))
                rbc = ps_.enter_context(nc.sbuf_tensor(uname("rbc"), [64, 512], F32))
                kmx = ps_.enter_context(nc.sbuf_tensor(uname("kmx"), [1, 24], F32))
                borow = sm2[0:1, 2048:3072]
                ln_to_xfm(lambda t: (('x', t), x[:, t * D:(t + 1) * D]))
                DMA('y', ['sm2'], ['sm2'], lambda e: e.dma_start(borow[:], bo_b[:]))
                for pr in range(8):
                    src = bass.AP(wk_b, pr * 128 * D, [[D, 128], [1, D]])
                    DMA('y', [], [('stg', 0)], lambda e, src=src: e.dma_start(
                        stg[:, 0:1024], src))
                    E('v', [('stg', 0)], [('wcs',)], lambda e: e.tensor_copy(
                        wcs[:], stg[:, 0:1024]))
                    for tg in range(4):
                        mmg(PK(0), [
                            ([('wcs',)],
                             (lambda e, dc=dc, tg=tg: e.matmul(
                                 pb[0][:], wcs[:, dc * 128:(dc + 1) * 128],
                                 xfm[:, dc * S + tg * 512:dc * S + (tg + 1) * 512],
                                 start=(dc == 0), stop=(dc == ND - 1))))
                            for dc in range(ND)])
                        E('v', [PK(0)], [('kf', pr)], lambda e, pr=pr, tg=tg: e.tensor_copy(
                            Kf[:, pr * 2048 + tg * 512:pr * 2048 + (tg + 1) * 512], pb[0][:]))
                    E('s', [('kf', pr)], [('ksq',)], lambda e, pr=pr: e.activation(
                        ptb[:, 0:2048], Kf[:, pr * 2048:(pr + 1) * 2048], AF.Square))
                    for hh in range(2):
                        for tg in range(4):
                            E('t', [('ksq',), 'hsel'], [PK(1)], lambda e, tg=tg, hh=hh: e.matmul(
                                pb[1][0:1, 0:512], hsel[:, hh:hh + 1],
                                ptb[:, tg * 512:(tg + 1) * 512], start=True, stop=True))
                            E('v', [PK(1), 'kmx'], ['kmx'], lambda e: e.tensor_reduce(
                                kmx[0:1, 16:17], pb[1][0:1, 0:512], mybir.AxisListType.X, ALU.max))
                            col = hh * 8 + pr
                            if tg == 0:
                                E('v', ['kmx'], ['kmx'], lambda e, col=col: e.tensor_copy(
                                    kmx[0:1, col:col + 1], kmx[0:1, 16:17]))
                            else:
                                E('v', ['kmx'], ['kmx'], lambda e, col=col: e.tensor_tensor(
                                    kmx[0:1, col:col + 1], kmx[0:1, col:col + 1], kmx[0:1, 16:17], ALU.max))
                    src = bass.AP(wv_b, pr * 128 * D, [[D, 128], [1, D]])
                    DMA('y', [], [('stg', 0)], lambda e, src=src: e.dma_start(
                        stg[:, 0:1024], src))
                    E('v', [('stg', 0)], [('wcs',)], lambda e: e.tensor_copy(
                        wcs[:], stg[:, 0:1024]))
                    for t in range(NT):
                        mmg(PK(2), [
                            ([('wcs',)],
                             (lambda e, dc=dc, t=t: e.matmul(
                                 pb[2][:, 0:128],
                                 xfm[:, dc * S + t * 128:dc * S + (t + 1) * 128],
                                 wcs[:, dc * 128:(dc + 1) * 128],
                                 start=(dc == 0), stop=(dc == ND - 1))))
                            for dc in range(ND)])
                        E('v', [PK(2)], [('vab', pr, t)], lambda e, t=t, pr=pr: e.tensor_copy(
                            bass.AP(Va, t * 1040 + pr * 130, [[NT * 1040, 128], [65, 2], [1, 64]]),
                            bass.AP(pb[2], 0, [[512, 128], [64, 2], [1, 64]])))
                        E('v', [], [('vab', pr, t)], lambda e, t=t, pr=pr: e.memset(
                            bass.AP(Va, t * 1040 + pr * 130 + 64, [[NT * 1040, 128], [65, 2], [1, 1]]), 1.0))
                E('s', ['kmx'], ['kmx'], lambda e: e.activation(kmx[0:1, 0:16], kmx[0:1, 0:16], AF.Sqrt))
                p.barrier()
                for qg in (range(4) if AB_QGS[0] is None else AB_QGS[0]):
                    qb_ = qg * 512
                    for pr in range(8):
                        src = bass.AP(wq_b, pr * 128 * D, [[D, 128], [1, D]])
                        DMA('y', [], [('stg', 0)], lambda e, src=src: e.dma_start(
                            stg[:, 0:1024], src))
                        E('v', [('stg', 0)], [('wcs',)], lambda e: e.tensor_copy(
                            wcs[:], stg[:, 0:1024]))
                        mmg(PK(0), [
                            ([('wcs',)],
                             (lambda e, dc=dc, qb_=qb_: e.matmul(
                                 pb[0][:], wcs[:, dc * 128:(dc + 1) * 128],
                                 xfm[:, dc * S + qb_:dc * S + qb_ + 512],
                                 start=(dc == 0), stop=(dc == ND - 1))))
                            for dc in range(ND)])
                        E('v', [PK(0)], [('qs', pr)], lambda e, pr=pr: e.tensor_scalar(
                            Qs[:, pr * 512:(pr + 1) * 512], pb[0][:], 0.125, 1.0,
                            ALU.mult, ALU.mult))
                    for pr in (range(8) if AB_PRS[0] is None else AB_PRS[0]):
                        E('s', [('qs', pr)], [('wcs',)], lambda e, pr=pr: e.activation(
                            wcs[:, 0:512], Qs[:, pr * 512:(pr + 1) * 512], AF.Square))
                        for hh in range(2):
                            csl = sm2[0:1, hh * 512:(hh + 1) * 512]
                            E('t', [('wcs',), 'hsel'], [PK(6)], lambda e, hh=hh: e.matmul(
                                pb[6][0:1, 0:512], hsel[:, hh:hh + 1], wcs[:, 0:512],
                                start=True, stop=True))
                            E('s', [PK(6), 'sm2'], ['sm2'], lambda e, csl=csl: e.activation(
                                csl, pb[6][0:1, 0:512], AF.Sqrt))
                            col = hh * 8 + pr
                            E('v', ['sm2', 'kmx'], ['sm2'], lambda e, csl=csl, col=col:
                              e.tensor_scalar(csl, csl, kmx[0:1, col:col + 1], 1.0625,
                                              ALU.mult, ALU.mult))
                            E('s', ['sm2'], ['sm2'], lambda e, csl=csl: e.activation(
                                csl, csl, AF.Exp))
                        def emit_st(kt):
                            kp = kt % 2
                            pS0, pS1 = pb[3 + kp], pb[5 + kp]
                            E('t', [('kf', pr), ('qs', pr)], [PK(3 + kp)],
                              lambda e, kt=kt, pS0=pS0: e.matmul(
                                  pS0[:], Kf[0:64, pr * 2048 + kt * 128:pr * 2048 + (kt + 1) * 128],
                                  Qs[0:64, pr * 512:(pr + 1) * 512], start=True, stop=True,
                                  tile_position=(0, 0)))
                            E('t', [], [PK(5 + kp)],
                              lambda e, kt=kt, pS1=pS1: e.matmul(
                                  pS1[:], Kf[64:128, pr * 2048 + kt * 128:pr * 2048 + (kt + 1) * 128],
                                  Qs[64:128, pr * 512:(pr + 1) * 512], start=True, stop=True,
                                  tile_position=(64, 0)))
                            E('s', [PK(3 + kp)], [('ptb', 0, kp)],
                              lambda e, kp=kp, pS0=pS0: e.activation(
                                  ptb[:, kp * 512:(kp + 1) * 512], pS0[:], AF.Exp))
                            E('s', [PK(5 + kp)], [('ptb', 1, kp)],
                              lambda e, kp=kp, pS1=pS1: e.activation(
                                  ptb[:, (2 + kp) * 512:(3 + kp) * 512], pS1[:], AF.Exp))

                        def emit_av(kt):
                            kp = kt % 2
                            for hh in range(2):
                                psl = ptb[:, (2 * hh + kp) * 512:(2 * hh + kp + 1) * 512]
                                vsl = bass.AP(Va, kt * 1040 + pr * 130 + hh * 65,
                                              [[NT * 1040, 128], [1, 65]])
                                E('t', [('ptb', hh, kp), ('vab', pr, kt)],
                                  [PK(hh)] if kt in (0, NT - 1) else [],
                                  lambda e, psl=psl, vsl=vsl, hh=hh, kt=kt:
                                  e.matmul(pb[hh][0:65, :], vsl, psl,
                                           start=(kt == 0), stop=(kt == NT - 1)),
                                  inc=(kt == NT - 1))
                        emit_st(0)
                        for kt in range(NT):
                            if kt + 1 < NT:
                                emit_st(kt + 1)
                            emit_av(kt)
                        for hh in range(2):
                            dsl = sm2[0:1, 1024 + hh * 512:1536 + hh * 512]
                            E('v', [PK(hh), 'sm2'], ['sm2'], lambda e, hh=hh, dsl=dsl:
                              e.scalar_tensor_tensor(dsl,
                                                     sm2[0:1, hh * 512:(hh + 1) * 512],
                                                     ATT_EPS, pb[hh][64:65, :], ALU.mult, ALU.add))
                            E('v', ['sm2'], ['sm2'], lambda e, dsl=dsl: e.reciprocal(dsl, dsl))
                            E('t', ['sm2', 'ones_f'], [PK(6)], lambda e, dsl=dsl: e.matmul(
                                pb[6][0:64, :], ones_f[0:1, 0:64], dsl,
                                start=True, stop=True))
                            E('v', [PK(6)], [('rbc',)], lambda e: e.tensor_copy(
                                rbc[:], pb[6][0:64, :]))
                            E('v', [PK(hh), ('rbc',)], [('af', pr, hh)],
                              lambda e, hh=hh, pr=pr: e.tensor_tensor(
                                  af[hh * 64:(hh + 1) * 64, pr * 512:(pr + 1) * 512],
                                  pb[hh][0:64, :], rbc[:], ALU.mult))
                    for dc in (range(2) if AB_OUTP[0] else []):
                        first = [True] * 4
                        for ic in range(AB_NIC[0]):
                            DMA('y', [], [('stg', 0)], lambda e, ic=ic, dc=dc:
                                e.dma_start(stg[:, 0:512],
                                            bass.AP(wo_b, ic * 128 * D + dc * 512,
                                                    [[D, 128], [1, 512]])))
                            E('v', [('stg', 0)], [('wcs',)], lambda e: e.tensor_copy(
                                wcs[:, 0:512], stg[:, 0:512]))
                            for ts_ in range(4):
                                pki = 3 + ts_
                                E('t', [('af', ic, 0), ('af', ic, 1), ('wcs',)],
                                  [PK(pki)] if first[ts_] else [],
                                  lambda e, ic=ic, ts_=ts_, pki=pki: e.matmul(
                                      pb[pki][:],
                                      af[:, ic * 512 + ts_ * 128:ic * 512 + (ts_ + 1) * 128],
                                      wcs[:, 0:512], start=(ic == 0), stop=False), inc=(ts_ == 3))
                            if False:
                                pass
                                first[ts_] = False
                        for ts_ in range(4):
                            pki = 3 + ts_
                            E('t', ['sm2', 'ones_f'], [PK(pki)],
                              lambda e, dc=dc, pki=pki: e.matmul(
                                  pb[pki][:], ones_f[:], borow[0:1, dc * 512:(dc + 1) * 512],
                                  start=False, stop=True))
                        for ts_ in range(4):
                            pki = 3 + ts_
                            t_tile = qg * 4 + ts_
                            rap = x[:, t_tile * D + dc * 512:t_tile * D + (dc + 1) * 512]
                            if dc == 0 and ts_ == 0 and AB_DBG[0]:
                                E('v', [PK(pki), ('ptb', 0, 0)], [('dbgs',), ('ptb', 0, 0)],
                                  lambda e: e.tensor_copy(ptb[:, 0:512], pb[pki][:]))
                                DMA('g', [('dbgs',)], [], lambda e: e.dma_start(
                                    dbg[:], ptb[:, 0:512]))
                            E('v', [PK(pki), ('x', t_tile)], [('x', t_tile)],
                              lambda e, pki=pki, rap=rap: e.scalar_tensor_tensor(
                                  rap, pb[pki][:], 1.0, rap, ALU.mult, ALU.add))
                p.barrier()

        if "attn_a" in phases:
            phase_attn_a()
        if "ffnq1" in phases:
            phase_ffnq()
        if "attn_b" in phases:
            phase_attn_b()
        if "ffnq2" in phases:
            phase_ffnq()

        for t in range(NT):
            DMA('g', [('x', t)], [], lambda e, t=t: e.dma_start(
                xout.rearrange("(t p) d -> t p d", p=128)[t], x[:, t * D:(t + 1) * D]))
        for i in range(p.NDMA):
            p._wait('g', (f'dg{i}', p.cnt[f'dg{i}']))
    return nc


def _prep(inputs):
    import ml_dtypes
    g = lambda k: np.asarray(inputs[k], dtype=np.float32)
    o = {}
    rowc = lambda a: np.ascontiguousarray(
        a.reshape(8, 128, 8, 128).transpose(2, 1, 0, 3).reshape(D, D))
    o['wq_a'] = rowc(g('L0_attn_a_q_proj').transpose(1, 0, 2).reshape(D, D))
    o['wk_a'] = rowc(g('L0_attn_a_k_proj').transpose(1, 0, 2).reshape(D, D))
    o['wv_a'] = rowc(g('L0_attn_a_v_proj').transpose(1, 0, 2).reshape(D, D))
    o['qb_a'] = np.ascontiguousarray(g('L0_attn_a_q_bias').reshape(D, 1))
    o['kb_a'] = np.ascontiguousarray(g('L0_attn_a_k_bias').reshape(D, 1))
    o['vb_a'] = np.ascontiguousarray(g('L0_attn_a_v_bias').reshape(1, D))
    o['lngb'] = np.ascontiguousarray(np.broadcast_to(np.concatenate(
        [g('L0_attn_a_ln1_g'), g('L0_attn_a_ln1_b'),
         g('L0_attn_a_ln2_g'), g('L0_attn_a_ln2_b')])[None, :], (128, 4 * D)))
    o['ff1a_w'] = np.ascontiguousarray(
        g('L0_attn_a_ff1_w').reshape(D, FFA // 128, 128).transpose(1, 0, 2)
        .reshape(FFA // 128, 8, 128, 128).transpose(0, 2, 1, 3).reshape(FFA, D))
    o['ff1a_b'] = np.ascontiguousarray(g('L0_attn_a_ff1_b').reshape(FFA, 1))
    o['ff2a_w'] = np.ascontiguousarray(g('L0_attn_a_ff2_w'))
    o['ff2a_b'] = np.ascontiguousarray(g('L0_attn_a_ff2_b').reshape(1, D))
    o['w1q'] = np.ascontiguousarray(
        g('L0_ffn_w1').T.reshape(D, FFQ // 128, 128).transpose(1, 0, 2)
        .reshape(FFQ // 128, 8, 128, 128).transpose(0, 2, 1, 3).reshape(FFQ, D))
    o['b1q'] = np.ascontiguousarray(g('L0_ffn_b1').reshape(FFQ, 1))
    o['w2q'] = np.ascontiguousarray(g('L0_ffn_w2').T)
    o['b2q'] = np.ascontiguousarray(g('L0_ffn_b2').reshape(1, D))
    o['wq_b'] = rowc(np.ascontiguousarray(g('L0_attn_b_Wq').T))
    o['wk_b'] = rowc(np.ascontiguousarray(g('L0_attn_b_Wk').T))
    o['wv_b'] = rowc(np.ascontiguousarray(g('L0_attn_b_Wv').T))
    o['wo_b'] = np.ascontiguousarray(g('L0_attn_b_Wo').T)
    o['bo_b'] = np.ascontiguousarray(g('L0_attn_b_bo').reshape(1, D))
    o['identc'] = np.eye(128, dtype=ml_dtypes.bfloat16)
    km = np.arange(128)
    o['maskc'] = (km[:, None] <= km[None, :]).astype(ml_dtypes.bfloat16)
    o['onesf'] = np.ones((1, 128), dtype=np.float32)
    hs = np.zeros((128, 2), dtype=ml_dtypes.bfloat16)
    hs[0:64, 0] = 1
    hs[64:128, 1] = 1
    o['hselc'] = hs
    return o


def kernel(**inputs):
    xfull = np.asarray(inputs['L0_attn_a_x'], dtype=np.float32)
    shared = _prep(inputs)
    nc = build()
    in_maps = []
    for c in range(NCORES):
        m = dict(shared)
        m['xin'] = np.ascontiguousarray(xfull[c])
        in_maps.append(m)
    res = run_bass_kernel_spmd(nc, in_maps, core_ids=list(range(NCORES)))
    return np.stack([r['xout'] for r in res.results], axis=0)


# version 09
# speedup vs baseline: 1.1302x; 1.0098x over previous
import numpy as np
import concourse.bass as bass
import concourse.mybir as mybir
from concourse.bass_utils import run_bass_kernel_spmd
from contextlib import ExitStack

F32 = mybir.dt.float32
BF16 = mybir.dt.bfloat16
FP8 = mybir.dt.float8e4
U8 = mybir.dt.uint8
AF = mybir.ActivationFunctionType
ALU = mybir.AluOpType

NCORES = 8
S, D = 2048, 1024
NT, ND = S // 128, D // 128
FFA, FFQ = 2048, 4096
LN_EPS = 1e-5
ATT_EPS = 1e-10
QE4 = dict(thr=2.0 ** -6, ovf=248.0, c1c=1, kc=2)
QE5 = dict(thr=2.0 ** -14, ovf=65520.0, c1c=3, kc=4)
PHASES = ("attn_a", "ffnq1", "attn_b", "ffnq2")
AA_STOP = [0]
AB_QGS = [None]
AB_NIC = [8]
AB_DBG = [False]
AB_PRS = [None]
AB_OUTP = [True]


class Prog:

    def __init__(self, nc, es):
        self.nc = nc
        self.eng = {'v': nc.vector, 's': nc.scalar, 't': nc.tensor,
                    'g': nc.gpsimd, 'y': nc.sync}
        self.sem, self.cnt = {}, {}
        for e in 'vstgy':
            self.sem[e] = es.enter_context(nc.semaphore("sm_" + e))
            self.cnt[e] = 0
        self.NDMA = 8
        self.dma_idx = {}
        for e in 'gy':
            for i in range(self.NDMA):
                nm = f'd{e}{i}'
                self.sem[nm] = es.enter_context(nc.semaphore("smd_" + nm))
                self.cnt[nm] = 0
            self.dma_idx[e] = 0
        self.wr, self.rd, self.mark = {}, {}, {}
        self.pend_w = {e: [] for e in 'vstgy'}
        self.pend_r = {e: [] for e in 'vstgy'}

    def _wait(self, e, tok):
        sem_name, val = tok
        if self.mark.get((e, sem_name), 0) >= val:
            return
        self.eng[e].wait_ge(self.sem[sem_name], val)
        self.mark[(e, sem_name)] = val

    def _pre(self, e, reads, writes):
        for k in reads:
            t = self.wr.get(k)
            if t is not None:
                self._wait(e, t)
        for k in writes:
            t = self.wr.get(k)
            if t is not None:
                self._wait(e, t)
            for t2 in self.rd.get(k, ()):
                self._wait(e, t2)

    def emit(self, e, reads, writes, fn, inc=True):
        self._pre(e, reads, writes)
        instr = fn(self.eng[e])
        if inc:
            self.cnt[e] += 1
            instr.then_inc(self.sem[e], 1)
            tok = (e, self.cnt[e])
            for k in self.pend_w[e]:
                self.wr[k] = tok
                self.rd[k] = []
            for k in self.pend_r[e]:
                self.rd.setdefault(k, []).append(tok)
            self.pend_w[e], self.pend_r[e] = [], []
            for k in writes:
                self.wr[k] = tok
                self.rd[k] = []
            for k in reads:
                self.rd.setdefault(k, []).append(tok)
            return tok
        self.pend_w[e].extend(writes)
        self.pend_r[e].extend(reads)
        return None

    def dma(self, e, reads, writes, fn):
        dn = f'd{e}{self.dma_idx[e]}'
        self.dma_idx[e] = (self.dma_idx[e] + 1) % self.NDMA
        self._wait(e, (dn, self.cnt[dn]))
        self._pre(e, reads, writes)
        instr = fn(self.eng[e])
        self.cnt[dn] += 16
        instr.then_inc(self.sem[dn], 16)
        tok = (dn, self.cnt[dn])
        for k in writes:
            self.wr[k] = tok
            self.rd[k] = []
        for k in reads:
            self.rd.setdefault(k, []).append(tok)
        return tok

    def barrier(self):
        for e in 'vstgy':
            assert not self.pend_w[e] and not self.pend_r[e], e
        for e in 'vstgy':
            for n in self.sem:
                self._wait(e, (n, self.cnt[n]))
        self.wr.clear()
        self.rd.clear()


def build(phases=PHASES):
    nc = bass.Bass("TRN2", target_bir_lowering=False)
    DT = lambda n, s_, d=F32, k="ExternalInput": nc.dram_tensor(n, s_, d, kind=k)
    xin = DT("xin", [S, D])
    xout = DT("xout", [S, D], F32, "ExternalOutput")
    wq_a, wk_a, wv_a = DT("wq_a", [D, D]), DT("wk_a", [D, D]), DT("wv_a", [D, D])
    qb_a, kb_a, vb_a = DT("qb_a", [D, 1]), DT("kb_a", [D, 1]), DT("vb_a", [1, D])
    lngb = DT("lngb", [128, 4 * D])
    ff1a_w, ff1a_b = DT("ff1a_w", [FFA, D]), DT("ff1a_b", [FFA, 1])
    ff2a_w, ff2a_b = DT("ff2a_w", [FFA, D]), DT("ff2a_b", [1, D])
    w1q_w, b1q_w = DT("w1q", [FFQ, D]), DT("b1q", [FFQ, 1])
    w2q_w, b2q_w = DT("w2q", [FFQ, D]), DT("b2q", [1, D])
    wq_b, wk_b = DT("wq_b", [D, D]), DT("wk_b", [D, D])
    wv_b, wo_b = DT("wv_b", [D, D]), DT("wo_b", [D, D])
    bo_b = DT("bo_b", [1, D])
    identc = DT("identc", [128, 128], BF16)
    maskc = DT("maskc", [128, 128], BF16)
    onesf = DT("onesf", [1, 128])
    hselc = DT("hselc", [128, 2], BF16)
    wqc = nc.dram_tensor("wqc", [FFQ, D], FP8)
    dbg = nc.dram_tensor("dbg", [128, 512], BF16, kind="ExternalOutput")

    es = ExitStack()
    with es:
        sb = lambda nm, shp, dt=F32: es.enter_context(nc.sbuf_tensor(nm, shp, dt))
        psb = lambda nm, shp, dt=F32: es.enter_context(nc.psum_tensor(nm, shp, dt))
        x = sb("x", [128, NT * D])
        xfm = sb("xfm", [128, ND * S], BF16)
        stg = sb("stg", [128, 1024])
        cons = sb("cons", [128, 16])
        ident = sb("ident", [128, 128], BF16)
        mask = sb("mask", [128, 128], BF16)
        ones_f = sb("ones_f", [1, 128])
        hsel = sb("hsel", [128, 2], BF16)
        infp = sb("infp", [128, 1])
        lnb = sb("lnb", [128, 16])
        pcb = sb("pcb", [128, 64])
        ybf = sb("ybf", [128, 2048], BF16)
        pb = [psb(f"pb{i}", [128, 512]) for i in range(7)]
        pT = psb("pT", [128, 1024], BF16)

        p = Prog(nc, es)
        E, DMA = p.emit, p.dma
        _uid = [0]

        def uname(base):
            _uid[0] += 1
            return f"{base}_{_uid[0]}"
        PK = lambda i: ('ps', i)

        def mmg(bank_key, parts):
            n = len(parts)
            for i, (reads, fn) in enumerate(parts):
                E('t', reads, [bank_key] if (i == 0 or i == n - 1) else [],
                  fn, inc=(i == n - 1))

        DMA('y', [], ['ident'], lambda e: e.dma_start(ident[:], identc[:]))
        DMA('y', [], ['mask'], lambda e: e.dma_start(mask[:], maskc[:]))
        DMA('y', [], ['ones_f'], lambda e: e.dma_start(ones_f[:], onesf[:]))
        DMA('y', [], ['hsel'], lambda e: e.dma_start(hsel[:], hselc[:]))
        for c in range(8):
            DMA('y', [], ['pcb'], lambda e, c=c: e.dma_start(
                pcb[:, c:c + 1], qb_a.rearrange("(c p) o -> c p o", p=128)[c]))
            DMA('y', [], ['pcb'], lambda e, c=c: e.dma_start(
                pcb[:, 8 + c:9 + c], kb_a.rearrange("(c p) o -> c p o", p=128)[c]))
        for c in range(32):
            DMA('y', [], ['pcb'], lambda e, c=c: e.dma_start(
                pcb[:, 16 + c:17 + c], b1q_w.rearrange("(c p) o -> c p o", p=128)[c]))
        for c in range(16):
            DMA('y', [], ['pcb'], lambda e, c=c: e.dma_start(
                pcb[:, 48 + c:49 + c], ff1a_b.rearrange("(c p) o -> c p o", p=128)[c]))
        for t in range(NT):
            DMA('y', [], [('x', t)], lambda e, t=t: e.dma_start(
                x[:, t * D:(t + 1) * D], xin.rearrange("(t p) d -> t p d", p=128)[t]))
        E('v', [], ['cons'], lambda e: e.memset(cons[:, 5:6], 1.0))
        E('v', ['cons'], ['cons'], lambda e: e.memset(cons[:, 0:1], 2.0 ** 13))
        for k in range(17, 53):
            tk = float(2.0 ** k)
            E('v', ['cons'], ['cons'], lambda e, tk=tk: e.tensor_scalar(
                cons[:, 6:7], cons[:, 5:6], tk, 1.0, ALU.add, ALU.mult))
            E('v', ['cons'], ['cons'], lambda e, tk=tk: e.tensor_scalar(
                cons[:, 6:7], cons[:, 6:7], tk, 1.0, ALU.subtract, ALU.mult))
            E('v', ['cons'], ['cons'], lambda e: e.tensor_scalar(
                cons[:, 7:8], cons[:, 6:7], 1.0, 1.0, ALU.add, ALU.mult))
            E('v', ['cons'], ['cons'], lambda e: e.tensor_tensor(
                cons[:, 0:1], cons[:, 0:1], cons[:, 7:8], ALU.mult))
        E('v', ['cons'], ['cons'], lambda e: e.tensor_scalar(
            cons[:, 1:2], cons[:, 0:1], 1.0, 1.0, ALU.add, ALU.mult))
        E('v', ['cons'], ['cons'], lambda e: e.tensor_scalar(
            cons[:, 2:3], cons[:, 0:1], float(12.0 * 2.0 ** -9), 1.0, ALU.mult, ALU.mult))
        E('v', ['cons'], ['cons'], lambda e: e.tensor_scalar(
            cons[:, 3:4], cons[:, 0:1], float(2.0 ** -7), 1.0, ALU.mult, ALU.mult))
        E('v', ['cons'], ['cons'], lambda e: e.tensor_scalar(
            cons[:, 3:4], cons[:, 3:4], 1.0, 1.0, ALU.add, ALU.mult))
        E('v', ['cons'], ['cons'], lambda e: e.tensor_scalar(
            cons[:, 4:5], cons[:, 0:1], float(12.0 * 2.0 ** -24), 1.0, ALU.mult, ALU.mult))
        E('v', ['cons'], ['cons'], lambda e: e.tensor_scalar(
            cons[:, 8:9], cons[:, 2:3], -1.0, 1.0, ALU.mult, ALU.mult))
        E('v', [], ['infp'], lambda e: e.memset(infp[:], float('inf')))
        E('v', ['pcb'], ['pcb'], lambda e: e.tensor_scalar(
            pcb[:, 0:8], pcb[:, 0:8], 0.125, 1.0, ALU.mult, ALU.mult))

        def ln_tile(src_key, src_ap, eps=LN_EPS):
            L = lnb
            E('v', [src_key, 'lnb'], ['lnb'], lambda e: e.bn_stats(L[:, 0:6], src_ap[:, 0:512]))
            E('v', [src_key, 'lnb'], ['lnb'], lambda e: e.bn_stats(L[:, 6:12], src_ap[:, 512:1024]))
            ms = [L[:, c:c + 1] for c in (1, 4, 7, 10)]
            cv = [L[:, c:c + 1] for c in (2, 5, 8, 11)]
            ee = lambda f: E('v', ['lnb'], ['lnb'], f)
            ee(lambda e: e.tensor_tensor(L[:, 12:13], ms[0], ms[1], ALU.add))
            ee(lambda e: e.tensor_tensor(L[:, 13:14], ms[2], ms[3], ALU.add))
            ee(lambda e: e.tensor_tensor(L[:, 12:13], L[:, 12:13], L[:, 13:14], ALU.add))
            ee(lambda e: e.tensor_scalar(L[:, 12:13], L[:, 12:13], 0.25, 1.0, ALU.mult, ALU.mult))
            ee(lambda e: e.tensor_tensor(L[:, 14:15], cv[0], cv[1], ALU.add))
            ee(lambda e: e.tensor_tensor(L[:, 15:16], cv[2], cv[3], ALU.add))
            ee(lambda e: e.tensor_tensor(L[:, 14:15], L[:, 14:15], L[:, 15:16], ALU.add))
            ee(lambda e: e.tensor_scalar(L[:, 14:15], L[:, 14:15], float(1 / 1024.0), 1.0, ALU.mult, ALU.mult))
            for m in ms:
                ee(lambda e, m=m: e.tensor_tensor(L[:, 15:16], m, m, ALU.mult))
                ee(lambda e: e.scalar_tensor_tensor(L[:, 14:15], L[:, 15:16], 0.25,
                                                    L[:, 14:15], ALU.mult, ALU.add))
            ee(lambda e: e.tensor_tensor(L[:, 15:16], L[:, 12:13], L[:, 12:13], ALU.mult))
            ee(lambda e: e.tensor_tensor(L[:, 14:15], L[:, 14:15], L[:, 15:16], ALU.subtract))
            ee(lambda e, ep=eps: e.tensor_scalar(L[:, 14:15], L[:, 14:15], float(ep), 1.0, ALU.add, ALU.mult))
            E('s', ['lnb'], ['lnb'], lambda e: e.activation(L[:, 14:15], L[:, 14:15], AF.Sqrt))
            ee(lambda e: e.reciprocal(L[:, 14:15], L[:, 14:15]))
            ee(lambda e: e.tensor_tensor(L[:, 13:14], L[:, 12:13], L[:, 14:15], ALU.mult))
            ee(lambda e: e.tensor_scalar(L[:, 13:14], L[:, 13:14], -1.0, 1.0, ALU.mult, ALU.mult))
            return L[:, 14:15], L[:, 13:14]

        def tile_to_xfm(src_key, src_bf_ap, t):
            for dcI in range(ND):
                E('t', [src_key, 'ident'],
                  [('pT',)] if dcI in (0, ND - 1) else [],
                  lambda e, dcI=dcI: e.transpose(
                      pT[:, dcI * 128:(dcI + 1) * 128],
                      src_bf_ap[:, dcI * 128:(dcI + 1) * 128], ident[:]),
                  inc=(dcI == ND - 1))
            dst = bass.AP(xfm, t * 128, [[ND * S, 128], [S, ND], [1, 128]])
            srcp = bass.AP(pT, 0, [[1024, 128], [128, ND], [1, 128]])
            E('v', [('pT',)], [('xfm', t)], lambda e: e.tensor_copy(dst, srcp))

        def ln_to_xfm(src_of, f32_dst_of=None):
            for t in range(NT):
                sk, sap = src_of(t)
                rstd, nmu = ln_tile(sk, sap)
                yb = ybf[:, (t % 2) * 1024:(t % 2 + 1) * 1024]
                ybk = ('ybf', t % 2)
                if f32_dst_of is not None:
                    dk, dap = f32_dst_of(t)
                    E('v', [sk, 'lnb'], [dk], lambda e, sap=sap, dap=dap, r=rstd, n=nmu:
                      e.tensor_scalar(dap, sap, r, n, ALU.mult, ALU.add))
                    E('v', [dk], [ybk], lambda e, dap=dap, yb=yb: e.tensor_copy(yb, dap))
                else:
                    E('v', [sk, 'lnb'], [ybk], lambda e, sap=sap, yb=yb, r=rstd, n=nmu:
                      e.tensor_scalar(yb, sap, r, n, ALU.mult, ALU.add))
                tile_to_xfm(ybk, yb, t)
            p.barrier()

        def emit_quant(src_key, src_ap, dst_key, dst_ap, sc, sck, m1, qp,
                       signed=False, n=512, tt_engs=('g', 'g', 'g'),
                       m2=None, infw_ap=None, defer_cp=False):
            T = sck
            srcs = (src_key if isinstance(src_key, list) else [src_key])
            npart = src_ap.shape[0]
            c1 = cons[0:npart, qp['c1c']:qp['c1c'] + 1]
            K = cons[0:npart, qp['kc']:qp['kc'] + 1]
            hpath = (not signed) and qp is QE4
            e1, e2, e3 = tt_engs
            use_m2 = m2 is not None
            if signed:
                E('s', srcs + [T], [T], lambda e: e.activation(sc['f'], src_ap, AF.Abs))
                mag = sc['f']
            else:
                mag = src_ap
            E('v', srcs + [T], [T], lambda e: e.tensor_scalar(m1, mag, float(qp['thr']), None, ALU.is_ge))
            if use_m2:
                E('v', srcs + [T], [T], lambda e: e.tensor_scalar(
                    m2, mag, float(qp['ovf']), None, ALU.is_ge))
            else:
                E('v', srcs + [T], [T], lambda e: e.tensor_scalar(
                    sc['f'], mag, float(qp['ovf']), None, ALU.is_ge))
                E('v', [T], [T], lambda e: e.tensor_scalar(
                    sc['f'], sc['f'], 3.4e38, 1.0, ALU.mult, ALU.add))
            if hpath:
                negK = cons[0:npart, 8:9]
                E('s', srcs + ['cons', T], [T], lambda e: e.activation(
                    sc['y'], src_ap, AF.Relu, bias=0.0, scale=c1))
                E(e1, srcs + [T], [T], lambda e: e.tensor_tensor(sc['t'], sc['y'], src_ap, ALU.subtract))
                E(e2, [T], [T], lambda e: e.tensor_tensor(sc['y'], sc['y'], sc['t'], ALU.subtract))
                E('s', srcs + ['cons', T], [T], lambda e: e.activation(
                    sc['u'], src_ap, AF.Relu, bias=K, scale=1.0))
                E('s', ['cons', T], [dst_key, T], lambda e: e.activation(
                    dst_ap, sc['u'], AF.Relu, bias=negK, scale=1.0))
            else:
                E('v', srcs + ['cons', T], [T], lambda e: e.tensor_scalar(
                    sc['y'], src_ap, c1, 1.0, ALU.mult, ALU.mult))
                E(e1, srcs + [T], [T], lambda e: e.tensor_tensor(sc['t'], sc['y'], src_ap, ALU.subtract))
                E(e2, [T], [T], lambda e: e.tensor_tensor(sc['y'], sc['y'], sc['t'], ALU.subtract))
                E('v', srcs + ['cons', T], [T], lambda e: e.tensor_scalar(
                    sc['u'], src_ap, K, 1.0, ALU.add, ALU.mult))
                E('v', ['cons', T], [dst_key, T], lambda e: e.tensor_scalar(
                    dst_ap, sc['u'], K, 1.0, ALU.subtract, ALU.mult))

            def fin():
                if not use_m2:
                    E(e3, [T], [T], lambda e: e.tensor_tensor(sc['y'], sc['y'], sc['f'], ALU.mult))
                E('v', [dst_key, T], [dst_key] if not use_m2 else [],
                  lambda e: e.copy_predicated(dst_ap, m1, sc['y']))
                if use_m2:
                    E('v', [dst_key, T, 'infw'], [dst_key],
                      lambda e: e.copy_predicated(dst_ap, m2, infw_ap))
            if defer_cp:
                return fin
            fin()

        def phase_attn_a():
            with ExitStack() as ps_:
                A = ps_.enter_context(nc.sbuf_tensor(uname("A"), [128, NT * D], F32))
                ps_in = ps_.enter_context(ExitStack())
                aB = ps_in.enter_context(nc.sbuf_tensor(uname("aB"), [128, 10240], BF16))
                vrow = ps_in.enter_context(nc.sbuf_tensor(uname("vrow"), [1, D], F32))
                DMA('y', [], ['vrow'], lambda e: e.dma_start(vrow[:], vb_a[:]))
                ln_to_xfm(lambda t: (('x', t), x[:, t * D:(t + 1) * D]),
                          f32_dst_of=lambda t: (('A', t), A[:, t * D:(t + 1) * D]))
                Qp, Kp = aB[:, 0:2048], aB[:, 2048:4096]
                PTr = [aB[:, 4096 + i * 256:4096 + (i + 1) * 256] for i in range(4)]
                wq_t, wk_t = aB[:, 5120:6144], aB[:, 6144:7168]
                VB = 7168
                for pr in range(8):
                    for wdram, wtile, wkn, soff in ((wq_a, wq_t, 'wqp', 0),
                                                    (wk_a, wk_t, 'wkp', 0)):
                        src = bass.AP(wdram, pr * 128 * D, [[D, 128], [1, D]])
                        DMA('y', [], [('stg', soff)], lambda e, src=src, soff=soff:
                            e.dma_start(stg[:, soff:soff + 1024], src))
                        E('v', [('stg', soff)], [(wkn, pr)],
                          lambda e, soff=soff, wtile=wtile: e.tensor_copy(
                              wtile, stg[:, soff:soff + 1024]))
                    for which, (wkn, wtile, dst, boff, scale) in enumerate(
                            (('wqp', wq_t, Qp, 0, 0.125), ('wkp', wk_t, Kp, 8, 1.0))):
                        for tg in range(4):
                            mmg(PK(which), [
                                ([(wkn, pr)],
                                 (lambda e, wtile=wtile, dc=dc, tg=tg, w=which: e.matmul(
                                     pb[w][:], wtile[:, dc * 128:(dc + 1) * 128],
                                     xfm[:, dc * S + tg * 512:dc * S + (tg + 1) * 512],
                                     start=(dc == 0), stop=(dc == ND - 1))))
                                for dc in range(ND)])
                            E('v', [PK(which), 'pcb'], [('qk', pr, which)],
                              lambda e, dst=dst, tg=tg, b=boff, sc_=scale, w=which:
                              e.tensor_scalar(dst[:, tg * 512:(tg + 1) * 512], pb[w][:],
                                              sc_, pcb[:, b + pr:b + pr + 1], ALU.mult, ALU.add))
                    src = bass.AP(wv_a, pr * 128 * D, [[D, 128], [1, D]])
                    DMA('y', [], [('stg', 0)], lambda e, src=src: e.dma_start(
                        stg[:, 0:1024], src))
                    E('v', [('stg', 0), ('wqp', pr)], [('wvp', pr)],
                      lambda e: e.tensor_copy(wq_t, stg[:, 0:1024]))
                    for t in range(NT):
                        mmg(PK(2),
                            [([('wvp', pr)],
                              (lambda e, dc=dc, t=t: e.matmul(
                                  pb[2][:, 0:128],
                                  xfm[:, dc * S + t * 128:dc * S + (t + 1) * 128],
                                  wq_t[:, dc * 128:(dc + 1) * 128],
                                  start=(dc == 0), stop=False)))
                             for dc in range(ND)] +
                            [(['ones_f', 'vrow'],
                              lambda e: e.matmul(pb[2][:, 0:128], ones_f[:],
                                                 vrow[0:1, pr * 128:(pr + 1) * 128],
                                                 start=False, stop=True))])
                        E('v', [PK(2)], [('vau', pr, t)], lambda e, t=t: e.tensor_copy(
                            bass.AP(aB, VB + t * 130, [[10240, 128], [65, 2], [1, 64]]),
                            bass.AP(pb[2], 0, [[512, 128], [64, 2], [1, 64]])))
                        E('v', [], [('vau', pr, t)], lambda e, t=t: e.memset(
                            bass.AP(aB, VB + t * 130 + 64, [[10240, 128], [65, 2], [1, 1]]), 1.0))
                    for kw in range(NT):
                        qn = 256 if kw < NT - 1 else 128
                        kwp = kw % 2
                        pA, pB_ = pb[3 + kwp], pb[5 + kwp]
                        E('t', [('qk', pr, 0), ('qk', pr, 1)], [PK(3 + kwp)],
                          lambda e, kw=kw, qn=qn, pA=pA: e.matmul(
                              pA[:, 0:qn], Kp[0:64, kw * 128:(kw + 1) * 128],
                              Qp[0:64, kw * 128:kw * 128 + qn], start=True, stop=True,
                              tile_position=(0, 0)))
                        E('t', [], [PK(5 + kwp)],
                          lambda e, kw=kw, qn=qn, pB_=pB_: e.matmul(
                              pB_[:, 0:qn], Kp[64:128, kw * 128:(kw + 1) * 128],
                              Qp[64:128, kw * 128:kw * 128 + qn], start=True, stop=True,
                              tile_position=(64, 0)))
                        for hh, pp, bk in ((0, pA, PK(3 + kwp)), (1, pB_, PK(5 + kwp))):
                            rt = PTr[kwp * 2 + hh]
                            E('s', [bk], [('pt', hh, kwp)], lambda e, rt=rt, qn=qn, pp=pp:
                              e.activation(rt[:, 0:qn], pp[:, 0:qn], AF.Exp))
                            E('v', [('pt', hh, kwp), 'mask'], [('pt', hh, kwp)],
                              lambda e, rt=rt: e.tensor_tensor(
                                  rt[:, 0:128], rt[:, 0:128], mask[:], ALU.mult))
                        pav = pb[kwp]
                        parts = []
                        for hh in range(2):
                            rprev = PTr[((kw - 1) % 2) * 2 + hh]
                            rown = PTr[kwp * 2 + hh]
                            vsl = bass.AP(aB, VB + kw * 130 + hh * 65, [[10240, 128], [1, 65]])
                            vslp = bass.AP(aB, VB + (kw - 1) * 130 + hh * 65, [[10240, 128], [1, 65]])
                            if kw > 0:
                                parts.append(([('pt', hh, (kw - 1) % 2), ('vau', pr, kw - 1)],
                                              (lambda e, rprev=rprev, vslp=vslp, hh=hh:
                                               e.matmul(pav[:, hh * 65:(hh + 1) * 65],
                                                        rprev[:, 128:256], vslp,
                                                        start=True, stop=False))))
                            parts.append(([('pt', hh, kwp), ('vau', pr, kw)],
                                          (lambda e, rown=rown, vsl=vsl, hh=hh, kw=kw:
                                           e.matmul(pav[:, hh * 65:(hh + 1) * 65],
                                                    rown[:, 0:128], vsl,
                                                    start=(kw == 0), stop=True))))
                        mmg(PK(kwp), parts)
                        for hh in range(2):
                            hcol = (2 * pr + hh) * 64
                            E('v', [PK(kwp), 'lnb'], ['lnb'], lambda e, hh=hh, pav=pav:
                              e.reciprocal(lnb[:, hh:hh + 1], pav[:, hh * 65 + 64:hh * 65 + 65]))
                            E('v', [PK(kwp), 'lnb', ('A', kw)], [('A', kw)],
                              lambda e, hh=hh, kw=kw, hcol=hcol, pav=pav: e.scalar_tensor_tensor(
                                  A[:, kw * D + hcol:kw * D + hcol + 64],
                                  pav[:, hh * 65:hh * 65 + 64], lnb[:, hh:hh + 1],
                                  A[:, kw * D + hcol:kw * D + hcol + 64], ALU.mult, ALU.add))
                p.barrier()
                if AA_STOP[0] == 1:
                    for t in range(NT):
                        E('v', [('A', t)], [('x', t)], lambda e, t=t: e.tensor_copy(
                            x[:, t * D:(t + 1) * D], A[:, t * D:(t + 1) * D]))
                    p.barrier()
                    return
                ps_in.close()
                with ExitStack() as ps2:
                    def inner_ln(gbt, add_to_x):
                        for t in range(NT):
                            Aap = A[:, t * D:(t + 1) * D]
                            rstd, nmu = ln_tile(('A', t), Aap)
                            E('v', [('A', t), 'lnb'], [('A', t)], lambda e, Aap=Aap, r=rstd, n=nmu:
                              e.tensor_scalar(Aap, Aap, r, n, ALU.mult, ALU.add))
                            E('v', [('A', t), 'gbt'], [('A', t)], lambda e, Aap=Aap:
                              e.tensor_tensor(Aap, Aap, gbt[:, 0:D], ALU.mult))
                            E('v', [('A', t), 'gbt'], [('A', t)], lambda e, Aap=Aap:
                              e.tensor_tensor(Aap, Aap, gbt[:, D:2 * D], ALU.add))
                            if add_to_x:
                                E('v', [('A', t), ('x', t)], [('x', t)], lambda e, t=t, Aap=Aap:
                                  e.tensor_tensor(x[:, t * D:(t + 1) * D],
                                                  x[:, t * D:(t + 1) * D], Aap, ALU.add))
                            else:
                                yb = ybf[:, (t % 2) * 1024:(t % 2 + 1) * 1024]
                                E('v', [('A', t)], [('ybf', t % 2)], lambda e, Aap=Aap, yb=yb:
                                  e.tensor_copy(yb, Aap))
                                tile_to_xfm(('ybf', t % 2), yb, t)
                    with nc.sbuf_tensor(uname("gbt1"), [128, 2 * D], F32) as gbt:
                        DMA('y', [], ['gbt'], lambda e: e.dma_start(gbt[:], lngb[:, 0:2 * D]))
                        inner_ln(gbt, add_to_x=False)
                        p.barrier()
                    if AA_STOP[0] == 2:
                        for t in range(NT):
                            E('v', [('A', t)], [('x', t)], lambda e, t=t: e.tensor_copy(
                                x[:, t * D:(t + 1) * D], A[:, t * D:(t + 1) * D]))
                        p.barrier()
                        return
                    emit_ffn(A, lambda t: ('A', t), hid=FFA, w1d=ff1a_w, b1off=48,
                             w2d=ff2a_w, brow_dram=ff2a_b, quant=False, tokgrp=512)
                    p.barrier()
                    if AA_STOP[0] == 3:
                        for t in range(NT):
                            E('v', [('A', t)], [('x', t)], lambda e, t=t: e.tensor_copy(
                                x[:, t * D:(t + 1) * D], A[:, t * D:(t + 1) * D]))
                        p.barrier()
                        return
                    with nc.sbuf_tensor(uname("gbt2"), [128, 2 * D], F32) as gbt:
                        DMA('y', [], ['gbt'], lambda e: e.dma_start(gbt[:], lngb[:, 2 * D:4 * D]))
                        inner_ln(gbt, add_to_x=True)
                        p.barrier()

        wqc_state = {'done': False}

        def emit_ffn(rbuf, rkey, hid, w1d, b1off, w2d, brow_dram, quant,
                     tokgrp=512):
            nj = hid // 128
            nhalf = S // tokgrp
            with ExitStack() as ps_:
                fw = ybf
                fstg = ps_.enter_context(nc.sbuf_tensor(uname("fstg"), [128, 2048], F32))
                brow = ps_.enter_context(nc.sbuf_tensor(uname("fbrow"), [1, D], F32))
                if quant:
                    fh = ps_.enter_context(nc.sbuf_tensor(uname("fh"), [128, 1024], F32))
                    fsc = ps_.enter_context(nc.sbuf_tensor(uname("fsc"), [128, 3072], F32))
                    fm8 = ps_.enter_context(nc.sbuf_tensor(uname("fm8"), [128, 2048], U8))
                    hq8 = ps_.enter_context(
                        nc.sbuf_tensor(uname("fq8"), [128, nj * 512 + 1536], FP8))
                    SC = [dict(y=fsc[:, p * 1536 + 0:p * 1536 + 512],
                               t=fsc[:, p * 1536 + 512:p * 1536 + 1024],
                               u=fsc[:, p * 1536 + 512:p * 1536 + 1024],
                               f=fsc[:, p * 1536 + 1024:p * 1536 + 1536])
                          for p in range(2)]
                    M1 = [fm8[:, p * 512:(p + 1) * 512] for p in range(2)]
                    M2 = [fm8[:, 1024 + p * 512:1024 + (p + 1) * 512] for p in range(2)]
                else:
                    hB = ps_.enter_context(
                        nc.sbuf_tensor(uname("fhB"), [128, nj * 512], BF16))
                DMA('y', [], ['brow'], lambda e: e.dma_start(brow[:], brow_dram[:]))
                if quant:
                    scW = dict(y=fsc[0:1, 0:512], t=fsc[0:1, 512:1024],
                               u=fsc[0:1, 512:1024], f=fsc[0:1, 1024:1536])
                    for hf in range(2):
                        emit_quant('brow', brow[0:1, hf * 512:(hf + 1) * 512], 'brow',
                                   brow[0:1, hf * 512:(hf + 1) * 512],
                                   scW, ('fscr', 0), fm8[0:1, 0:512],
                                   QE5, signed=True, n=512)
                building = quant and not wqc_state['done']
                wq_dma = emit_wqc_chain = None
                if building:
                    wstg1 = ps_.enter_context(
                        nc.sbuf_tensor(uname("wstg1"), [128, 1024], F32))
                    wslot = [stg[:, 0:1024], wstg1[:, 0:1024]]
                    wsc3 = ps_.enter_context(
                        nc.sbuf_tensor(uname("wsc3"), [128, 6144], F32))
                    wm1k = ps_.enter_context(
                        nc.sbuf_tensor(uname("wm1k"), [128, 2048], U8))
                    wq82 = ps_.enter_context(
                        nc.sbuf_tensor(uname("wq82"), [128, 2048], FP8))
                    scQ = [dict(y=wsc3[:, p * 3072 + 0:p * 3072 + 1024],
                                t=wsc3[:, p * 3072 + 1024:p * 3072 + 2048],
                                u=wsc3[:, p * 3072 + 1024:p * 3072 + 2048],
                                f=wsc3[:, p * 3072 + 2048:p * 3072 + 3072])
                           for p in range(2)]

                    def wq_dma(j):
                        jw = j % 2
                        DMA('y' if jw == 0 else 'g', [], [('wstg', jw)],
                            lambda e, j=j, jw=jw: e.dma_start(
                                wslot[jw],
                                w2q_w.rearrange("(c p) d -> c p d", p=128)[j]))

                    def emit_wqc_chain(j, pendw):
                        jw = j % 2
                        te = ('g', 'g', 'g')
                        f0 = emit_quant(('wstg', jw), wslot[jw],
                                        ('wq8w', jw), wq82[:, jw * 1024:(jw + 1) * 1024],
                                        scQ[jw], ('wscr', jw),
                                        wm1k[:, jw * 1024:(jw + 1) * 1024],
                                        QE4, signed=True, n=1024, tt_engs=te,
                                        defer_cp=True)

                        def fin(j=j, jw=jw, f0=f0):
                            f0()
                            DMA('g' if jw == 0 else 'y', [('wq8w', jw)], [('wqc', j)],
                                lambda e, j=j, jw=jw: e.dma_start(
                                    wqc.rearrange("(c p) d -> c p d", p=128)[j],
                                    wq82[:, jw * 1024:(jw + 1) * 1024]))
                        pendw.append(fin)
                    wqc_state['done'] = True
                for half in range(nhalf):
                    hb = half * tokgrp

                    def w1_dma(j):
                        jb = j % 2
                        aw = [('fws', jb)] + ([('fwsb', 0), ('fwsb', 1)] if jb == 0 else [('fwsb', 2)])
                        DMA('y' if jb == 0 else 'g', [], aw,
                            lambda e, j=j, jb=jb: e.dma_start(
                                fstg[:, jb * 1024:(jb + 1) * 1024],
                                bass.AP(w1d, j * 128 * D, [[D, 128], [1, D]])))
                    w1_dma(0)
                    w1_dma(1)

                    def w1_cast(j):
                        jb = j % 2
                        wsl = fw[:, jb * 1024:(jb + 1) * 1024]
                        cw = [('fw', jb)] + ([('fwb', 0), ('fwb', 1)] if jb == 0 else [('fwb', 2)])
                        E('v', [('fws', jb)], cw,
                          lambda e, jb=jb, wsl=wsl: e.tensor_copy(
                              wsl, fstg[:, jb * 1024:(jb + 1) * 1024]))
                    w1_cast(0)
                    bld = building and half == 0
                    if bld:
                        wq_dma(0)
                    pend, pendw = [], []
                    for j in range(nj + 1):
                        if j < nj:
                            jb = j % 2
                            wsl = fw[:, jb * 1024:(jb + 1) * 1024]
                            if j + 1 < nj:
                                w1_cast(j + 1)
                            if j + 2 < nj:
                                w1_dma(j + 2)
                            if bld and j + 1 < nj:
                                wq_dma(j + 1)
                            mmg(PK(jb), [
                                ([('fw', jb)],
                                 (lambda e, wsl=wsl, dc=dc, jb=jb: e.matmul(
                                     pb[jb][:], wsl[:, dc * 128:(dc + 1) * 128],
                                     xfm[:, dc * S + hb:dc * S + hb + 512],
                                     start=(dc == 0), stop=(dc == ND - 1))))
                                for dc in range(ND)])
                            if quant:
                                hsl = fh[:, jb * 512:(jb + 1) * 512]
                                E('s', [PK(jb), 'pcb', ('fh', jb)], [('fh', jb)],
                                  lambda e, jb=jb, j=j, hsl=hsl: e.activation(
                                      hsl, pb[jb][:], AF.Relu,
                                      bias=pcb[:, b1off + j:b1off + j + 1], scale=1.0))
                                pend.append(emit_quant(
                                    ('fh', jb), hsl,
                                    ('h8', j), hq8[:, j * 512:(j + 1) * 512],
                                    SC[jb], ('fscr', jb), M1[jb], QE4,
                                    tt_engs=('g', 'g', 'v'), defer_cp=True))
                            else:
                                E('s', [PK(jb), 'pcb', ('h8', j)], [('h8', j)],
                                  lambda e, jb=jb, j=j: e.activation(
                                      hB[:, j * 512:(j + 1) * 512],
                                      pb[jb][:], AF.Relu,
                                      bias=pcb[:, b1off + j:b1off + j + 1], scale=1.0))
                            if bld:
                                emit_wqc_chain(j, pendw)
                        while pend and (len(pend) > 1 or j == nj):
                            pend.pop(0)()
                        while pendw and (len(pendw) > 1 or j == nj):
                            pendw.pop(0)()
                    if quant:
                        rts = [hq8[:, nj * 512 + q * 512:nj * 512 + (q + 1) * 512]
                               for q in range(3)]
                        for dc in range(2):
                            def rt_dma(jq, dc=dc):
                                q = jq % 3
                                DMA('y' if q % 2 == 0 else 'g',
                                    [('fq', q), ('wqc', jq)], [('fq', q)],
                                    lambda e, jq=jq, q=q: e.dma_start(
                                        rts[q], bass.AP(wqc, jq * 128 * D + dc * 512,
                                                        [[D, 128], [1, 512]])))
                            for q0 in range(3):
                                rt_dma(q0)
                            for j in range(nj):
                                q = j % 3
                                rt = rts[q]
                                for tt in range(4):
                                    pki = 3 + tt
                                    lap = hq8[:, j * 512 + tt * 128:j * 512 + (tt + 1) * 128]
                                    E('t', [('h8', j), ('fq', q)],
                                      [PK(pki)] if j == 0 else [],
                                      lambda e, lap=lap, rt=rt, pki=pki, j=j: e.matmul(
                                          pb[pki][:], lap, rt, start=(j == 0), stop=False),
                                      inc=(tt == 3))
                                if j + 3 < nj:
                                    rt_dma(j + 3)
                            for tt in range(4):
                                pki = 3 + tt
                                E('t', ['brow', 'ones_f'], [PK(pki)],
                                  lambda e, dc=dc, pki=pki: e.matmul(
                                      pb[pki][:], ones_f[:], brow[0:1, dc * 512:(dc + 1) * 512],
                                      start=False, stop=True))
                            efin = []
                            for tt in range(4):
                                if len(efin) >= 2:
                                    efin.pop(0)()
                                pki = 3 + tt
                                t_tile = half * (tokgrp // 128) + tt
                                par = tt % 2
                                osl = fh[:, par * 512:(par + 1) * 512]
                                rap = rbuf[:, t_tile * D + dc * 512:t_tile * D + (dc + 1) * 512]
                                E('v', [PK(pki)], [('fh', par)],
                                  lambda e, pki=pki, osl=osl: e.tensor_copy(osl, pb[pki][:]))
                                f0 = emit_quant(('fh', par), osl, ('fh', par), osl,
                                                SC[par], ('fscr', par), M1[par], QE5,
                                                signed=True, tt_engs=('g', 'g', 'v'),
                                                defer_cp=True)

                                def fin(f0=f0, osl=osl, rap=rap, t_tile=t_tile, par=par):
                                    f0()
                                    E('v', [('fh', par), rkey(t_tile)], [rkey(t_tile)],
                                      lambda e: e.scalar_tensor_tensor(
                                          rap, osl, 1.0, rap, ALU.mult, ALU.add))
                                efin.append(fin)
                            for f in efin:
                                f()
                    else:
                        for dc in range(2):
                            def w2_dma(j, dc=dc):
                                q = j % 3
                                DMA('y' if q % 2 == 0 else 'g',
                                    [('fwb', q)], [('fwsb', q), ('fws', q // 2)],
                                    lambda e, j=j, dc=dc, q=q: e.dma_start(
                                        fstg[:, q * 512:(q + 1) * 512],
                                        bass.AP(w2d, j * 128 * D + dc * 512,
                                                [[D, 128], [1, 512]])))
                            w2_dma(0)
                            w2_dma(1)
                            w2_dma(2)
                            for j in range(nj):
                                q = j % 3
                                rt = fw[:, q * 512:(q + 1) * 512]
                                E('v', [('fwsb', q)], [('fwb', q), ('fw', q // 2)],
                                  lambda e, q=q, rt=rt: e.tensor_copy(
                                      rt, fstg[:, q * 512:(q + 1) * 512]))
                                for tt in range(4):
                                    pki = 3 + tt
                                    lap = hB[:, j * 512 + tt * 128:j * 512 + (tt + 1) * 128]
                                    E('t', [('h8', j), ('fwb', q)],
                                      [PK(pki)] if j == 0 else [],
                                      lambda e, lap=lap, rt=rt, pki=pki, j=j: e.matmul(
                                          pb[pki][:], lap, rt, start=(j == 0), stop=False),
                                      inc=(tt == 3))
                                if j + 3 < nj:
                                    w2_dma(j + 3)
                            for tt in range(4):
                                pki = 3 + tt
                                E('t', ['brow', 'ones_f'], [PK(pki)],
                                  lambda e, dc=dc, pki=pki: e.matmul(
                                      pb[pki][:], ones_f[:], brow[0:1, dc * 512:(dc + 1) * 512],
                                      start=False, stop=True))
                            for tt in range(4):
                                pki = 3 + tt
                                t_tile = half * (tokgrp // 128) + tt
                                rap = rbuf[:, t_tile * D + dc * 512:t_tile * D + (dc + 1) * 512]
                                E('v', [PK(pki), rkey(t_tile)], [rkey(t_tile)],
                                  lambda e, pki=pki, rap=rap: e.scalar_tensor_tensor(
                                      rap, pb[pki][:], 1.0, rap, ALU.mult, ALU.add))
                p.barrier()

        def phase_ffnq():
            ln_to_xfm(lambda t: (('x', t), x[:, t * D:(t + 1) * D]))
            emit_ffn(x, lambda t: ('x', t), hid=FFQ, w1d=w1q_w, b1off=16,
                     w2d=w2q_w, brow_dram=b2q_w, quant=True)

        def phase_attn_b():
            with ExitStack() as ps_:
                Kf = ps_.enter_context(nc.sbuf_tensor(uname("Kf"), [128, 16384], BF16))
                Va = ps_.enter_context(nc.sbuf_tensor(uname("Va"), [128, NT * 1040], BF16))
                Qs = ps_.enter_context(nc.sbuf_tensor(uname("Qs"), [128, 4096], BF16))
                af = ps_.enter_context(nc.sbuf_tensor(uname("af"), [128, 4096], BF16))
                ptb = ps_.enter_context(nc.sbuf_tensor(uname("ptb"), [128, 2048], BF16))
                wcs = ps_.enter_context(nc.sbuf_tensor(uname("wcs"), [128, 1024], BF16))
                sm2 = ps_.enter_context(nc.sbuf_tensor(uname("sm2"), [1, 3072], F32))
                rbc = ps_.enter_context(nc.sbuf_tensor(uname("rbc"), [64, 512], F32))
                kmx = ps_.enter_context(nc.sbuf_tensor(uname("kmx"), [1, 24], F32))
                borow = sm2[0:1, 2048:3072]
                ln_to_xfm(lambda t: (('x', t), x[:, t * D:(t + 1) * D]))
                DMA('y', ['sm2'], ['sm2'], lambda e: e.dma_start(borow[:], bo_b[:]))
                for pr in range(8):
                    src = bass.AP(wk_b, pr * 128 * D, [[D, 128], [1, D]])
                    DMA('y', [], [('stg', 0)], lambda e, src=src: e.dma_start(
                        stg[:, 0:1024], src))
                    E('v', [('stg', 0)], [('wcs',)], lambda e: e.tensor_copy(
                        wcs[:], stg[:, 0:1024]))
                    for tg in range(4):
                        mmg(PK(0), [
                            ([('wcs',)],
                             (lambda e, dc=dc, tg=tg: e.matmul(
                                 pb[0][:], wcs[:, dc * 128:(dc + 1) * 128],
                                 xfm[:, dc * S + tg * 512:dc * S + (tg + 1) * 512],
                                 start=(dc == 0), stop=(dc == ND - 1))))
                            for dc in range(ND)])
                        E('v', [PK(0)], [('kf', pr)], lambda e, pr=pr, tg=tg: e.tensor_copy(
                            Kf[:, pr * 2048 + tg * 512:pr * 2048 + (tg + 1) * 512], pb[0][:]))
                    E('s', [('kf', pr)], [('ksq',)], lambda e, pr=pr: e.activation(
                        ptb[:, 0:2048], Kf[:, pr * 2048:(pr + 1) * 2048], AF.Square))
                    for hh in range(2):
                        for tg in range(4):
                            E('t', [('ksq',), 'hsel'], [PK(1)], lambda e, tg=tg, hh=hh: e.matmul(
                                pb[1][0:1, 0:512], hsel[:, hh:hh + 1],
                                ptb[:, tg * 512:(tg + 1) * 512], start=True, stop=True))
                            E('v', [PK(1), 'kmx'], ['kmx'], lambda e: e.tensor_reduce(
                                kmx[0:1, 16:17], pb[1][0:1, 0:512], mybir.AxisListType.X, ALU.max))
                            col = hh * 8 + pr
                            if tg == 0:
                                E('v', ['kmx'], ['kmx'], lambda e, col=col: e.tensor_copy(
                                    kmx[0:1, col:col + 1], kmx[0:1, 16:17]))
                            else:
                                E('v', ['kmx'], ['kmx'], lambda e, col=col: e.tensor_tensor(
                                    kmx[0:1, col:col + 1], kmx[0:1, col:col + 1], kmx[0:1, 16:17], ALU.max))
                    src = bass.AP(wv_b, pr * 128 * D, [[D, 128], [1, D]])
                    DMA('y', [], [('stg', 0)], lambda e, src=src: e.dma_start(
                        stg[:, 0:1024], src))
                    E('v', [('stg', 0)], [('wcs',)], lambda e: e.tensor_copy(
                        wcs[:], stg[:, 0:1024]))
                    for t in range(NT):
                        mmg(PK(2), [
                            ([('wcs',)],
                             (lambda e, dc=dc, t=t: e.matmul(
                                 pb[2][:, 0:128],
                                 xfm[:, dc * S + t * 128:dc * S + (t + 1) * 128],
                                 wcs[:, dc * 128:(dc + 1) * 128],
                                 start=(dc == 0), stop=(dc == ND - 1))))
                            for dc in range(ND)])
                        E('v', [PK(2)], [('vab', pr, t)], lambda e, t=t, pr=pr: e.tensor_copy(
                            bass.AP(Va, t * 1040 + pr * 130, [[NT * 1040, 128], [65, 2], [1, 64]]),
                            bass.AP(pb[2], 0, [[512, 128], [64, 2], [1, 64]])))
                        E('v', [], [('vab', pr, t)], lambda e, t=t, pr=pr: e.memset(
                            bass.AP(Va, t * 1040 + pr * 130 + 64, [[NT * 1040, 128], [65, 2], [1, 1]]), 1.0))
                E('s', ['kmx'], ['kmx'], lambda e: e.activation(kmx[0:1, 0:16], kmx[0:1, 0:16], AF.Sqrt))
                p.barrier()
                for qg in (range(4) if AB_QGS[0] is None else AB_QGS[0]):
                    qb_ = qg * 512
                    for pr in range(8):
                        src = bass.AP(wq_b, pr * 128 * D, [[D, 128], [1, D]])
                        DMA('y', [], [('stg', 0)], lambda e, src=src: e.dma_start(
                            stg[:, 0:1024], src))
                        E('v', [('stg', 0)], [('wcs',)], lambda e: e.tensor_copy(
                            wcs[:], stg[:, 0:1024]))
                        mmg(PK(0), [
                            ([('wcs',)],
                             (lambda e, dc=dc, qb_=qb_: e.matmul(
                                 pb[0][:], wcs[:, dc * 128:(dc + 1) * 128],
                                 xfm[:, dc * S + qb_:dc * S + qb_ + 512],
                                 start=(dc == 0), stop=(dc == ND - 1))))
                            for dc in range(ND)])
                        E('v', [PK(0)], [('qs', pr)], lambda e, pr=pr: e.tensor_scalar(
                            Qs[:, pr * 512:(pr + 1) * 512], pb[0][:], 0.125, 1.0,
                            ALU.mult, ALU.mult))
                    for pr in (range(8) if AB_PRS[0] is None else AB_PRS[0]):
                        E('s', [('qs', pr)], [('wcs',)], lambda e, pr=pr: e.activation(
                            wcs[:, 0:512], Qs[:, pr * 512:(pr + 1) * 512], AF.Square))
                        for hh in range(2):
                            csl = sm2[0:1, hh * 512:(hh + 1) * 512]
                            E('t', [('wcs',), 'hsel'], [PK(6)], lambda e, hh=hh: e.matmul(
                                pb[6][0:1, 0:512], hsel[:, hh:hh + 1], wcs[:, 0:512],
                                start=True, stop=True))
                            E('s', [PK(6), 'sm2'], ['sm2'], lambda e, csl=csl: e.activation(
                                csl, pb[6][0:1, 0:512], AF.Sqrt))
                            col = hh * 8 + pr
                            E('v', ['sm2', 'kmx'], ['sm2'], lambda e, csl=csl, col=col:
                              e.tensor_scalar(csl, csl, kmx[0:1, col:col + 1], 1.0625,
                                              ALU.mult, ALU.mult))
                            E('s', ['sm2'], ['sm2'], lambda e, csl=csl: e.activation(
                                csl, csl, AF.Exp))
                        def emit_st(kt):
                            kp = kt % 2
                            pS0, pS1 = pb[3 + kp], pb[5 + kp]
                            E('t', [('kf', pr), ('qs', pr)], [PK(3 + kp)],
                              lambda e, kt=kt, pS0=pS0: e.matmul(
                                  pS0[:], Kf[0:64, pr * 2048 + kt * 128:pr * 2048 + (kt + 1) * 128],
                                  Qs[0:64, pr * 512:(pr + 1) * 512], start=True, stop=True,
                                  tile_position=(0, 0)))
                            E('t', [], [PK(5 + kp)],
                              lambda e, kt=kt, pS1=pS1: e.matmul(
                                  pS1[:], Kf[64:128, pr * 2048 + kt * 128:pr * 2048 + (kt + 1) * 128],
                                  Qs[64:128, pr * 512:(pr + 1) * 512], start=True, stop=True,
                                  tile_position=(64, 0)))
                            E('s', [PK(3 + kp)], [('ptb', 0, kp)],
                              lambda e, kp=kp, pS0=pS0: e.activation(
                                  ptb[:, kp * 512:(kp + 1) * 512], pS0[:], AF.Exp))
                            E('s', [PK(5 + kp)], [('ptb', 1, kp)],
                              lambda e, kp=kp, pS1=pS1: e.activation(
                                  ptb[:, (2 + kp) * 512:(3 + kp) * 512], pS1[:], AF.Exp))

                        def emit_av(kt):
                            kp = kt % 2
                            for hh in range(2):
                                psl = ptb[:, (2 * hh + kp) * 512:(2 * hh + kp + 1) * 512]
                                vsl = bass.AP(Va, kt * 1040 + pr * 130 + hh * 65,
                                              [[NT * 1040, 128], [1, 65]])
                                E('t', [('ptb', hh, kp), ('vab', pr, kt)],
                                  [PK(hh)] if kt in (0, NT - 1) else [],
                                  lambda e, psl=psl, vsl=vsl, hh=hh, kt=kt:
                                  e.matmul(pb[hh][0:65, :], vsl, psl,
                                           start=(kt == 0), stop=(kt == NT - 1)),
                                  inc=(kt == NT - 1))
                        emit_st(0)
                        for kt in range(NT):
                            if kt + 1 < NT:
                                emit_st(kt + 1)
                            emit_av(kt)
                        for hh in range(2):
                            dsl = sm2[0:1, 1024 + hh * 512:1536 + hh * 512]
                            E('v', [PK(hh), 'sm2'], ['sm2'], lambda e, hh=hh, dsl=dsl:
                              e.scalar_tensor_tensor(dsl,
                                                     sm2[0:1, hh * 512:(hh + 1) * 512],
                                                     ATT_EPS, pb[hh][64:65, :], ALU.mult, ALU.add))
                            E('v', ['sm2'], ['sm2'], lambda e, dsl=dsl: e.reciprocal(dsl, dsl))
                            E('t', ['sm2', 'ones_f'], [PK(6)], lambda e, dsl=dsl: e.matmul(
                                pb[6][0:64, :], ones_f[0:1, 0:64], dsl,
                                start=True, stop=True))
                            E('v', [PK(6)], [('rbc',)], lambda e: e.tensor_copy(
                                rbc[:], pb[6][0:64, :]))
                            E('v', [PK(hh), ('rbc',)], [('af', pr, hh)],
                              lambda e, hh=hh, pr=pr: e.tensor_tensor(
                                  af[hh * 64:(hh + 1) * 64, pr * 512:(pr + 1) * 512],
                                  pb[hh][0:64, :], rbc[:], ALU.mult))
                    for dc in (range(2) if AB_OUTP[0] else []):
                        first = [True] * 4
                        for ic in range(AB_NIC[0]):
                            DMA('y', [], [('stg', 0)], lambda e, ic=ic, dc=dc:
                                e.dma_start(stg[:, 0:512],
                                            bass.AP(wo_b, ic * 128 * D + dc * 512,
                                                    [[D, 128], [1, 512]])))
                            E('v', [('stg', 0)], [('wcs',)], lambda e: e.tensor_copy(
                                wcs[:, 0:512], stg[:, 0:512]))
                            for ts_ in range(4):
                                pki = 3 + ts_
                                E('t', [('af', ic, 0), ('af', ic, 1), ('wcs',)],
                                  [PK(pki)] if first[ts_] else [],
                                  lambda e, ic=ic, ts_=ts_, pki=pki: e.matmul(
                                      pb[pki][:],
                                      af[:, ic * 512 + ts_ * 128:ic * 512 + (ts_ + 1) * 128],
                                      wcs[:, 0:512], start=(ic == 0), stop=False), inc=(ts_ == 3))
                            if False:
                                pass
                                first[ts_] = False
                        for ts_ in range(4):
                            pki = 3 + ts_
                            E('t', ['sm2', 'ones_f'], [PK(pki)],
                              lambda e, dc=dc, pki=pki: e.matmul(
                                  pb[pki][:], ones_f[:], borow[0:1, dc * 512:(dc + 1) * 512],
                                  start=False, stop=True))
                        for ts_ in range(4):
                            pki = 3 + ts_
                            t_tile = qg * 4 + ts_
                            rap = x[:, t_tile * D + dc * 512:t_tile * D + (dc + 1) * 512]
                            if dc == 0 and ts_ == 0 and AB_DBG[0]:
                                E('v', [PK(pki), ('ptb', 0, 0)], [('dbgs',), ('ptb', 0, 0)],
                                  lambda e: e.tensor_copy(ptb[:, 0:512], pb[pki][:]))
                                DMA('g', [('dbgs',)], [], lambda e: e.dma_start(
                                    dbg[:], ptb[:, 0:512]))
                            E('v', [PK(pki), ('x', t_tile)], [('x', t_tile)],
                              lambda e, pki=pki, rap=rap: e.scalar_tensor_tensor(
                                  rap, pb[pki][:], 1.0, rap, ALU.mult, ALU.add))
                p.barrier()

        if "attn_a" in phases:
            phase_attn_a()
        if "ffnq1" in phases:
            phase_ffnq()
        if "attn_b" in phases:
            phase_attn_b()
        if "ffnq2" in phases:
            phase_ffnq()

        for t in range(NT):
            DMA('g', [('x', t)], [], lambda e, t=t: e.dma_start(
                xout.rearrange("(t p) d -> t p d", p=128)[t], x[:, t * D:(t + 1) * D]))
        for i in range(p.NDMA):
            p._wait('g', (f'dg{i}', p.cnt[f'dg{i}']))
    return nc


def _prep(inputs):
    import ml_dtypes
    g = lambda k: np.asarray(inputs[k], dtype=np.float32)
    o = {}
    rowc = lambda a: np.ascontiguousarray(
        a.reshape(8, 128, 8, 128).transpose(2, 1, 0, 3).reshape(D, D))
    o['wq_a'] = rowc(g('L0_attn_a_q_proj').transpose(1, 0, 2).reshape(D, D))
    o['wk_a'] = rowc(g('L0_attn_a_k_proj').transpose(1, 0, 2).reshape(D, D))
    o['wv_a'] = rowc(g('L0_attn_a_v_proj').transpose(1, 0, 2).reshape(D, D))
    o['qb_a'] = np.ascontiguousarray(g('L0_attn_a_q_bias').reshape(D, 1))
    o['kb_a'] = np.ascontiguousarray(g('L0_attn_a_k_bias').reshape(D, 1))
    o['vb_a'] = np.ascontiguousarray(g('L0_attn_a_v_bias').reshape(1, D))
    o['lngb'] = np.ascontiguousarray(np.broadcast_to(np.concatenate(
        [g('L0_attn_a_ln1_g'), g('L0_attn_a_ln1_b'),
         g('L0_attn_a_ln2_g'), g('L0_attn_a_ln2_b')])[None, :], (128, 4 * D)))
    o['ff1a_w'] = np.ascontiguousarray(
        g('L0_attn_a_ff1_w').reshape(D, FFA // 128, 128).transpose(1, 0, 2)
        .reshape(FFA // 128, 8, 128, 128).transpose(0, 2, 1, 3).reshape(FFA, D))
    o['ff1a_b'] = np.ascontiguousarray(g('L0_attn_a_ff1_b').reshape(FFA, 1))
    o['ff2a_w'] = np.ascontiguousarray(g('L0_attn_a_ff2_w'))
    o['ff2a_b'] = np.ascontiguousarray(g('L0_attn_a_ff2_b').reshape(1, D))
    o['w1q'] = np.ascontiguousarray(
        g('L0_ffn_w1').T.reshape(D, FFQ // 128, 128).transpose(1, 0, 2)
        .reshape(FFQ // 128, 8, 128, 128).transpose(0, 2, 1, 3).reshape(FFQ, D))
    o['b1q'] = np.ascontiguousarray(g('L0_ffn_b1').reshape(FFQ, 1))
    o['w2q'] = np.ascontiguousarray(g('L0_ffn_w2').T)
    o['b2q'] = np.ascontiguousarray(g('L0_ffn_b2').reshape(1, D))
    o['wq_b'] = rowc(np.ascontiguousarray(g('L0_attn_b_Wq').T))
    o['wk_b'] = rowc(np.ascontiguousarray(g('L0_attn_b_Wk').T))
    o['wv_b'] = rowc(np.ascontiguousarray(g('L0_attn_b_Wv').T))
    o['wo_b'] = np.ascontiguousarray(g('L0_attn_b_Wo').T)
    o['bo_b'] = np.ascontiguousarray(g('L0_attn_b_bo').reshape(1, D))
    o['identc'] = np.eye(128, dtype=ml_dtypes.bfloat16)
    km = np.arange(128)
    o['maskc'] = (km[:, None] <= km[None, :]).astype(ml_dtypes.bfloat16)
    o['onesf'] = np.ones((1, 128), dtype=np.float32)
    hs = np.zeros((128, 2), dtype=ml_dtypes.bfloat16)
    hs[0:64, 0] = 1
    hs[64:128, 1] = 1
    o['hselc'] = hs
    return o


def kernel(**inputs):
    xfull = np.asarray(inputs['L0_attn_a_x'], dtype=np.float32)
    shared = _prep(inputs)
    nc = build()
    in_maps = []
    for c in range(NCORES):
        m = dict(shared)
        m['xin'] = np.ascontiguousarray(xfull[c])
        in_maps.append(m)
    res = run_bass_kernel_spmd(nc, in_maps, core_ids=list(range(NCORES)))
    return np.stack([r['xout'] for r in res.results], axis=0)


# version 11
# speedup vs baseline: 1.2786x; 1.1313x over previous
import numpy as np
import concourse.bass as bass
import concourse.mybir as mybir
from concourse.bass_utils import run_bass_kernel_spmd
from contextlib import ExitStack

F32 = mybir.dt.float32
BF16 = mybir.dt.bfloat16
FP8 = mybir.dt.float8e4
U8 = mybir.dt.uint8
AF = mybir.ActivationFunctionType
ALU = mybir.AluOpType

NCORES = 8
S, D = 2048, 1024
NT, ND = S // 128, D // 128
FFA, FFQ = 2048, 4096
LN_EPS = 1e-5
ATT_EPS = 1e-10
QE4 = dict(thr=2.0 ** -6, ovf=248.0, c1c=1, kc=2)
QE5 = dict(thr=2.0 ** -14, ovf=65520.0, c1c=3, kc=4)
PHASES = ("attn_a", "ffnq1", "attn_b", "ffnq2")
AA_STOP = [0]
AB_QGS = [None]
AB_NIC = [8]
AB_DBG = [False]
AB_PRS = [None]
AB_OUTP = [True]


class Prog:

    def __init__(self, nc, es):
        self.nc = nc
        self.eng = {'v': nc.vector, 's': nc.scalar, 't': nc.tensor,
                    'g': nc.gpsimd, 'y': nc.sync}
        self.sem, self.cnt = {}, {}
        for e in 'vstgy':
            self.sem[e] = es.enter_context(nc.semaphore("sm_" + e))
            self.cnt[e] = 0
        self.NDMA = 8
        self.dma_idx = {}
        for e in 'gy':
            for i in range(self.NDMA):
                nm = f'd{e}{i}'
                self.sem[nm] = es.enter_context(nc.semaphore("smd_" + nm))
                self.cnt[nm] = 0
            self.dma_idx[e] = 0
        self.wr, self.rd, self.mark = {}, {}, {}
        self.pend_w = {e: [] for e in 'vstgy'}
        self.pend_r = {e: [] for e in 'vstgy'}

    def _wait(self, e, tok):
        sem_name, val = tok
        if self.mark.get((e, sem_name), 0) >= val:
            return
        self.eng[e].wait_ge(self.sem[sem_name], val)
        self.mark[(e, sem_name)] = val

    def _pre(self, e, reads, writes):
        for k in reads:
            t = self.wr.get(k)
            if t is not None:
                self._wait(e, t)
        for k in writes:
            t = self.wr.get(k)
            if t is not None:
                self._wait(e, t)
            for t2 in self.rd.get(k, ()):
                self._wait(e, t2)

    def emit(self, e, reads, writes, fn, inc=True):
        self._pre(e, reads, writes)
        instr = fn(self.eng[e])
        if inc:
            self.cnt[e] += 1
            instr.then_inc(self.sem[e], 1)
            tok = (e, self.cnt[e])
            for k in self.pend_w[e]:
                self.wr[k] = tok
                self.rd[k] = []
            for k in self.pend_r[e]:
                self.rd.setdefault(k, []).append(tok)
            self.pend_w[e], self.pend_r[e] = [], []
            for k in writes:
                self.wr[k] = tok
                self.rd[k] = []
            for k in reads:
                self.rd.setdefault(k, []).append(tok)
            return tok
        self.pend_w[e].extend(writes)
        self.pend_r[e].extend(reads)
        return None

    def dma(self, e, reads, writes, fn):
        dn = f'd{e}{self.dma_idx[e]}'
        self.dma_idx[e] = (self.dma_idx[e] + 1) % self.NDMA
        self._wait(e, (dn, self.cnt[dn]))
        self._pre(e, reads, writes)
        instr = fn(self.eng[e])
        self.cnt[dn] += 16
        instr.then_inc(self.sem[dn], 16)
        tok = (dn, self.cnt[dn])
        for k in writes:
            self.wr[k] = tok
            self.rd[k] = []
        for k in reads:
            self.rd.setdefault(k, []).append(tok)
        return tok

    def barrier(self):
        for e in 'vstgy':
            assert not self.pend_w[e] and not self.pend_r[e], e
        for e in 'vstgy':
            for n in self.sem:
                self._wait(e, (n, self.cnt[n]))
        self.wr.clear()
        self.rd.clear()


def build(phases=PHASES):
    nc = bass.Bass("TRN2", target_bir_lowering=False)
    DT = lambda n, s_, d=F32, k="ExternalInput": nc.dram_tensor(n, s_, d, kind=k)
    xin = DT("xin", [S, D])
    xout = DT("xout", [S, D], F32, "ExternalOutput")
    wq_a, wk_a, wv_a = DT("wq_a", [D, D]), DT("wk_a", [D, D]), DT("wv_a", [D, D])
    qb_a, kb_a, vb_a = DT("qb_a", [D, 1]), DT("kb_a", [D, 1]), DT("vb_a", [1, D])
    lngb = DT("lngb", [128, 4 * D])
    ff1a_w, ff1a_b = DT("ff1a_w", [FFA, D]), DT("ff1a_b", [FFA, 1])
    ff2a_w, ff2a_b = DT("ff2a_w", [FFA, D]), DT("ff2a_b", [1, D])
    w1q_w, b1q_w = DT("w1q", [FFQ, D]), DT("b1q", [FFQ, 1])
    w2q_w, b2q_w = DT("w2q", [FFQ, D]), DT("b2q", [1, D])
    wq_b, wk_b = DT("wq_b", [D, D]), DT("wk_b", [D, D])
    wv_b, wo_b = DT("wv_b", [D, D]), DT("wo_b", [D, D])
    bo_b = DT("bo_b", [1, D])
    identc = DT("identc", [128, 128], BF16)
    maskc = DT("maskc", [128, 128], BF16)
    onesf = DT("onesf", [1, 128])
    hselc = DT("hselc", [128, 2], BF16)
    wqc = nc.dram_tensor("wqc", [FFQ, D], FP8)
    dbg = nc.dram_tensor("dbg", [128, 512], BF16, kind="ExternalOutput")

    es = ExitStack()
    with es:
        sb = lambda nm, shp, dt=F32: es.enter_context(nc.sbuf_tensor(nm, shp, dt))
        psb = lambda nm, shp, dt=F32: es.enter_context(nc.psum_tensor(nm, shp, dt))
        x = sb("x", [128, NT * D])
        xfm = sb("xfm", [128, ND * S], BF16)
        stg = sb("stg", [128, 1024])
        cons = sb("cons", [128, 16])
        ident = sb("ident", [128, 128], BF16)
        mask = sb("mask", [128, 128], BF16)
        ones_f = sb("ones_f", [1, 128])
        hsel = sb("hsel", [128, 2], BF16)
        infp = sb("infp", [128, 1])
        lnb = sb("lnb", [128, 16])
        lnS = sb("lnS", [128, 192])
        lnT = sb("lnT", [128, 96])
        lnR = sb("lnR", [128, 32])
        pcb = sb("pcb", [128, 64])
        ybf = sb("ybf", [128, 2048], BF16)
        pb = [psb(f"pb{i}", [128, 512]) for i in range(7)]
        pT = psb("pT", [128, 1024], BF16)

        p = Prog(nc, es)
        E, DMA = p.emit, p.dma
        _uid = [0]

        def uname(base):
            _uid[0] += 1
            return f"{base}_{_uid[0]}"
        PK = lambda i: ('ps', i)

        def mmg(bank_key, parts):
            n = len(parts)
            for i, (reads, fn) in enumerate(parts):
                E('t', reads, [bank_key] if (i == 0 or i == n - 1) else [],
                  fn, inc=(i == n - 1))

        DMA('y', [], ['ident'], lambda e: e.dma_start(ident[:], identc[:]))
        DMA('y', [], ['mask'], lambda e: e.dma_start(mask[:], maskc[:]))
        DMA('y', [], ['ones_f'], lambda e: e.dma_start(ones_f[:], onesf[:]))
        DMA('y', [], ['hsel'], lambda e: e.dma_start(hsel[:], hselc[:]))
        for c in range(8):
            DMA('y', [], ['pcb'], lambda e, c=c: e.dma_start(
                pcb[:, c:c + 1], qb_a.rearrange("(c p) o -> c p o", p=128)[c]))
            DMA('y', [], ['pcb'], lambda e, c=c: e.dma_start(
                pcb[:, 8 + c:9 + c], kb_a.rearrange("(c p) o -> c p o", p=128)[c]))
        for c in range(32):
            DMA('y', [], ['pcb'], lambda e, c=c: e.dma_start(
                pcb[:, 16 + c:17 + c], b1q_w.rearrange("(c p) o -> c p o", p=128)[c]))
        for c in range(16):
            DMA('y', [], ['pcb'], lambda e, c=c: e.dma_start(
                pcb[:, 48 + c:49 + c], ff1a_b.rearrange("(c p) o -> c p o", p=128)[c]))
        for t in range(NT):
            DMA('y', [], [('x', t)], lambda e, t=t: e.dma_start(
                x[:, t * D:(t + 1) * D], xin.rearrange("(t p) d -> t p d", p=128)[t]))
        E('v', [], ['cons'], lambda e: e.memset(cons[:, 5:6], 1.0))
        E('v', ['cons'], ['cons'], lambda e: e.memset(cons[:, 0:1], 2.0 ** 13))
        for k in range(17, 53):
            tk = float(2.0 ** k)
            E('v', ['cons'], ['cons'], lambda e, tk=tk: e.tensor_scalar(
                cons[:, 6:7], cons[:, 5:6], tk, 1.0, ALU.add, ALU.mult))
            E('v', ['cons'], ['cons'], lambda e, tk=tk: e.tensor_scalar(
                cons[:, 6:7], cons[:, 6:7], tk, 1.0, ALU.subtract, ALU.mult))
            E('v', ['cons'], ['cons'], lambda e: e.tensor_scalar(
                cons[:, 7:8], cons[:, 6:7], 1.0, 1.0, ALU.add, ALU.mult))
            E('v', ['cons'], ['cons'], lambda e: e.tensor_tensor(
                cons[:, 0:1], cons[:, 0:1], cons[:, 7:8], ALU.mult))
        E('v', ['cons'], ['cons'], lambda e: e.tensor_scalar(
            cons[:, 1:2], cons[:, 0:1], 1.0, 1.0, ALU.add, ALU.mult))
        E('v', ['cons'], ['cons'], lambda e: e.tensor_scalar(
            cons[:, 2:3], cons[:, 0:1], float(12.0 * 2.0 ** -9), 1.0, ALU.mult, ALU.mult))
        E('v', ['cons'], ['cons'], lambda e: e.tensor_scalar(
            cons[:, 3:4], cons[:, 0:1], float(2.0 ** -7), 1.0, ALU.mult, ALU.mult))
        E('v', ['cons'], ['cons'], lambda e: e.tensor_scalar(
            cons[:, 3:4], cons[:, 3:4], 1.0, 1.0, ALU.add, ALU.mult))
        E('v', ['cons'], ['cons'], lambda e: e.tensor_scalar(
            cons[:, 4:5], cons[:, 0:1], float(12.0 * 2.0 ** -24), 1.0, ALU.mult, ALU.mult))
        E('v', ['cons'], ['cons'], lambda e: e.tensor_scalar(
            cons[:, 8:9], cons[:, 2:3], -1.0, 1.0, ALU.mult, ALU.mult))
        E('v', [], ['infp'], lambda e: e.memset(infp[:], float('inf')))
        E('v', ['pcb'], ['pcb'], lambda e: e.tensor_scalar(
            pcb[:, 0:8], pcb[:, 0:8], 0.125, 1.0, ALU.mult, ALU.mult))

        def ln_tile(src_key, src_ap, eps=LN_EPS):
            L = lnb
            E('v', [src_key, 'lnb'], ['lnb'], lambda e: e.bn_stats(L[:, 0:6], src_ap[:, 0:512]))
            E('v', [src_key, 'lnb'], ['lnb'], lambda e: e.bn_stats(L[:, 6:12], src_ap[:, 512:1024]))
            ms = [L[:, c:c + 1] for c in (1, 4, 7, 10)]
            cv = [L[:, c:c + 1] for c in (2, 5, 8, 11)]
            ee = lambda f: E('v', ['lnb'], ['lnb'], f)
            ee(lambda e: e.tensor_tensor(L[:, 12:13], ms[0], ms[1], ALU.add))
            ee(lambda e: e.tensor_tensor(L[:, 13:14], ms[2], ms[3], ALU.add))
            ee(lambda e: e.tensor_tensor(L[:, 12:13], L[:, 12:13], L[:, 13:14], ALU.add))
            ee(lambda e: e.tensor_scalar(L[:, 12:13], L[:, 12:13], 0.25, 1.0, ALU.mult, ALU.mult))
            ee(lambda e: e.tensor_tensor(L[:, 14:15], cv[0], cv[1], ALU.add))
            ee(lambda e: e.tensor_tensor(L[:, 15:16], cv[2], cv[3], ALU.add))
            ee(lambda e: e.tensor_tensor(L[:, 14:15], L[:, 14:15], L[:, 15:16], ALU.add))
            ee(lambda e: e.tensor_scalar(L[:, 14:15], L[:, 14:15], float(1 / 1024.0), 1.0, ALU.mult, ALU.mult))
            for m in ms:
                ee(lambda e, m=m: e.tensor_tensor(L[:, 15:16], m, m, ALU.mult))
                ee(lambda e: e.scalar_tensor_tensor(L[:, 14:15], L[:, 15:16], 0.25,
                                                    L[:, 14:15], ALU.mult, ALU.add))
            ee(lambda e: e.tensor_tensor(L[:, 15:16], L[:, 12:13], L[:, 12:13], ALU.mult))
            ee(lambda e: e.tensor_tensor(L[:, 14:15], L[:, 14:15], L[:, 15:16], ALU.subtract))
            ee(lambda e, ep=eps: e.tensor_scalar(L[:, 14:15], L[:, 14:15], float(ep), 1.0, ALU.add, ALU.mult))
            E('s', ['lnb'], ['lnb'], lambda e: e.activation(L[:, 14:15], L[:, 14:15], AF.Sqrt))
            ee(lambda e: e.reciprocal(L[:, 14:15], L[:, 14:15]))
            ee(lambda e: e.tensor_tensor(L[:, 13:14], L[:, 12:13], L[:, 14:15], ALU.mult))
            ee(lambda e: e.tensor_scalar(L[:, 13:14], L[:, 13:14], -1.0, 1.0, ALU.mult, ALU.mult))
            return L[:, 14:15], L[:, 13:14]

        def ln_stats_all(src_of, eps=LN_EPS):
            for t in range(NT):
                sk, sap = src_of(t)
                E('v', [sk, 'lnS'], ['lnS'], lambda e, t=t, sap=sap: e.bn_stats(
                    lnS[:, t * 12:t * 12 + 6], sap[:, 0:512]))
                E('v', [sk, 'lnS'], ['lnS'], lambda e, t=t, sap=sap: e.bn_stats(
                    lnS[:, t * 12 + 6:t * 12 + 12], sap[:, 512:1024]))
            g = lambda off: bass.AP(lnS, off, [[192, 128], [12, NT]])
            sc_ = lambda i: lnT[:, i * 16:(i + 1) * 16]
            ee = lambda f: E('v', ['lnS', 'lnT'], ['lnT'], f)
            ms = [g(1), g(4), g(7), g(10)]
            cv = [g(2), g(5), g(8), g(11)]
            ee(lambda e: e.tensor_tensor(sc_(0), ms[0], ms[1], ALU.add))
            ee(lambda e: e.tensor_tensor(sc_(1), ms[2], ms[3], ALU.add))
            ee(lambda e: e.tensor_tensor(sc_(0), sc_(0), sc_(1), ALU.add))
            ee(lambda e: e.tensor_scalar(sc_(0), sc_(0), 0.25, 1.0, ALU.mult, ALU.mult))
            ee(lambda e: e.tensor_tensor(sc_(2), cv[0], cv[1], ALU.add))
            ee(lambda e: e.tensor_tensor(sc_(3), cv[2], cv[3], ALU.add))
            ee(lambda e: e.tensor_tensor(sc_(2), sc_(2), sc_(3), ALU.add))
            ee(lambda e: e.tensor_scalar(sc_(2), sc_(2), float(1 / 1024.0), 1.0, ALU.mult, ALU.mult))
            for m in ms:
                ee(lambda e, m=m: e.tensor_tensor(sc_(3), m, m, ALU.mult))
                ee(lambda e: e.scalar_tensor_tensor(sc_(2), sc_(3), 0.25,
                                                    sc_(2), ALU.mult, ALU.add))
            ee(lambda e: e.tensor_tensor(sc_(3), sc_(0), sc_(0), ALU.mult))
            ee(lambda e: e.tensor_tensor(sc_(2), sc_(2), sc_(3), ALU.subtract))
            ee(lambda e, ep=eps: e.tensor_scalar(sc_(2), sc_(2), float(ep), 1.0, ALU.add, ALU.mult))
            E('s', ['lnT'], ['lnT'], lambda e: e.activation(sc_(2), sc_(2), AF.Sqrt))
            E('v', ['lnT', 'lnR'], ['lnR'], lambda e: e.reciprocal(lnR[:, 0:16], sc_(2)))
            E('v', ['lnT', 'lnR'], ['lnR'], lambda e: e.tensor_tensor(
                lnR[:, 16:32], sc_(0), lnR[:, 0:16], ALU.mult))
            E('v', ['lnR'], ['lnR'], lambda e: e.tensor_scalar(
                lnR[:, 16:32], lnR[:, 16:32], -1.0, 1.0, ALU.mult, ALU.mult))

        def tile_to_xfm(src_key, src_bf_ap, t):
            for dcI in range(ND):
                E('t', [src_key, 'ident'],
                  [('pT',)] if dcI in (0, ND - 1) else [],
                  lambda e, dcI=dcI: e.transpose(
                      pT[:, dcI * 128:(dcI + 1) * 128],
                      src_bf_ap[:, dcI * 128:(dcI + 1) * 128], ident[:]),
                  inc=(dcI == ND - 1))
            dst = bass.AP(xfm, t * 128, [[ND * S, 128], [S, ND], [1, 128]])
            srcp = bass.AP(pT, 0, [[1024, 128], [128, ND], [1, 128]])
            E('v', [('pT',)], [('xfm', t)], lambda e: e.tensor_copy(dst, srcp))

        def ln_to_xfm(src_of, f32_dst_of=None):
            ln_stats_all(src_of)
            for t in range(NT):
                sk, sap = src_of(t)
                rstd, nmu = lnR[:, t:t + 1], lnR[:, 16 + t:17 + t]
                yb = ybf[:, (t % 2) * 1024:(t % 2 + 1) * 1024]
                ybk = ('ybf', t % 2)
                if f32_dst_of is not None:
                    dk, dap = f32_dst_of(t)
                    E('v', [sk, 'lnR'], [dk], lambda e, sap=sap, dap=dap, r=rstd, n=nmu:
                      e.tensor_scalar(dap, sap, r, n, ALU.mult, ALU.add))
                    E('v', [dk], [ybk], lambda e, dap=dap, yb=yb: e.tensor_copy(yb, dap))
                else:
                    E('v', [sk, 'lnR'], [ybk], lambda e, sap=sap, yb=yb, r=rstd, n=nmu:
                      e.tensor_scalar(yb, sap, r, n, ALU.mult, ALU.add))
                tile_to_xfm(ybk, yb, t)
            p.barrier()

        def emit_quant(src_key, src_ap, dst_key, dst_ap, sc, sck, m1, qp,
                       signed=False, n=512, tt_engs=('g', 'g', 'g'),
                       m2=None, infw_ap=None, defer_cp=False):
            T = sck
            srcs = (src_key if isinstance(src_key, list) else [src_key])
            npart = src_ap.shape[0]
            c1 = cons[0:npart, qp['c1c']:qp['c1c'] + 1]
            K = cons[0:npart, qp['kc']:qp['kc'] + 1]
            hpath = (not signed) and qp is QE4
            e1, e2, e3 = tt_engs
            use_m2 = m2 is not None
            if signed:
                E('s', srcs + [T], [T], lambda e: e.activation(sc['f'], src_ap, AF.Abs))
                mag = sc['f']
            else:
                mag = src_ap
            E('v', srcs + [T], [T], lambda e: e.tensor_scalar(m1, mag, float(qp['thr']), None, ALU.is_ge))
            if use_m2:
                E('v', srcs + [T], [T], lambda e: e.tensor_scalar(
                    m2, mag, float(qp['ovf']), None, ALU.is_ge))
            else:
                E('v', srcs + [T], [T], lambda e: e.tensor_scalar(
                    sc['f'], mag, float(qp['ovf']), None, ALU.is_ge))
                E('v', [T], [T], lambda e: e.tensor_scalar(
                    sc['f'], sc['f'], 3.4e38, 1.0, ALU.mult, ALU.add))
            if hpath:
                negK = cons[0:npart, 8:9]
                E('s', srcs + ['cons', T], [T], lambda e: e.activation(
                    sc['y'], src_ap, AF.Relu, bias=0.0, scale=c1))
                E(e1, srcs + [T], [T], lambda e: e.tensor_tensor(sc['t'], sc['y'], src_ap, ALU.subtract))
                E(e2, [T], [T], lambda e: e.tensor_tensor(sc['y'], sc['y'], sc['t'], ALU.subtract))
                E('s', srcs + ['cons', T], [T], lambda e: e.activation(
                    sc['u'], src_ap, AF.Relu, bias=K, scale=1.0))
                E('s', ['cons', T], [dst_key, T], lambda e: e.activation(
                    dst_ap, sc['u'], AF.Relu, bias=negK, scale=1.0))
            else:
                E('v', srcs + ['cons', T], [T], lambda e: e.tensor_scalar(
                    sc['y'], src_ap, c1, 1.0, ALU.mult, ALU.mult))
                E(e1, srcs + [T], [T], lambda e: e.tensor_tensor(sc['t'], sc['y'], src_ap, ALU.subtract))
                E(e2, [T], [T], lambda e: e.tensor_tensor(sc['y'], sc['y'], sc['t'], ALU.subtract))
                E('v', srcs + ['cons', T], [T], lambda e: e.tensor_scalar(
                    sc['u'], src_ap, K, 1.0, ALU.add, ALU.mult))
                E('v', ['cons', T], [dst_key, T], lambda e: e.tensor_scalar(
                    dst_ap, sc['u'], K, 1.0, ALU.subtract, ALU.mult))

            def fin():
                if not use_m2:
                    E(e3, [T], [T], lambda e: e.tensor_tensor(sc['y'], sc['y'], sc['f'], ALU.mult))
                E('v', [dst_key, T], [dst_key] if not use_m2 else [],
                  lambda e: e.copy_predicated(dst_ap, m1, sc['y']))
                if use_m2:
                    E('v', [dst_key, T, 'infw'], [dst_key],
                      lambda e: e.copy_predicated(dst_ap, m2, infw_ap))
            if defer_cp:
                return fin
            fin()

        def phase_attn_a():
            with ExitStack() as ps_:
                A = ps_.enter_context(nc.sbuf_tensor(uname("A"), [128, NT * D], F32))
                ps_in = ps_.enter_context(ExitStack())
                aB = ps_in.enter_context(nc.sbuf_tensor(uname("aB"), [128, 10240], BF16))
                vrow = ps_in.enter_context(nc.sbuf_tensor(uname("vrow"), [1, D], F32))
                DMA('y', [], ['vrow'], lambda e: e.dma_start(vrow[:], vb_a[:]))
                ln_to_xfm(lambda t: (('x', t), x[:, t * D:(t + 1) * D]),
                          f32_dst_of=lambda t: (('A', t), A[:, t * D:(t + 1) * D]))
                Qp, Kp = aB[:, 0:2048], aB[:, 2048:4096]
                PTr = [aB[:, 4096 + i * 256:4096 + (i + 1) * 256] for i in range(4)]
                wq_t, wk_t = aB[:, 5120:6144], aB[:, 6144:7168]
                VB = 7168
                for pr in range(8):
                    for wdram, wtile, wkn, soff in ((wq_a, wq_t, 'wqp', 0),
                                                    (wk_a, wk_t, 'wkp', 0)):
                        src = bass.AP(wdram, pr * 128 * D, [[D, 128], [1, D]])
                        DMA('y', [], [('stg', soff)], lambda e, src=src, soff=soff:
                            e.dma_start(stg[:, soff:soff + 1024], src))
                        E('v', [('stg', soff)], [(wkn, pr)],
                          lambda e, soff=soff, wtile=wtile: e.tensor_copy(
                              wtile, stg[:, soff:soff + 1024]))
                    for which, (wkn, wtile, dst, boff, scale) in enumerate(
                            (('wqp', wq_t, Qp, 0, 0.125), ('wkp', wk_t, Kp, 8, 1.0))):
                        for tg in range(4):
                            mmg(PK(which), [
                                ([(wkn, pr)],
                                 (lambda e, wtile=wtile, dc=dc, tg=tg, w=which: e.matmul(
                                     pb[w][:], wtile[:, dc * 128:(dc + 1) * 128],
                                     xfm[:, dc * S + tg * 512:dc * S + (tg + 1) * 512],
                                     start=(dc == 0), stop=(dc == ND - 1))))
                                for dc in range(ND)])
                            E('v', [PK(which), 'pcb'], [('qk', pr, which)],
                              lambda e, dst=dst, tg=tg, b=boff, sc_=scale, w=which:
                              e.tensor_scalar(dst[:, tg * 512:(tg + 1) * 512], pb[w][:],
                                              sc_, pcb[:, b + pr:b + pr + 1], ALU.mult, ALU.add))
                    src = bass.AP(wv_a, pr * 128 * D, [[D, 128], [1, D]])
                    DMA('y', [], [('stg', 0)], lambda e, src=src: e.dma_start(
                        stg[:, 0:1024], src))
                    E('v', [('stg', 0), ('wqp', pr)], [('wvp', pr)],
                      lambda e: e.tensor_copy(wq_t, stg[:, 0:1024]))
                    for t in range(NT):
                        mmg(PK(2),
                            [([('wvp', pr)],
                              (lambda e, dc=dc, t=t: e.matmul(
                                  pb[2][:, 0:128],
                                  xfm[:, dc * S + t * 128:dc * S + (t + 1) * 128],
                                  wq_t[:, dc * 128:(dc + 1) * 128],
                                  start=(dc == 0), stop=False)))
                             for dc in range(ND)] +
                            [(['ones_f', 'vrow'],
                              lambda e: e.matmul(pb[2][:, 0:128], ones_f[:],
                                                 vrow[0:1, pr * 128:(pr + 1) * 128],
                                                 start=False, stop=True))])
                        E('v', [PK(2)], [('vau', pr, t)], lambda e, t=t: e.tensor_copy(
                            bass.AP(aB, VB + t * 130, [[10240, 128], [65, 2], [1, 64]]),
                            bass.AP(pb[2], 0, [[512, 128], [64, 2], [1, 64]])))
                        E('v', [], [('vau', pr, t)], lambda e, t=t: e.memset(
                            bass.AP(aB, VB + t * 130 + 64, [[10240, 128], [65, 2], [1, 1]]), 1.0))
                    for kw in range(NT):
                        qn = 256 if kw < NT - 1 else 128
                        kwp = kw % 2
                        pA, pB_ = pb[3 + kwp], pb[5 + kwp]
                        E('t', [('qk', pr, 0), ('qk', pr, 1)], [PK(3 + kwp)],
                          lambda e, kw=kw, qn=qn, pA=pA: e.matmul(
                              pA[:, 0:qn], Kp[0:64, kw * 128:(kw + 1) * 128],
                              Qp[0:64, kw * 128:kw * 128 + qn], start=True, stop=True,
                              tile_position=(0, 0)))
                        E('t', [], [PK(5 + kwp)],
                          lambda e, kw=kw, qn=qn, pB_=pB_: e.matmul(
                              pB_[:, 0:qn], Kp[64:128, kw * 128:(kw + 1) * 128],
                              Qp[64:128, kw * 128:kw * 128 + qn], start=True, stop=True,
                              tile_position=(64, 0)))
                        for hh, pp, bk in ((0, pA, PK(3 + kwp)), (1, pB_, PK(5 + kwp))):
                            rt = PTr[kwp * 2 + hh]
                            E('s', [bk], [('pt', hh, kwp)], lambda e, rt=rt, qn=qn, pp=pp:
                              e.activation(rt[:, 0:qn], pp[:, 0:qn], AF.Exp))
                            E('v', [('pt', hh, kwp), 'mask'], [('pt', hh, kwp)],
                              lambda e, rt=rt: e.tensor_tensor(
                                  rt[:, 0:128], rt[:, 0:128], mask[:], ALU.mult))
                        pav = pb[kwp]
                        parts = []
                        for hh in range(2):
                            rprev = PTr[((kw - 1) % 2) * 2 + hh]
                            rown = PTr[kwp * 2 + hh]
                            vsl = bass.AP(aB, VB + kw * 130 + hh * 65, [[10240, 128], [1, 65]])
                            vslp = bass.AP(aB, VB + (kw - 1) * 130 + hh * 65, [[10240, 128], [1, 65]])
                            if kw > 0:
                                parts.append(([('pt', hh, (kw - 1) % 2), ('vau', pr, kw - 1)],
                                              (lambda e, rprev=rprev, vslp=vslp, hh=hh:
                                               e.matmul(pav[:, hh * 65:(hh + 1) * 65],
                                                        rprev[:, 128:256], vslp,
                                                        start=True, stop=False))))
                            parts.append(([('pt', hh, kwp), ('vau', pr, kw)],
                                          (lambda e, rown=rown, vsl=vsl, hh=hh, kw=kw:
                                           e.matmul(pav[:, hh * 65:(hh + 1) * 65],
                                                    rown[:, 0:128], vsl,
                                                    start=(kw == 0), stop=True))))
                        mmg(PK(kwp), parts)
                        for hh in range(2):
                            hcol = (2 * pr + hh) * 64
                            E('v', [PK(kwp), 'lnb'], ['lnb'], lambda e, hh=hh, pav=pav:
                              e.reciprocal(lnb[:, hh:hh + 1], pav[:, hh * 65 + 64:hh * 65 + 65]))
                            E('v', [PK(kwp), 'lnb', ('A', kw)], [('A', kw)],
                              lambda e, hh=hh, kw=kw, hcol=hcol, pav=pav: e.scalar_tensor_tensor(
                                  A[:, kw * D + hcol:kw * D + hcol + 64],
                                  pav[:, hh * 65:hh * 65 + 64], lnb[:, hh:hh + 1],
                                  A[:, kw * D + hcol:kw * D + hcol + 64], ALU.mult, ALU.add))
                p.barrier()
                if AA_STOP[0] == 1:
                    for t in range(NT):
                        E('v', [('A', t)], [('x', t)], lambda e, t=t: e.tensor_copy(
                            x[:, t * D:(t + 1) * D], A[:, t * D:(t + 1) * D]))
                    p.barrier()
                    return
                ps_in.close()
                with ExitStack() as ps2:
                    def inner_ln(gbt, add_to_x):
                        ln_stats_all(lambda t: (('A', t), A[:, t * D:(t + 1) * D]))
                        for t in range(NT):
                            Aap = A[:, t * D:(t + 1) * D]
                            rstd, nmu = lnR[:, t:t + 1], lnR[:, 16 + t:17 + t]
                            E('v', [('A', t), 'lnR'], [('A', t)], lambda e, Aap=Aap, r=rstd, n=nmu:
                              e.tensor_scalar(Aap, Aap, r, n, ALU.mult, ALU.add))
                            E('v', [('A', t), 'gbt'], [('A', t)], lambda e, Aap=Aap:
                              e.tensor_tensor(Aap, Aap, gbt[:, 0:D], ALU.mult))
                            E('v', [('A', t), 'gbt'], [('A', t)], lambda e, Aap=Aap:
                              e.tensor_tensor(Aap, Aap, gbt[:, D:2 * D], ALU.add))
                            if add_to_x:
                                E('v', [('A', t), ('x', t)], [('x', t)], lambda e, t=t, Aap=Aap:
                                  e.tensor_tensor(x[:, t * D:(t + 1) * D],
                                                  x[:, t * D:(t + 1) * D], Aap, ALU.add))
                            else:
                                yb = ybf[:, (t % 2) * 1024:(t % 2 + 1) * 1024]
                                E('v', [('A', t)], [('ybf', t % 2)], lambda e, Aap=Aap, yb=yb:
                                  e.tensor_copy(yb, Aap))
                                tile_to_xfm(('ybf', t % 2), yb, t)
                    with nc.sbuf_tensor(uname("gbt1"), [128, 2 * D], F32) as gbt:
                        DMA('y', [], ['gbt'], lambda e: e.dma_start(gbt[:], lngb[:, 0:2 * D]))
                        inner_ln(gbt, add_to_x=False)
                        p.barrier()
                    if AA_STOP[0] == 2:
                        for t in range(NT):
                            E('v', [('A', t)], [('x', t)], lambda e, t=t: e.tensor_copy(
                                x[:, t * D:(t + 1) * D], A[:, t * D:(t + 1) * D]))
                        p.barrier()
                        return
                    emit_ffn(A, lambda t: ('A', t), hid=FFA, w1d=ff1a_w, b1off=48,
                             w2d=ff2a_w, brow_dram=ff2a_b, quant=False, tokgrp=512)
                    p.barrier()
                    if AA_STOP[0] == 3:
                        for t in range(NT):
                            E('v', [('A', t)], [('x', t)], lambda e, t=t: e.tensor_copy(
                                x[:, t * D:(t + 1) * D], A[:, t * D:(t + 1) * D]))
                        p.barrier()
                        return
                    with nc.sbuf_tensor(uname("gbt2"), [128, 2 * D], F32) as gbt:
                        DMA('y', [], ['gbt'], lambda e: e.dma_start(gbt[:], lngb[:, 2 * D:4 * D]))
                        inner_ln(gbt, add_to_x=True)
                        p.barrier()

        wqc_state = {'done': False}

        def emit_ffn(rbuf, rkey, hid, w1d, b1off, w2d, brow_dram, quant,
                     tokgrp=512):
            nj = hid // 128
            nhalf = S // tokgrp
            with ExitStack() as ps_:
                fw = ybf
                fstg = ps_.enter_context(nc.sbuf_tensor(uname("fstg"), [128, 2048], F32))
                brow = ps_.enter_context(nc.sbuf_tensor(uname("fbrow"), [1, D], F32))
                if quant:
                    fh = ps_.enter_context(nc.sbuf_tensor(uname("fh"), [128, 1024], F32))
                    fsc = ps_.enter_context(nc.sbuf_tensor(uname("fsc"), [128, 3072], F32))
                    fm8 = ps_.enter_context(nc.sbuf_tensor(uname("fm8"), [128, 2048], U8))
                    hq8 = ps_.enter_context(
                        nc.sbuf_tensor(uname("fq8"), [128, nj * 512 + 1536], FP8))
                    SC = [dict(y=fsc[:, p * 1536 + 0:p * 1536 + 512],
                               t=fsc[:, p * 1536 + 512:p * 1536 + 1024],
                               u=fsc[:, p * 1536 + 512:p * 1536 + 1024],
                               f=fsc[:, p * 1536 + 1024:p * 1536 + 1536])
                          for p in range(2)]
                    M1 = [fm8[:, p * 512:(p + 1) * 512] for p in range(2)]
                    M2 = [fm8[:, 1024 + p * 512:1024 + (p + 1) * 512] for p in range(2)]
                else:
                    hB = ps_.enter_context(
                        nc.sbuf_tensor(uname("fhB"), [128, nj * 512], BF16))
                DMA('y', [], ['brow'], lambda e: e.dma_start(brow[:], brow_dram[:]))
                if quant:
                    scW = dict(y=fsc[0:1, 0:512], t=fsc[0:1, 512:1024],
                               u=fsc[0:1, 512:1024], f=fsc[0:1, 1024:1536])
                    for hf in range(2):
                        emit_quant('brow', brow[0:1, hf * 512:(hf + 1) * 512], 'brow',
                                   brow[0:1, hf * 512:(hf + 1) * 512],
                                   scW, ('fscr', 0), fm8[0:1, 0:512],
                                   QE5, signed=True, n=512)
                building = quant and not wqc_state['done']
                wq_dma = emit_wqc_chain = None
                if building:
                    wstg1 = ps_.enter_context(
                        nc.sbuf_tensor(uname("wstg1"), [128, 1024], F32))
                    wslot = [stg[:, 0:1024], wstg1[:, 0:1024]]
                    wsc3 = ps_.enter_context(
                        nc.sbuf_tensor(uname("wsc3"), [128, 6144], F32))
                    wm1k = ps_.enter_context(
                        nc.sbuf_tensor(uname("wm1k"), [128, 2048], U8))
                    wq82 = ps_.enter_context(
                        nc.sbuf_tensor(uname("wq82"), [128, 2048], FP8))
                    scQ = [dict(y=wsc3[:, p * 3072 + 0:p * 3072 + 1024],
                                t=wsc3[:, p * 3072 + 1024:p * 3072 + 2048],
                                u=wsc3[:, p * 3072 + 1024:p * 3072 + 2048],
                                f=wsc3[:, p * 3072 + 2048:p * 3072 + 3072])
                           for p in range(2)]

                    def wq_dma(j):
                        jw = j % 2
                        DMA('y' if jw == 0 else 'g', [], [('wstg', jw)],
                            lambda e, j=j, jw=jw: e.dma_start(
                                wslot[jw],
                                w2q_w.rearrange("(c p) d -> c p d", p=128)[j]))

                    def emit_wqc_chain(j, pendw):
                        jw = j % 2
                        te = ('g', 'g', 'g')
                        f0 = emit_quant(('wstg', jw), wslot[jw],
                                        ('wq8w', jw), wq82[:, jw * 1024:(jw + 1) * 1024],
                                        scQ[jw], ('wscr', jw),
                                        wm1k[:, jw * 1024:(jw + 1) * 1024],
                                        QE4, signed=True, n=1024, tt_engs=te,
                                        defer_cp=True)

                        def fin(j=j, jw=jw, f0=f0):
                            f0()
                            DMA('g' if jw == 0 else 'y', [('wq8w', jw)], [('wqc', j)],
                                lambda e, j=j, jw=jw: e.dma_start(
                                    wqc.rearrange("(c p) d -> c p d", p=128)[j],
                                    wq82[:, jw * 1024:(jw + 1) * 1024]))
                        pendw.append(fin)
                    wqc_state['done'] = True
                for half in range(nhalf):
                    hb = half * tokgrp

                    def w1_dma(j):
                        jb = j % 2
                        aw = [('fws', jb)] + ([('fwsb', 0), ('fwsb', 1)] if jb == 0 else [('fwsb', 2)])
                        DMA('y' if jb == 0 else 'g', [], aw,
                            lambda e, j=j, jb=jb: e.dma_start(
                                fstg[:, jb * 1024:(jb + 1) * 1024],
                                bass.AP(w1d, j * 128 * D, [[D, 128], [1, D]])))
                    w1_dma(0)
                    w1_dma(1)

                    def w1_cast(j):
                        jb = j % 2
                        wsl = fw[:, jb * 1024:(jb + 1) * 1024]
                        cw = [('fw', jb)] + ([('fwb', 0), ('fwb', 1)] if jb == 0 else [('fwb', 2)])
                        E('v', [('fws', jb)], cw,
                          lambda e, jb=jb, wsl=wsl: e.tensor_copy(
                              wsl, fstg[:, jb * 1024:(jb + 1) * 1024]))
                    w1_cast(0)
                    bld = building and half == 0
                    if bld:
                        wq_dma(0)
                    pend, pendw = [], []
                    for j in range(nj + 1):
                        if j < nj:
                            jb = j % 2
                            wsl = fw[:, jb * 1024:(jb + 1) * 1024]
                            if j + 1 < nj:
                                w1_cast(j + 1)
                            if j + 2 < nj:
                                w1_dma(j + 2)
                            if bld and j + 1 < nj:
                                wq_dma(j + 1)
                            mmg(PK(jb), [
                                ([('fw', jb)],
                                 (lambda e, wsl=wsl, dc=dc, jb=jb: e.matmul(
                                     pb[jb][:], wsl[:, dc * 128:(dc + 1) * 128],
                                     xfm[:, dc * S + hb:dc * S + hb + 512],
                                     start=(dc == 0), stop=(dc == ND - 1))))
                                for dc in range(ND)])
                            if quant:
                                hsl = fh[:, jb * 512:(jb + 1) * 512]
                                E('s', [PK(jb), 'pcb', ('fh', jb)], [('fh', jb)],
                                  lambda e, jb=jb, j=j, hsl=hsl: e.activation(
                                      hsl, pb[jb][:], AF.Relu,
                                      bias=pcb[:, b1off + j:b1off + j + 1], scale=1.0))
                                pend.append(emit_quant(
                                    ('fh', jb), hsl,
                                    ('h8', j), hq8[:, j * 512:(j + 1) * 512],
                                    SC[jb], ('fscr', jb), M1[jb], QE4,
                                    tt_engs=('g', 'g', 'v'), defer_cp=True))
                            else:
                                E('s', [PK(jb), 'pcb', ('h8', j)], [('h8', j)],
                                  lambda e, jb=jb, j=j: e.activation(
                                      hB[:, j * 512:(j + 1) * 512],
                                      pb[jb][:], AF.Relu,
                                      bias=pcb[:, b1off + j:b1off + j + 1], scale=1.0))
                            if bld:
                                emit_wqc_chain(j, pendw)
                        while pend and (len(pend) > 1 or j == nj):
                            pend.pop(0)()
                        while pendw and (len(pendw) > 1 or j == nj):
                            pendw.pop(0)()
                    if quant:
                        rts = [hq8[:, nj * 512 + q * 512:nj * 512 + (q + 1) * 512]
                               for q in range(3)]
                        for dc in range(2):
                            def rt_dma(jq, dc=dc):
                                q = jq % 3
                                DMA('y' if q % 2 == 0 else 'g',
                                    [('fq', q), ('wqc', jq)], [('fq', q)],
                                    lambda e, jq=jq, q=q: e.dma_start(
                                        rts[q], bass.AP(wqc, jq * 128 * D + dc * 512,
                                                        [[D, 128], [1, 512]])))
                            for q0 in range(3):
                                rt_dma(q0)
                            for j in range(nj):
                                q = j % 3
                                rt = rts[q]
                                for tt in range(4):
                                    pki = 3 + tt
                                    lap = hq8[:, j * 512 + tt * 128:j * 512 + (tt + 1) * 128]
                                    E('t', [('h8', j), ('fq', q)],
                                      [PK(pki)] if j == 0 else [],
                                      lambda e, lap=lap, rt=rt, pki=pki, j=j: e.matmul(
                                          pb[pki][:], lap, rt, start=(j == 0), stop=False),
                                      inc=(tt == 3))
                                if j + 3 < nj:
                                    rt_dma(j + 3)
                            for tt in range(4):
                                pki = 3 + tt
                                E('t', ['brow', 'ones_f'], [PK(pki)],
                                  lambda e, dc=dc, pki=pki: e.matmul(
                                      pb[pki][:], ones_f[:], brow[0:1, dc * 512:(dc + 1) * 512],
                                      start=False, stop=True))
                            efin = []
                            for tt in range(4):
                                if len(efin) >= 2:
                                    efin.pop(0)()
                                pki = 3 + tt
                                t_tile = half * (tokgrp // 128) + tt
                                par = tt % 2
                                osl = fh[:, par * 512:(par + 1) * 512]
                                rap = rbuf[:, t_tile * D + dc * 512:t_tile * D + (dc + 1) * 512]
                                E('v', [PK(pki)], [('fh', par)],
                                  lambda e, pki=pki, osl=osl: e.tensor_copy(osl, pb[pki][:]))
                                f0 = emit_quant(('fh', par), osl, ('fh', par), osl,
                                                SC[par], ('fscr', par), M1[par], QE5,
                                                signed=True, tt_engs=('g', 'g', 'v'),
                                                defer_cp=True)

                                def fin(f0=f0, osl=osl, rap=rap, t_tile=t_tile, par=par):
                                    f0()
                                    E('v', [('fh', par), rkey(t_tile)], [rkey(t_tile)],
                                      lambda e: e.scalar_tensor_tensor(
                                          rap, osl, 1.0, rap, ALU.mult, ALU.add))
                                efin.append(fin)
                            for f in efin:
                                f()
                    else:
                        for dc in range(2):
                            def w2_dma(j, dc=dc):
                                q = j % 3
                                DMA('y' if q % 2 == 0 else 'g',
                                    [('fwb', q)], [('fwsb', q), ('fws', q // 2)],
                                    lambda e, j=j, dc=dc, q=q: e.dma_start(
                                        fstg[:, q * 512:(q + 1) * 512],
                                        bass.AP(w2d, j * 128 * D + dc * 512,
                                                [[D, 128], [1, 512]])))
                            w2_dma(0)
                            w2_dma(1)
                            w2_dma(2)
                            for j in range(nj):
                                q = j % 3
                                rt = fw[:, q * 512:(q + 1) * 512]
                                E('v', [('fwsb', q)], [('fwb', q), ('fw', q // 2)],
                                  lambda e, q=q, rt=rt: e.tensor_copy(
                                      rt, fstg[:, q * 512:(q + 1) * 512]))
                                for tt in range(4):
                                    pki = 3 + tt
                                    lap = hB[:, j * 512 + tt * 128:j * 512 + (tt + 1) * 128]
                                    E('t', [('h8', j), ('fwb', q)],
                                      [PK(pki)] if j == 0 else [],
                                      lambda e, lap=lap, rt=rt, pki=pki, j=j: e.matmul(
                                          pb[pki][:], lap, rt, start=(j == 0), stop=False),
                                      inc=(tt == 3))
                                if j + 3 < nj:
                                    w2_dma(j + 3)
                            for tt in range(4):
                                pki = 3 + tt
                                E('t', ['brow', 'ones_f'], [PK(pki)],
                                  lambda e, dc=dc, pki=pki: e.matmul(
                                      pb[pki][:], ones_f[:], brow[0:1, dc * 512:(dc + 1) * 512],
                                      start=False, stop=True))
                            for tt in range(4):
                                pki = 3 + tt
                                t_tile = half * (tokgrp // 128) + tt
                                rap = rbuf[:, t_tile * D + dc * 512:t_tile * D + (dc + 1) * 512]
                                E('v', [PK(pki), rkey(t_tile)], [rkey(t_tile)],
                                  lambda e, pki=pki, rap=rap: e.scalar_tensor_tensor(
                                      rap, pb[pki][:], 1.0, rap, ALU.mult, ALU.add))
                p.barrier()

        def phase_ffnq():
            ln_to_xfm(lambda t: (('x', t), x[:, t * D:(t + 1) * D]))
            emit_ffn(x, lambda t: ('x', t), hid=FFQ, w1d=w1q_w, b1off=16,
                     w2d=w2q_w, brow_dram=b2q_w, quant=True)

        def phase_attn_b():
            with ExitStack() as ps_:
                Kf = ps_.enter_context(nc.sbuf_tensor(uname("Kf"), [128, 16384], BF16))
                Va = ps_.enter_context(nc.sbuf_tensor(uname("Va"), [128, NT * 1040], BF16))
                Qs = ps_.enter_context(nc.sbuf_tensor(uname("Qs"), [128, 4096], BF16))
                af = ps_.enter_context(nc.sbuf_tensor(uname("af"), [128, 4096], BF16))
                ptb = ps_.enter_context(nc.sbuf_tensor(uname("ptb"), [128, 2048], BF16))
                wcs = ps_.enter_context(nc.sbuf_tensor(uname("wcs"), [128, 1024], BF16))
                sm2 = ps_.enter_context(nc.sbuf_tensor(uname("sm2"), [1, 3072], F32))
                rbc = ps_.enter_context(nc.sbuf_tensor(uname("rbc"), [64, 512], F32))
                kmx = ps_.enter_context(nc.sbuf_tensor(uname("kmx"), [1, 24], F32))
                borow = sm2[0:1, 2048:3072]
                ln_to_xfm(lambda t: (('x', t), x[:, t * D:(t + 1) * D]))
                DMA('y', ['sm2'], ['sm2'], lambda e: e.dma_start(borow[:], bo_b[:]))
                for pr in range(8):
                    src = bass.AP(wk_b, pr * 128 * D, [[D, 128], [1, D]])
                    DMA('y', [], [('stg', 0)], lambda e, src=src: e.dma_start(
                        stg[:, 0:1024], src))
                    E('v', [('stg', 0)], [('wcs',)], lambda e: e.tensor_copy(
                        wcs[:], stg[:, 0:1024]))
                    for tg in range(4):
                        mmg(PK(0), [
                            ([('wcs',)],
                             (lambda e, dc=dc, tg=tg: e.matmul(
                                 pb[0][:], wcs[:, dc * 128:(dc + 1) * 128],
                                 xfm[:, dc * S + tg * 512:dc * S + (tg + 1) * 512],
                                 start=(dc == 0), stop=(dc == ND - 1))))
                            for dc in range(ND)])
                        E('v', [PK(0)], [('kf', pr)], lambda e, pr=pr, tg=tg: e.tensor_copy(
                            Kf[:, pr * 2048 + tg * 512:pr * 2048 + (tg + 1) * 512], pb[0][:]))
                    src = bass.AP(wv_b, pr * 128 * D, [[D, 128], [1, D]])
                    DMA('y', [], [('stg', 0)], lambda e, src=src: e.dma_start(
                        stg[:, 0:1024], src))
                    E('v', [('stg', 0)], [('wcs',)], lambda e: e.tensor_copy(
                        wcs[:], stg[:, 0:1024]))
                    for t in range(NT):
                        mmg(PK(2), [
                            ([('wcs',)],
                             (lambda e, dc=dc, t=t: e.matmul(
                                 pb[2][:, 0:128],
                                 xfm[:, dc * S + t * 128:dc * S + (t + 1) * 128],
                                 wcs[:, dc * 128:(dc + 1) * 128],
                                 start=(dc == 0), stop=(dc == ND - 1))))
                            for dc in range(ND)])
                        E('v', [PK(2)], [('vab', pr, t)], lambda e, t=t, pr=pr: e.tensor_copy(
                            bass.AP(Va, t * 1040 + pr * 130, [[NT * 1040, 128], [65, 2], [1, 64]]),
                            bass.AP(pb[2], 0, [[512, 128], [64, 2], [1, 64]])))
                        E('v', [], [('vab', pr, t)], lambda e, t=t, pr=pr: e.memset(
                            bass.AP(Va, t * 1040 + pr * 130 + 64, [[NT * 1040, 128], [65, 2], [1, 1]]), 1.0))
                p.barrier()
                for qg in (range(4) if AB_QGS[0] is None else AB_QGS[0]):
                    qb_ = qg * 512
                    for pr in range(8):
                        src = bass.AP(wq_b, pr * 128 * D, [[D, 128], [1, D]])
                        DMA('y', [], [('stg', 0)], lambda e, src=src: e.dma_start(
                            stg[:, 0:1024], src))
                        E('v', [('stg', 0)], [('wcs',)], lambda e: e.tensor_copy(
                            wcs[:], stg[:, 0:1024]))
                        mmg(PK(0), [
                            ([('wcs',)],
                             (lambda e, dc=dc, qb_=qb_: e.matmul(
                                 pb[0][:], wcs[:, dc * 128:(dc + 1) * 128],
                                 xfm[:, dc * S + qb_:dc * S + qb_ + 512],
                                 start=(dc == 0), stop=(dc == ND - 1))))
                            for dc in range(ND)])
                        E('v', [PK(0)], [('qs', pr)], lambda e, pr=pr: e.tensor_scalar(
                            Qs[:, pr * 512:(pr + 1) * 512], pb[0][:], 0.125, 1.0,
                            ALU.mult, ALU.mult))
                    for pr in (range(8) if AB_PRS[0] is None else AB_PRS[0]):
                        def emit_st(kt):
                            kp = kt % 2
                            pS0, pS1 = pb[3 + kp], pb[5 + kp]
                            E('t', [('kf', pr), ('qs', pr)], [PK(3 + kp)],
                              lambda e, kt=kt, pS0=pS0: e.matmul(
                                  pS0[:], Kf[0:64, pr * 2048 + kt * 128:pr * 2048 + (kt + 1) * 128],
                                  Qs[0:64, pr * 512:(pr + 1) * 512], start=True, stop=True,
                                  tile_position=(0, 0)))
                            E('t', [], [PK(5 + kp)],
                              lambda e, kt=kt, pS1=pS1: e.matmul(
                                  pS1[:], Kf[64:128, pr * 2048 + kt * 128:pr * 2048 + (kt + 1) * 128],
                                  Qs[64:128, pr * 512:(pr + 1) * 512], start=True, stop=True,
                                  tile_position=(64, 0)))
                            E('s', [PK(3 + kp)], [('ptb', 0, kp)],
                              lambda e, kp=kp, pS0=pS0: e.activation(
                                  ptb[:, kp * 512:(kp + 1) * 512], pS0[:], AF.Exp))
                            E('s', [PK(5 + kp)], [('ptb', 1, kp)],
                              lambda e, kp=kp, pS1=pS1: e.activation(
                                  ptb[:, (2 + kp) * 512:(3 + kp) * 512], pS1[:], AF.Exp))

                        def emit_av(kt):
                            kp = kt % 2
                            for hh in range(2):
                                psl = ptb[:, (2 * hh + kp) * 512:(2 * hh + kp + 1) * 512]
                                vsl = bass.AP(Va, kt * 1040 + pr * 130 + hh * 65,
                                              [[NT * 1040, 128], [1, 65]])
                                E('t', [('ptb', hh, kp), ('vab', pr, kt)],
                                  [PK(hh)] if kt in (0, NT - 1) else [],
                                  lambda e, psl=psl, vsl=vsl, hh=hh, kt=kt:
                                  e.matmul(pb[hh][0:65, :], vsl, psl,
                                           start=(kt == 0), stop=(kt == NT - 1)),
                                  inc=(kt == NT - 1))
                        emit_st(0)
                        for kt in range(NT):
                            if kt + 1 < NT:
                                emit_st(kt + 1)
                            emit_av(kt)
                        for hh in range(2):
                            dsl = sm2[0:1, 1024 + hh * 512:1536 + hh * 512]
                            lsl = sm2[0:1, hh * 512:(hh + 1) * 512]
                            E('v', [PK(hh), 'sm2'], ['sm2'], lambda e, hh=hh, lsl=lsl:
                              e.tensor_copy(lsl, pb[hh][64:65, :]))
                            E('v', [PK(hh), 'sm2'], ['sm2'], lambda e, hh=hh, dsl=dsl, lsl=lsl:
                              e.scalar_tensor_tensor(dsl, lsl,
                                                     ATT_EPS, pb[hh][64:65, :], ALU.mult, ALU.add))
                        E('v', ['sm2'], ['sm2'], lambda e: e.reciprocal(
                            sm2[0:1, 1024:2048], sm2[0:1, 1024:2048]))
                        for hh in range(2):
                            dsl = sm2[0:1, 1024 + hh * 512:1536 + hh * 512]
                            E('t', ['sm2', 'ones_f'], [PK(6)], lambda e, dsl=dsl: e.matmul(
                                pb[6][0:64, :], ones_f[0:1, 0:64], dsl,
                                start=True, stop=True))
                            E('v', [PK(6)], [('rbc',)], lambda e: e.tensor_copy(
                                rbc[:], pb[6][0:64, :]))
                            E('v', [PK(hh), ('rbc',)], [('af', pr, hh)],
                              lambda e, hh=hh, pr=pr: e.tensor_tensor(
                                  af[hh * 64:(hh + 1) * 64, pr * 512:(pr + 1) * 512],
                                  pb[hh][0:64, :], rbc[:], ALU.mult))
                    for dc in (range(2) if AB_OUTP[0] else []):
                        first = [True] * 4
                        for ic in range(AB_NIC[0]):
                            DMA('y', [], [('stg', 0)], lambda e, ic=ic, dc=dc:
                                e.dma_start(stg[:, 0:512],
                                            bass.AP(wo_b, ic * 128 * D + dc * 512,
                                                    [[D, 128], [1, 512]])))
                            E('v', [('stg', 0)], [('wcs',)], lambda e: e.tensor_copy(
                                wcs[:, 0:512], stg[:, 0:512]))
                            for ts_ in range(4):
                                pki = 3 + ts_
                                E('t', [('af', ic, 0), ('af', ic, 1), ('wcs',)],
                                  [PK(pki)] if first[ts_] else [],
                                  lambda e, ic=ic, ts_=ts_, pki=pki: e.matmul(
                                      pb[pki][:],
                                      af[:, ic * 512 + ts_ * 128:ic * 512 + (ts_ + 1) * 128],
                                      wcs[:, 0:512], start=(ic == 0), stop=False), inc=(ts_ == 3))
                            if False:
                                pass
                                first[ts_] = False
                        for ts_ in range(4):
                            pki = 3 + ts_
                            E('t', ['sm2', 'ones_f'], [PK(pki)],
                              lambda e, dc=dc, pki=pki: e.matmul(
                                  pb[pki][:], ones_f[:], borow[0:1, dc * 512:(dc + 1) * 512],
                                  start=False, stop=True))
                        for ts_ in range(4):
                            pki = 3 + ts_
                            t_tile = qg * 4 + ts_
                            rap = x[:, t_tile * D + dc * 512:t_tile * D + (dc + 1) * 512]
                            if dc == 0 and ts_ == 0 and AB_DBG[0]:
                                E('v', [PK(pki), ('ptb', 0, 0)], [('dbgs',), ('ptb', 0, 0)],
                                  lambda e: e.tensor_copy(ptb[:, 0:512], pb[pki][:]))
                                DMA('g', [('dbgs',)], [], lambda e: e.dma_start(
                                    dbg[:], ptb[:, 0:512]))
                            E('v', [PK(pki), ('x', t_tile)], [('x', t_tile)],
                              lambda e, pki=pki, rap=rap: e.scalar_tensor_tensor(
                                  rap, pb[pki][:], 1.0, rap, ALU.mult, ALU.add))
                p.barrier()

        if "attn_a" in phases:
            phase_attn_a()
        if "ffnq1" in phases:
            phase_ffnq()
        if "attn_b" in phases:
            phase_attn_b()
        if "ffnq2" in phases:
            phase_ffnq()

        for t in range(NT):
            DMA('g', [('x', t)], [], lambda e, t=t: e.dma_start(
                xout.rearrange("(t p) d -> t p d", p=128)[t], x[:, t * D:(t + 1) * D]))
        for i in range(p.NDMA):
            p._wait('g', (f'dg{i}', p.cnt[f'dg{i}']))
    return nc


def _prep(inputs):
    import ml_dtypes
    g = lambda k: np.asarray(inputs[k], dtype=np.float32)
    o = {}
    rowc = lambda a: np.ascontiguousarray(
        a.reshape(8, 128, 8, 128).transpose(2, 1, 0, 3).reshape(D, D))
    o['wq_a'] = rowc(g('L0_attn_a_q_proj').transpose(1, 0, 2).reshape(D, D))
    o['wk_a'] = rowc(g('L0_attn_a_k_proj').transpose(1, 0, 2).reshape(D, D))
    o['wv_a'] = rowc(g('L0_attn_a_v_proj').transpose(1, 0, 2).reshape(D, D))
    o['qb_a'] = np.ascontiguousarray(g('L0_attn_a_q_bias').reshape(D, 1))
    o['kb_a'] = np.ascontiguousarray(g('L0_attn_a_k_bias').reshape(D, 1))
    o['vb_a'] = np.ascontiguousarray(g('L0_attn_a_v_bias').reshape(1, D))
    o['lngb'] = np.ascontiguousarray(np.broadcast_to(np.concatenate(
        [g('L0_attn_a_ln1_g'), g('L0_attn_a_ln1_b'),
         g('L0_attn_a_ln2_g'), g('L0_attn_a_ln2_b')])[None, :], (128, 4 * D)))
    o['ff1a_w'] = np.ascontiguousarray(
        g('L0_attn_a_ff1_w').reshape(D, FFA // 128, 128).transpose(1, 0, 2)
        .reshape(FFA // 128, 8, 128, 128).transpose(0, 2, 1, 3).reshape(FFA, D))
    o['ff1a_b'] = np.ascontiguousarray(g('L0_attn_a_ff1_b').reshape(FFA, 1))
    o['ff2a_w'] = np.ascontiguousarray(g('L0_attn_a_ff2_w'))
    o['ff2a_b'] = np.ascontiguousarray(g('L0_attn_a_ff2_b').reshape(1, D))
    o['w1q'] = np.ascontiguousarray(
        g('L0_ffn_w1').T.reshape(D, FFQ // 128, 128).transpose(1, 0, 2)
        .reshape(FFQ // 128, 8, 128, 128).transpose(0, 2, 1, 3).reshape(FFQ, D))
    o['b1q'] = np.ascontiguousarray(g('L0_ffn_b1').reshape(FFQ, 1))
    o['w2q'] = np.ascontiguousarray(g('L0_ffn_w2').T)
    o['b2q'] = np.ascontiguousarray(g('L0_ffn_b2').reshape(1, D))
    o['wq_b'] = rowc(np.ascontiguousarray(g('L0_attn_b_Wq').T))
    o['wk_b'] = rowc(np.ascontiguousarray(g('L0_attn_b_Wk').T))
    o['wv_b'] = rowc(np.ascontiguousarray(g('L0_attn_b_Wv').T))
    o['wo_b'] = np.ascontiguousarray(g('L0_attn_b_Wo').T)
    o['bo_b'] = np.ascontiguousarray(g('L0_attn_b_bo').reshape(1, D))
    o['identc'] = np.eye(128, dtype=ml_dtypes.bfloat16)
    km = np.arange(128)
    o['maskc'] = (km[:, None] <= km[None, :]).astype(ml_dtypes.bfloat16)
    o['onesf'] = np.ones((1, 128), dtype=np.float32)
    hs = np.zeros((128, 2), dtype=ml_dtypes.bfloat16)
    hs[0:64, 0] = 1
    hs[64:128, 1] = 1
    o['hselc'] = hs
    return o


def kernel(**inputs):
    xfull = np.asarray(inputs['L0_attn_a_x'], dtype=np.float32)
    shared = _prep(inputs)
    nc = build()
    in_maps = []
    for c in range(NCORES):
        m = dict(shared)
        m['xin'] = np.ascontiguousarray(xfull[c])
        in_maps.append(m)
    res = run_bass_kernel_spmd(nc, in_maps, core_ids=list(range(NCORES)))
    return np.stack([r['xout'] for r in res.results], axis=0)
